# version 08
# baseline (speedup 1.0000x reference)
import numpy as np
import ml_dtypes
from contextlib import ExitStack
import concourse.bass as bass
import concourse.mybir as mybir
from concourse.bass_utils import run_bass_kernel_spmd
from concourse.bass_core_rust import IndirectOffsetOnAxis
from concourse import library_config
from concourse.library_overlay import lower_extended_insts

F32 = mybir.dt.float32
BF16 = mybir.dt.bfloat16
I32 = mybir.dt.int32
I16 = mybir.dt.int16
U32 = mybir.dt.uint32
AF = mybir.ActivationFunctionType
ALU = mybir.AluOpType
AX = mybir.AxisListType

NCORES = 8
T = 4096
D = 1024
NT = T // 128
NCH = D // 128
EPS_LN = 1e-5

BF16NP = ml_dtypes.bfloat16
ID_F32 = np.eye(128, dtype=np.float32)
ID_BF16 = np.eye(128, dtype=BF16NP)




def _sb(ctx, nc, name, shape, dt):
    return ctx.enter_context(nc.sbuf_tensor(name, shape, dt))


def _ps(ctx, nc, name, shape, dt=F32):
    return ctx.enter_context(nc.psum_tensor(name, shape, dt))


def _sm(ctx, nc, name):
    return ctx.enter_context(nc.semaphore(name))


class NCProxy:

    def __init__(self, nc, pre):
        self._nc = nc
        self._pre = pre

    def sbuf_tensor(self, name, shape, dt):
        return self._nc.sbuf_tensor(self._pre + name, shape, dt)

    def psum_tensor(self, name, shape, dt):
        return self._nc.psum_tensor(self._pre + name, shape, dt)

    def __getattr__(self, a):
        return getattr(self._nc, a)


SEM_NAMES = ["s_w", "s_ld", "s_le", "s_st", "s_wp", "s_wp0", "s_sr", "s_sx",
             "s_wf", "s_wg", "s_cc", "s_sc", "s_g2"]


def _stage_fence(P, env):
    P.fence(extra=[(env[n], P.cnt.get(id(env[n]), 0)) for n in SEM_NAMES])


class Pgm:

    def __init__(self, nc, ctx):
        self.nc = nc
        self.v, self.s, self.t = nc.vector, nc.scalar, nc.tensor
        self.sy, self.g = nc.sync, nc.gpsimd
        self.cnt = {}
        self.last = {}
        self.SV = ctx.enter_context(nc.semaphore("chainV"))
        self.SS = ctx.enter_context(nc.semaphore("chainS"))
        self.ST = ctx.enter_context(nc.semaphore("chainT"))
        self.SG = ctx.enter_context(nc.semaphore("chainG"))

    def _mark(self, name, instr, sem, k):
        c = self.cnt.get(id(sem), 0) + k
        self.cnt[id(sem)] = c
        instr.then_inc(sem, k)
        ev = (sem, c)
        if name is not None:
            self.last[name] = ev
        return ev

    def _chain(self, name, eng, dep):
        if dep and name in self.last:
            s, c = self.last[name]
            eng.wait_ge(s, c)

    def vo(self, instr_f, dep=True):
        self._chain("v", self.v, dep)
        return self._mark("v", instr_f(), self.SV, 1)

    def so(self, instr_f, dep=True):
        self._chain("s", self.s, dep)
        return self._mark("s", instr_f(), self.SS, 1)

    def go(self, instr_f, dep=True):
        self._chain("g", self.g, dep)
        return self._mark("g", instr_f(), self.SG, 1)

    def tmark(self):
        return self._mark("t", self.t.nop(), self.ST, 1)

    def tlast(self, instr):
        return self._mark("t", instr, self.ST, 1)

    def dma(self, eng, out, in_, sem, **kw):
        return self._mark(None, eng.dma_start(out=out, in_=in_, **kw), sem, 16)

    def dma_serial(self, eng, out, in_, sem, **kw):
        prev = self.cnt.get(id(sem), 0)
        eng.wait_ge(sem, prev)
        return self._mark(None, eng.dma_start(out=out, in_=in_, **kw), sem, 16)

    def w(self, eng, ev):
        eng.wait_ge(ev[0], ev[1])

    def fence(self, extra=()):
        evs = list(self.last.values()) + [e for e in extra if e is not None]
        engs = [self.v, self.s, self.t, self.sy, self.g]
        for e in engs:
            for ev in evs:
                e.wait_ge(ev[0], ev[1])

    def wd(self, eng, sem, val):
        eng.wait_ge(sem, val)



SLOPES = [2.0 ** (-(h + 1) / 2.0) for h in range(16)]
R_PAIR = [128, 128, 256, 512, 512, 512, 512, 512]


def _s1_plan():
    cols = {}
    vals = []

    def col(h, doff, sign):
        key = (h, doff, sign)
        if key not in cols:
            cols[key] = len(vals)
            vals.append(key)
        return cols[key]

    plan = {}
    for h in range(16):
        R = R_PAIR[h // 2]
        sl = SLOPES[h]
        A_off = R // 2
        for ri in range(T // R):
            q0 = ri * R
            A = q0 + A_off
            items = []
            for c0 in range(0, T, 128):
                rel = c0 - q0
                if rel < 0:
                    if 3 + sl * (c0 + 127 - A) >= -110:
                        items.append((c0, "lo", col(h, c0 - A, 1), None))
                elif rel < R:
                    j = rel // 128
                    items.append((c0, "lo", col(h, c0 - A, 1), j))
                    items.append((c0, "up", col(h, c0 - A, -1), j))
                else:
                    if 3 - sl * (c0 - A) >= -110:
                        items.append((c0, "up", col(h, c0 - A, -1), None))
            plan[(h, ri)] = items
    hspan = {}
    for cidx, (h, doff, sign) in enumerate(vals):
        if h not in hspan:
            hspan[h] = [cidx, cidx + 1]
        else:
            hspan[h][1] = cidx + 1
    nb = len(vals)
    bias = np.zeros((128, nb), np.float32)
    i_ = np.arange(128, dtype=np.float64)
    for c, (h, doff, sign) in enumerate(vals):
        bias[:, c] = (sign * SLOPES[h] * (doff + i_)).astype(np.float32)
    rows = np.zeros((16, 2, 512), np.float32)
    for h in range(16):
        R = R_PAIR[h // 2]
        m = np.arange(R, dtype=np.float64)
        rows[h, 0, :R] = np.exp(-SLOPES[h] * (m - R // 2)).astype(np.float32)
        rows[h, 1, :R] = np.exp(-SLOPES[h] * (R // 2 - m)).astype(np.float32)
    n_ = np.arange(1024)[None, :]
    i2 = np.arange(128)[:, None]
    mlo = (n_ >= i2 + 512).astype(BF16NP)
    mup = (n_ < i2 + 512).astype(BF16NP)
    pairw = {}
    for hp in range(8):
        lo = hspan[2 * hp][0]
        hi = hspan[2 * hp + 1][1]
        pairw[hp] = (lo, hi)
    return plan, bias, rows, mlo, mup, hspan, pairw


_S1 = _s1_plan()


def emit_stage1(nc, P, env):
    plan, bias_np, rows_np, mlo_np, mup_np, hspan, pairw = _S1
    PAIRNB = max(hi - lo for lo, hi in pairw.values())
    x = env["x0"]
    wqkv3 = env["s1_wqkv"]
    wo = env["s1_wo"]
    vin = env["s1_vecs"]
    y = env["x1"]
    idb = env["idb"]
    idf = env["idf"]
    biasc = env["s1_biasc"]
    rowsc = env["s1_rowsc"]
    mloc = env["s1_mloc"]
    mupc = env["s1_mupc"]
    ones2c = env["ones2c"]
    sel65c = env["sel65c"]
    cselc = env["cselc"]
    onesrc = env["s1_onesrc"]

    ctx = ExitStack()
    with ctx:
        ATT = _sb(ctx, nc, "ATT", [128, NCH * T], BF16)
        QF = _sb(ctx, nc, "QF", [128, T], BF16)
        KF = _sb(ctx, nc, "KF", [128, T], BF16)
        VF = _sb(ctx, nc, "VF", [128, T], BF16)
        VT = _sb(ctx, nc, "VT", [128, NT * 130], BF16)
        WP0 = _sb(ctx, nc, "WP0", [128, 3 * D], BF16)
        BIASP = _sb(ctx, nc, "BIASP", [128, PAIRNB], F32)
        BD = _sb(ctx, nc, "BD", [128, PAIRNB], F32)
        MLO = _sb(ctx, nc, "MLO", [128, 1024], BF16)
        MUP = _sb(ctx, nc, "MUP", [128, 1024], BF16)
        ONES2 = _sb(ctx, nc, "ONES2", [128, 2], F32)
        SEL65 = _sb(ctx, nc, "SEL65", [65, 64], F32)
        CSEL = _sb(ctx, nc, "CSEL", [2, 256], F32)
        ONESR = _sb(ctx, nc, "ONESR", [1, 128], F32)
        AB = [_sb(ctx, nc, f"AB{i}", [128, 512], BF16) for i in range(4)]
        PB = [_sb(ctx, nc, f"PB{i}", [128, 512], BF16) for i in range(4)]
        CMB = [_sb(ctx, nc, f"CMB{i}", [65, 512], F32) for i in range(2)]
        CM2 = [_sb(ctx, nc, f"CM2{i}", [65, 512], F32) for i in range(2)]
        IDB = _sb(ctx, nc, "IDB", [128, 128], BF16)
        IDF = _sb(ctx, nc, "IDF", [128, 128], F32)
        BO = _sb(ctx, nc, "BO", [128, NCH], F32)
        CST = _sb(ctx, nc, "CST", [2, 72], F32)
        CB = _sb(ctx, nc, "CB", [128, 2], F32)
        qsq = _sb(ctx, nc, "qsq", [128, 512], F32)
        st8 = _sb(ctx, nc, "st8", [128, 8], F32)
        s_w = env["s_w"]
        s_wp = env["s_wp"]
        s_ld = env["s_ld"]
        s_le = env["s_le"]
        s_st = env["s_st"]

        xctx = ExitStack()
        XH = xctx.enter_context(nc.sbuf_tensor("XH", [128, NCH * T], BF16))
        actxA = ExitStack()
        xb0 = actxA.enter_context(nc.sbuf_tensor("xb0", [128, D], F32))
        xb1 = actxA.enter_context(nc.sbuf_tensor("xb1", [128, D], F32))
        sq = actxA.enter_context(nc.sbuf_tensor("sqA", [128, D], F32))
        xhbt = actxA.enter_context(nc.sbuf_tensor("xhbt", [128, D], BF16))
        xbufs = [xb0, xb1]
        lds = [s_ld, s_le]
        WPs = [WP0, WP0]

        def emit(_):
            v, sc, t, sy, g = P.v, P.s, P.t, P.sy, P.g
            P.dma(g, IDB[:], idb.ap(), s_w)
            P.dma(g, IDF[:], idf.ap(), s_w)
            P.dma(g, MLO[:], mloc.ap(), s_w)
            P.dma(g, MUP[:], mupc.ap(), s_w)
            P.dma(g, ONES2[:], ones2c.ap(), s_w)
            P.dma(g, SEL65[:], sel65c.ap(), s_w)
            P.dma(g, CSEL[:], cselc.ap(), s_w)
            P.dma(g, ONESR[:], onesrc.ap(), s_w)
            for c in range(NCH):
                P.dma(g, BO[:, c:c + 1], vin[c * 128:(c + 1) * 128, :], s_w)
            vt_ini = P.go(lambda: g.memset(VT[:], 1.0))
            W_ALL = P.cnt[id(s_w)]

            xh_ev = [None] * NT
            ev_ev = [None] * NT
            vst_ev = [None] * NT
            tp_ev = [None] * NT
            sqA_ev = [None] * NT
            with ExitStack() as actx:
                tpA = _ps(actx, nc, "tpA", [128, 1024], BF16)
                tpB = _ps(actx, nc, "tpB", [128, 1024], BF16)
                P.wd(v, s_w, W_ALL)
                P.vo(lambda: v.memset(st8[:, 7:8], EPS_LN))
                for i in range(NT):
                    xb = xbufs[i % 2]
                    if i >= 2:
                        P.w(sy, xh_ev[i - 2])
                        P.w(sy, sqA_ev[i - 2])
                    ld_ev = P.dma(sy, xb[:], x[i * 128:(i + 1) * 128, :], lds[i % 2])
                    P.w(sc, ld_ev)
                    if i >= 1:
                        P.w(sc, vst_ev[i - 1])
                    sqA_ev[i] = P.so(lambda: sc.activation(out=sq[:], in_=xb[:],
                                                           func=AF.Square))
                    P.w(v, ld_ev)
                    P.w(v, sqA_ev[i])
                    P.vo(lambda: v.reduce_sum(st8[:, 0:1], xb[:], axis=AX.X))
                    P.vo(lambda: v.reduce_sum(st8[:, 1:2], sq[:], axis=AX.X))
                    P.vo(lambda: v.tensor_scalar_mul(st8[:, 2:3], st8[:, 0:1], -1.0 / D))
                    P.vo(lambda: v.tensor_scalar_mul(st8[:, 3:4], st8[:, 1:2], 1.0 / D))
                    P.vo(lambda: v.tensor_mul(st8[:, 4:5], st8[:, 2:3], st8[:, 2:3]))
                    vst_ev[i] = P.vo(lambda: v.tensor_sub(st8[:, 3:4], st8[:, 3:4],
                                                          st8[:, 4:5]))
                    P.w(sc, vst_ev[i])
                    sd_ev = P.so(lambda: sc.activation(out=st8[:, 5:6], in_=st8[:, 3:4],
                                                       func=AF.Sqrt, bias=st8[:, 7:8]))
                    P.w(v, sd_ev)
                    P.vo(lambda: v.reciprocal(st8[:, 6:7], st8[:, 5:6]))
                    if i >= 2:
                        P.w(v, tp_ev[i - 1])
                    xh_ev[i] = P.vo(lambda: v.tensor_scalar(
                        out=xhbt[:], in0=xb[:], scalar1=st8[:, 2:3],
                        scalar2=st8[:, 6:7], op0=ALU.add, op1=ALU.mult))
                    tp = tpA if i % 2 == 0 else tpB
                    P.w(t, xh_ev[i])
                    if i == 0:
                        P.wd(t, s_w, W_ALL)
                    if i >= 2:
                        P.w(t, ev_ev[i - 2])
                    ti = None
                    for c in range(NCH):
                        ti = t.transpose(tp[:, c * 128:(c + 1) * 128],
                                         xhbt[:, c * 128:(c + 1) * 128], IDB[:])
                    tp_ev[i] = P.tlast(ti)
                    P.w(v, tp_ev[i])
                    for c in range(NCH):
                        ev_ev[i] = P.vo(lambda c=c: v.tensor_copy(
                            out=XH[:, c * T + i * 128: c * T + (i + 1) * 128],
                            in_=tp[:, c * 128:(c + 1) * 128]))
            xh_done = ev_ev[NT - 1]
            P.fence(extra=[(s_ld, P.cnt.get(id(s_ld), 0)), (s_le, P.cnt.get(id(s_le), 0)),
                           (s_w, P.cnt.get(id(s_w), 0))])
            actxA.close()

            wp_sems = [env["s_wp0"]]
            wp_tot = [0]
            proj_done = [None] * 8

            def load_pair_weights(hp):
                if hp >= 1 and proj_done[hp - 1] is not None:
                    P.w(g, proj_done[hp - 1])
                for mi in range(3):
                    P.dma(g, WPs[0][:, mi * D:(mi + 1) * D],
                          wqkv3[(mi * 8 + hp) * 128:(mi * 8 + hp + 1) * 128, :],
                          wp_sems[0])
                wp_tot[0] = P.cnt[id(wp_sems[0])]
                return wp_tot[0]

            pair_last_exp = [None]
            prev_pair_done = [None]

            with ExitStack() as bctx:
                pQ0 = _ps(bctx, nc, "pQ0", [128, 512], F32)
                pQ1 = _ps(bctx, nc, "pQ1", [128, 512], F32)
                pS0 = _ps(bctx, nc, "pS0", [128, 512], F32)
                pS1 = _ps(bctx, nc, "pS1", [128, 512], F32)
                pAV = [_ps(bctx, nc, f"pAV{i}", [65, 512], F32) for i in range(4)]
                pC = pAV[0]

                for hp in range(8):
                    slot = 0
                    wpt = load_pair_weights(hp)
                    WPt = WPs[0]
                    R = R_PAIR[hp]
                    evac_ev = [None] * 24
                    cst_ev = {}
                    cst_last = None
                    qsq_last_mm = None
                    for mi, DF in enumerate((QF, KF, VF)):
                        for trg in range(8):
                            kk = mi * 8 + trg
                            pQ = pQ0 if kk % 2 == 0 else pQ1
                            if kk == 0:
                                P.wd(t, wp_sems[slot], wpt)
                                if hp == 0:
                                    P.w(t, xh_done)
                                if prev_pair_done[0] is not None:
                                    P.w(t, prev_pair_done[0])
                                if pair_last_exp[0] is not None:
                                    P.w(t, pair_last_exp[0])
                            if kk >= 2:
                                P.w(t, evac_ev[kk - 2])
                            ti = None
                            for c in range(NCH):
                                ti = t.matmul(pQ[:],
                                              WPt[:, mi * D + c * 128: mi * D + (c + 1) * 128],
                                              XH[:, c * T + trg * 512: c * T + (trg + 1) * 512],
                                              start=(c == 0), stop=(c == NCH - 1))
                            mm_ev = P.tlast(ti)
                            P.w(sc, mm_ev)
                            evac_ev[kk] = P.so(lambda: sc.activation(
                                out=DF[:, trg * 512:(trg + 1) * 512], in_=pQ[:],
                                func=AF.Identity))
                            if mi < 2:
                                if qsq_last_mm is not None:
                                    P.w(sc, qsq_last_mm)
                                sq_ev = P.so(lambda: sc.activation(
                                    out=qsq[:], in_=pQ[:], func=AF.Square))
                                P.w(t, sq_ev)
                                if cst_last is not None:
                                    P.w(t, cst_last)
                                ti = t.matmul(pC[0:2, :], ONES2[:], qsq[:],
                                              start=True, stop=True)
                                cm_ev = P.tlast(ti)
                                qsq_last_mm = cm_ev
                                P.w(v, cm_ev)
                                P.vo(lambda: v.reduce_max(CST[:, 4:5], pC[0:2, :],
                                                          axis=AX.X))
                                if trg == 0:
                                    cst_ev[mi] = P.vo(lambda: v.tensor_copy(
                                        out=CST[:, mi:mi + 1], in_=CST[:, 4:5]))
                                else:
                                    cst_ev[mi] = P.vo(lambda: v.tensor_max(
                                        CST[:, mi:mi + 1], CST[:, mi:mi + 1],
                                        CST[:, 4:5]))
                                cst_last = cst_ev[mi]
                    proj_done[hp] = evac_ev[23]

                    P.w(sc, cst_ev[1])
                    P.so(lambda: sc.activation(out=CST[:, 2:3], in_=CST[:, 0:1],
                                               func=AF.Sqrt))
                    sd_ev = P.so(lambda: sc.activation(out=CST[:, 3:4], in_=CST[:, 1:2],
                                                       func=AF.Sqrt))
                    P.w(v, sd_ev)
                    P.vo(lambda: v.tensor_mul(CST[:, 5:6], CST[:, 2:3], CST[:, 3:4]))
                    c2_ev = P.vo(lambda: v.tensor_scalar_mul(CST[:, 5:6], CST[:, 5:6],
                                                             0.125))
                    P.w(t, c2_ev)
                    t.matmul(pS0[:, 0:1], CSEL[:, 0:128], CST[:, 5:6],
                             start=True, stop=True)
                    ti = t.matmul(pS0[:, 1:2], CSEL[:, 128:256], CST[:, 5:6],
                                  start=True, stop=True)
                    cb_ev = P.tlast(ti)
                    P.w(v, cb_ev)
                    cbv_ev = P.vo(lambda: v.tensor_copy(out=CB[:], in_=pS0[:, 0:2]))
                    plo, phi = pairw[hp]
                    if pair_last_exp[0] is not None:
                        P.w(g, pair_last_exp[0])
                    bp_ev = P.dma_serial(g, BIASP[:, 0:phi - plo],
                                         biasc.ap()[:, plo:phi], s_w)
                    if pair_last_exp[0] is not None:
                        P.w(v, pair_last_exp[0])
                    P.wd(v, s_w, bp_ev[1])
                    bd_ev = cbv_ev
                    for hh in range(2):
                        h = hp * 2 + hh
                        lo, hi = hspan[h]
                        bd_ev = P.vo(lambda lo=lo, hi=hi, hh=hh: v.tensor_scalar(
                            out=BD[:, lo - plo:hi - plo], in0=BIASP[:, lo - plo:hi - plo],
                            scalar1=CB[:, hh:hh + 1], scalar2=None, op0=ALU.subtract))
                    ab_last = None
                    for hh in range(2):
                        h = hp * 2 + hh
                        if prev_pair_done[0] is not None:
                            P.w(g, prev_pair_done[0])
                        if ab_last is not None:
                            P.w(g, ab_last)
                        rw0 = P.dma_serial(g, CMB[hh][0:1, 0:512],
                                           rowsc.ap()[0:1, h * 1024: h * 1024 + 512], s_w)
                        rw1 = P.dma_serial(g, CM2[hh][0:1, 0:512],
                                           rowsc.ap()[0:1, h * 1024 + 512: (h + 1) * 1024], s_w)
                        for si in range(2):
                            if ab_last is not None:
                                P.w(t, ab_last)
                            P.wd(t, s_w, rw1[1])
                            src_row = CMB[hh] if si == 0 else CM2[hh]
                            ti = t.matmul(pS1[:, 0:R], ONESR[:],
                                          src_row[0:1, 0:R],
                                          start=True, stop=True)
                            rb_ev = P.tlast(ti)
                            P.w(v, rb_ev)
                            ab_last = P.vo(lambda hh=hh, si=si: v.tensor_copy(
                                out=AB[hh * 2 + si][:, 0:R], in_=pS1[:, 0:R]))
                    vt_ev = [None] * NT
                    for i2 in range(NT):
                        pT = pS0 if i2 % 2 == 0 else pS1
                        if i2 == 0:
                            P.w(t, evac_ev[23])
                            P.w(t, cbv_ev)
                        if i2 == 1:
                            P.w(t, ab_last)
                        if i2 >= 2:
                            P.w(t, vt_ev[i2 - 2])
                        ti = t.matmul(pT[:, 0:128], VF[:, i2 * 128:(i2 + 1) * 128],
                                      IDB[:], start=True, stop=True)
                        tt_ev = P.tlast(ti)
                        P.w(v, tt_ev)
                        if hp == 0 and i2 == 0:
                            P.w(v, vt_ini)
                        P.vo(lambda: v.tensor_copy(
                            out=VT[:, i2 * 130: i2 * 130 + 64], in_=pT[:, 0:64]))
                        vt_ev[i2] = P.vo(lambda: v.tensor_copy(
                            out=VT[:, i2 * 130 + 65: i2 * 130 + 129], in_=pT[:, 64:128]))
                    vt_done = vt_ev[NT - 1]

                    last_fin = None
                    last_exp = None
                    for hh in range(2):
                        h = hp * 2 + hh
                        rbase = 64 * hh
                        spool = [pS0, pQ0] if hh == 0 else [pS1, pQ1]
                        sfree = [[vt_done], [proj_done[hp]]]
                        sidx = 0
                        pAVlo = pAV[hh * 2]
                        pAVup = pAV[hh * 2 + 1]
                        comb = CMB[hh]
                        comb2 = CM2[hh]
                        prev_fin = None
                        for ri in range(T // R):
                            q0 = ri * R
                            items = plan[(h, ri)]
                            n_it = len(items)
                            n_lo = sum(1 for it in items if it[1] == "lo")
                            n_up = n_it - n_lo
                            base = sidx
                            sidx += n_it + 1
                            ilo = iup = 0
                            avlo_ev = avup_ev = None
                            prev_exp = None
                            pb_av = [None, None]
                            smm = [None] * n_it

                            def emit_S(i, q0=q0):
                                c0 = items[i][0]
                                sl = (base + i) % 2
                                pS_ = spool[sl]
                                for evf in sfree[sl]:
                                    P.w(t, evf)
                                sfree[sl] = []
                                ti = t.matmul(pS_[:, 0:R],
                                              KF[rbase:rbase + 64, c0:c0 + 128],
                                              QF[rbase:rbase + 64, q0:q0 + R],
                                              start=True, stop=True)
                                smm[i] = P.tlast(ti)

                            emit_S(0)
                            if ri == 0:
                                P.w(sc, bd_ev)
                            for i, (c0, side, bcol, j) in enumerate(items):
                                if i + 1 < n_it:
                                    emit_S(i + 1)
                                sl = (base + i) % 2
                                pS_ = spool[sl]
                                pbslot = i % 2
                                pb = PB[hh * 2 + pbslot]
                                P.w(sc, smm[i])
                                if pb_av[pbslot] is not None:
                                    P.w(sc, pb_av[pbslot])
                                e_ev = P.so(lambda: sc.activation(
                                    out=pb[:, 0:R], in_=pS_[:, 0:R], func=AF.Exp,
                                    bias=BD[:, bcol - pairw[hp][0]:bcol - pairw[hp][0] + 1], scale=0.125))
                                sfree[sl].append(e_ev)
                                prev_exp = e_ev
                                mask_ev = e_ev
                                if j is not None:
                                    M = MLO if side == "lo" else MUP
                                    P.w(v, e_ev)
                                    mask_ev = P.vo(lambda: v.tensor_mul(
                                        pb[:, 0:R], pb[:, 0:R],
                                        M[:, 512 - 128 * j: 512 - 128 * j + R]))
                                P.w(t, mask_ev)
                                if side == "lo":
                                    first = ilo == 0
                                    ilo += 1
                                    if first and prev_fin is not None:
                                        P.w(t, prev_fin)
                                    ti = t.matmul(pAVlo[:, 0:R],
                                                  VT[:, (c0 // 128) * 130 + 65 * hh:
                                                     (c0 // 128) * 130 + 65 * hh + 65],
                                                  pb[:, 0:R], start=first,
                                                  stop=(ilo == n_lo),
                                                  skip_group_check=True)
                                    avlo_ev = P.tlast(ti)
                                    pb_av[pbslot] = avlo_ev
                                else:
                                    first = iup == 0
                                    iup += 1
                                    if first and prev_fin is not None:
                                        P.w(t, prev_fin)
                                    ti = t.matmul(pAVup[:, 0:R],
                                                  VT[:, (c0 // 128) * 130 + 65 * hh:
                                                     (c0 // 128) * 130 + 65 * hh + 65],
                                                  pb[:, 0:R], start=first,
                                                  stop=(iup == n_up),
                                                  skip_group_check=True)
                                    avup_ev = P.tlast(ti)
                                    pb_av[pbslot] = avup_ev
                            last_exp = prev_exp
                            P.w(v, avlo_ev)
                            P.vo(lambda: v.tensor_mul(comb2[:, 0:R], pAVlo[:, 0:R],
                                                      AB[hh * 2][0:65, 0:R]))
                            if avup_ev is not None:
                                P.w(v, avup_ev)
                                P.vo(lambda: v.tensor_mul(comb[:, 0:R], pAVup[:, 0:R],
                                                          AB[hh * 2 + 1][0:65, 0:R]))
                                P.vo(lambda: v.tensor_add(comb[:, 0:R], comb2[:, 0:R],
                                                          comb[:, 0:R]))
                            else:
                                P.vo(lambda: v.tensor_copy(out=comb[:, 0:R],
                                                           in_=comb2[:, 0:R]))
                            rc_ev = P.vo(lambda: v.reciprocal(comb[64:65, 0:R],
                                                              comb[64:65, 0:R]))
                            rbslot = (base + n_it) % 2
                            pS_ = spool[rbslot]
                            P.w(t, rc_ev)
                            for evf in sfree[rbslot]:
                                P.w(t, evf)
                            sfree[rbslot] = []
                            ti = t.matmul(pS_[0:64, 0:R], SEL65[:], comb[:, 0:R],
                                          start=True, stop=True)
                            rbc_ev = P.tlast(ti)
                            P.w(v, rbc_ev)
                            last_fin = P.vo(lambda pS_=pS_: v.tensor_mul(
                                ATT[rbase:rbase + 64, hp * T + q0: hp * T + q0 + R],
                                comb[0:64, 0:R], pS_[0:64, 0:R]))
                            sfree[rbslot].append(last_fin)
                            prev_fin = last_fin
                    pair_last_exp[0] = last_exp
                    prev_pair_done[0] = last_fin

            att_all = prev_pair_done[0]
            P.fence(extra=[(s_w, P.cnt.get(id(s_w), 0)),
                           (wp_sems[0], P.cnt.get(id(wp_sems[0]), 0))])
            xctx.close()
            WOS = ctx.enter_context(nc.sbuf_tensor("WOS", [128, NCH * D], BF16))
            OUTF = ctx.enter_context(nc.sbuf_tensor("OUTF", [128, NCH * 512], F32))
            yt0 = ctx.enter_context(nc.sbuf_tensor("yt0", [128, D], F32))
            xbC = [ctx.enter_context(nc.sbuf_tensor("xc0", [128, D], F32)),
                   ctx.enter_context(nc.sbuf_tensor("xc1", [128, D], F32))]
            sqC = ctx.enter_context(nc.sbuf_tensor("sqC", [128, D], F32))
            first = True
            for c in range(NCH):
                if first:
                    P.dma_serial(g, WOS[:, c * D:(c + 1) * D],
                                 wo[c * 128:(c + 1) * 128, :], s_w)
                    first = False
                else:
                    P.dma(g, WOS[:, c * D:(c + 1) * D], wo[c * 128:(c + 1) * 128, :], s_w)
            WOS_ALL = P.cnt[id(s_w)]
            of_ev = [None] * (8 * NCH)
            ty_ev = [None] * NT
            ev2_ev = [None] * NT
            fin_ev = [None] * NT
            vst2_ev = [None] * NT
            stb1 = P.cnt.get(id(s_st), 0)
            with ExitStack() as cctx:
                pU0 = _ps(cctx, nc, "pU0c", [128, 512], F32)
                pU1 = _ps(cctx, nc, "pU1c", [128, 512], F32)
                pT0 = _ps(cctx, nc, "pT0c", [128, 512], F32)
                pT1 = _ps(cctx, nc, "pT1c", [128, 512], F32)
                pV0 = _ps(cctx, nc, "pV0c", [128, 512], F32)
                pV1 = _ps(cctx, nc, "pV1c", [128, 512], F32)
                pUs2 = [pU0, pU1]
                for tr in range(8):
                    add_ev = None
                    for do in range(NCH):
                        k = tr * NCH + do
                        pU = pUs2[k % 2]
                        if k == 0:
                            P.w(t, att_all)
                            P.wd(t, s_w, WOS_ALL)
                        if k >= 2:
                            P.w(t, of_ev[k - 2])
                        ti = None
                        for c in range(NCH):
                            ti = t.matmul(pU[:], WOS[:, c * D + do * 128: c * D + (do + 1) * 128],
                                          ATT[:, c * T + tr * 512: c * T + (tr + 1) * 512],
                                          start=(c == 0), stop=(c == NCH - 1))
                        mm_ev = P.tlast(ti)
                        P.w(sc, mm_ev)
                        if tr > 0:
                            P.w(sc, ty_ev[tr * 4 - 1])
                        of_ev[k] = P.so(lambda: sc.activation(
                            out=OUTF[:, do * 512:(do + 1) * 512], in_=pU[:],
                            func=AF.Identity, bias=BO[:, do:do + 1]))
                        add_ev = of_ev[k]
                    for i in range(4):
                        gidx = tr * 4 + i
                        pa, pb2 = (pT0, pT1) if gidx % 2 == 0 else (pV0, pV1)
                        P.w(t, add_ev)
                        if gidx >= 2:
                            P.w(t, ev2_ev[gidx - 2])
                        ti = None
                        for do in range(NCH):
                            src = OUTF[:, do * 512 + i * 128: do * 512 + (i + 1) * 128]
                            dst = (pa if do < 4 else pb2)[:, (do % 4) * 128:(do % 4 + 1) * 128]
                            ti = t.transpose(dst, src, IDF[:])
                        ty_ev[gidx] = P.tlast(ti)
                        if gidx >= 2:
                            P.w(sy, fin_ev[gidx - 2])
                        ld_ev = P.dma(sy, xbC[gidx % 2][:],
                                      x[gidx * 128:(gidx + 1) * 128, :], lds[gidx % 2])
                        P.w(v, ty_ev[gidx])
                        if gidx >= 1:
                            P.wd(v, s_st, stb1 + 16 * gidx)
                        P.vo(lambda: v.tensor_copy(out=yt0[:, 0:512], in_=pa[:]))
                        ev2_ev[gidx] = P.vo(lambda: v.tensor_copy(out=yt0[:, 512:1024],
                                                                  in_=pb2[:]))
                        P.w(sc, ev2_ev[gidx])
                        if gidx >= 1:
                            P.w(sc, vst2_ev[gidx - 1])
                        sq_ev = P.so(lambda: sc.activation(out=sqC[:], in_=yt0[:],
                                                           func=AF.Square))
                        P.w(v, sq_ev)
                        P.vo(lambda: v.reduce_sum(st8[:, 0:1], yt0[:], axis=AX.X))
                        P.vo(lambda: v.reduce_sum(st8[:, 1:2], sqC[:], axis=AX.X))
                        P.vo(lambda: v.tensor_scalar_mul(st8[:, 2:3], st8[:, 0:1], -1.0 / D))
                        P.vo(lambda: v.tensor_scalar_mul(st8[:, 3:4], st8[:, 1:2], 1.0 / D))
                        P.vo(lambda: v.tensor_mul(st8[:, 4:5], st8[:, 2:3], st8[:, 2:3]))
                        vst2_ev[gidx] = P.vo(lambda: v.tensor_sub(st8[:, 3:4], st8[:, 3:4],
                                                                  st8[:, 4:5]))
                        P.w(sc, vst2_ev[gidx])
                        sd_ev = P.so(lambda: sc.activation(out=st8[:, 5:6], in_=st8[:, 3:4],
                                                           func=AF.Sqrt, bias=st8[:, 7:8]))
                        P.w(v, sd_ev)
                        P.vo(lambda: v.reciprocal(st8[:, 6:7], st8[:, 5:6]))
                        P.vo(lambda: v.tensor_scalar(out=yt0[:], in0=yt0[:],
                                                     scalar1=st8[:, 2:3],
                                                     scalar2=st8[:, 6:7],
                                                     op0=ALU.add, op1=ALU.mult))
                        P.w(v, ld_ev)
                        fin_ev[gidx] = P.vo(lambda: v.tensor_add(yt0[:], xbC[gidx % 2][:],
                                                                 yt0[:]))
                        P.w(sy, fin_ev[gidx])
                        P.dma(sy, y[gidx * 128:(gidx + 1) * 128, :], yt0[:], s_st)
                P.wd(sy, s_st, P.cnt[id(s_st)])
        emit(None)
        _stage_fence(P, env)


def _stage1_inputs(inputs):
    def pack(name):
        w_t = np.asarray(inputs[name]).T.astype(np.float32)
        return w_t.reshape(NCH, 128, 8, 128).transpose(2, 1, 0, 3).reshape(
            8 * 128, D)
    wqkv3 = np.ascontiguousarray(np.concatenate(
        [pack("L0_attn_wq"), pack("L0_attn_wk"), pack("L0_attn_wv")], axis=0))
    wo_t = np.ascontiguousarray(np.asarray(inputs["L0_attn_wo"]).T).astype(np.float32)
    vecs = np.asarray(inputs["L0_attn_bo"], np.float32).reshape(D, 1)
    return {"s1_wqkv": wqkv3, "s1_wo": wo_t, "s1_vecs": vecs}



HID2 = 2816
NF2 = HID2 // 128


def _rotary_tables():
    DH, WIN, N = 64, 512, T
    half = np.arange(0, DH, 2, dtype=np.float64)
    inv = 1.0 / 10000.0 ** (half / DH)
    tt = np.arange(N, dtype=np.float64)
    fr = tt[:, None] * inv[None, :]
    fr = np.concatenate([fr, fr], axis=1)
    base = (half + 0.4 * DH) / (1.4 * DH)
    power = (tt - N // 2) / float(WIN // 2)
    scl = base[None, :] ** power[:, None]
    scl = np.concatenate([scl, scl], axis=1)
    def dup(a):
        return np.ascontiguousarray(np.concatenate([a, a], axis=0))
    cosq = dup((np.cos(fr) * scl).T.astype(BF16NP))
    sinq = dup((np.sin(fr) * scl).T.astype(BF16NP))
    cosk = dup((np.cos(fr) / scl).T.astype(BF16NP))
    sink = dup((np.sin(fr) / scl).T.astype(BF16NP))
    perm = np.zeros((64, 64), np.float32)
    for d in range(32):
        perm[d + 32, d] = -1.0
        perm[d, d + 32] = 1.0
    permc = np.zeros((128, 128), BF16NP)
    permc[:64, :64] = perm
    permc[64:, 64:] = perm
    return cosq, sinq, cosk, sink, permc


def emit_stage2(nc, P, env):
    x = env["x1"]
    wqkv = env["s2_wqkvp"]
    wol = env["s2_wol"]
    w1t = env["s2_w1t"]
    w2t = env["s2_w2t"]
    vec3 = env["s2_vec3"]
    qsks = env["s2_qsks"]
    b1p = env["s2_b1p"]
    y = env["x2"]
    xi = env["s2_xi"]
    idb = env["idb"]
    idf = env["idf"]
    cosqc = env["s2_cosqc"]
    sinqc = env["s2_sinqc"]
    coskc = env["s2_coskc"]
    sinkc = env["s2_sinkc"]
    permc = env["s2_permc"]
    amgec = env["s2_amgec"]
    amlec = env["s2_amlec"]
    ones2c = env["ones2c"]
    sel65c = env["sel65c"]
    cselc = env["cselc"]
    sh1c = env["s2_sh1c"]
    e0c = env["s2_e0c"]
    one1c = env["s2_one1c"]

    TP1 = T + 1

    ctx = ExitStack()
    with ctx:
        XIB = _sb(ctx, nc, "XIB", [128, NCH * TP1], BF16)
        IDB = _sb(ctx, nc, "IDB", [128, 128], BF16)
        IDF = _sb(ctx, nc, "IDF", [128, 128], F32)
        SH1 = _sb(ctx, nc, "SH1", [128, 128], F32)
        E0 = _sb(ctx, nc, "E0", [1, 128], F32)
        ONE1 = _sb(ctx, nc, "ONE1", [1, 128], F32)
        PERM = _sb(ctx, nc, "PERM", [128, 128], BF16)
        ONES2 = _sb(ctx, nc, "ONES2", [128, 2], F32)
        SEL65 = _sb(ctx, nc, "SEL65", [65, 64], F32)
        CSEL = _sb(ctx, nc, "CSEL", [2, 256], F32)
        VEC = _sb(ctx, nc, "VEC", [128, 3 * NCH], F32)
        B1A = _sb(ctx, nc, "B1A", [128, NF2], F32)
        B1G = _sb(ctx, nc, "B1G", [128, NF2], F32)
        QS2 = _sb(ctx, nc, "QS2", [2, 256], F32)
        CST = _sb(ctx, nc, "CST", [2, 72], F32)
        CB = _sb(ctx, nc, "CB", [128, 2], F32)
        st8 = _sb(ctx, nc, "st8", [128, 16], F32)
        xb0 = _sb(ctx, nc, "xb0", [128, D], F32)
        xb1 = _sb(ctx, nc, "xb1", [128, D], F32)
        sq = _sb(ctx, nc, "sq", [128, D], F32)
        asm = _sb(ctx, nc, "asm", [128, D], F32)
        rowb = _sb(ctx, nc, "rowb", [1, 512], F32)
        xhbt = _sb(ctx, nc, "xhbt", [128, D], BF16)
        yt0 = _sb(ctx, nc, "yt0", [128, D], F32)
        s_w = env["s_w"]
        s_ld = env["s_ld"]
        s_le = env["s_le"]
        s_st = env["s_st"]
        s_sr = env["s_sr"]
        s_sx = env["s_sx"]
        s_wp = env["s_wp"]
        s_wf = env["s_wf"]
        s_wg = env["s_wg"]

        xbufs = [xb0, xb1]
        lds = [s_ld, s_le]

        def emit(_):
            v, sc, t, sy, g = P.v, P.s, P.t, P.sy, P.g

            def ln_stats(src, sqbuf):
                if "v" in P.last:
                    P.w(sc, P.last["v"])
                sq_ev = P.so(lambda: sc.activation(out=sqbuf[:], in_=src[:],
                                                   func=AF.Square))
                P.w(v, sq_ev)
                P.vo(lambda: v.reduce_sum(st8[:, 0:1], src[:], axis=AX.X))
                P.vo(lambda: v.reduce_sum(st8[:, 1:2], sqbuf[:], axis=AX.X))
                P.vo(lambda: v.tensor_scalar_mul(st8[:, 2:3], st8[:, 0:1], -1.0 / D))
                P.vo(lambda: v.tensor_scalar_mul(st8[:, 3:4], st8[:, 1:2], 1.0 / D))
                P.vo(lambda: v.tensor_mul(st8[:, 4:5], st8[:, 2:3], st8[:, 2:3]))
                vst = P.vo(lambda: v.tensor_sub(st8[:, 3:4], st8[:, 3:4], st8[:, 4:5]))
                P.w(sc, vst)
                sd = P.so(lambda: sc.activation(out=st8[:, 5:6], in_=st8[:, 3:4],
                                                func=AF.Sqrt, bias=st8[:, 7:8]))
                P.w(v, sd)
                P.vo(lambda: v.reciprocal(st8[:, 6:7], st8[:, 5:6]))
                return vst, sq_ev

            for dst, src in ((IDB, idb), (IDF, idf), (SH1, sh1c), (E0, e0c),
                             (ONE1, one1c), (PERM, permc), (ONES2, ones2c),
                             (SEL65, sel65c), (CSEL, cselc)):
                P.dma(g, dst[:], src.ap(), s_w)
            W_ALL0 = P.cnt[id(s_w)]
            P.wd(v, s_w, W_ALL0)
            P.vo(lambda: v.memset(st8[:, 7:8], EPS_LN))
            P.vo(lambda: v.memset(st8[:, 15:16], EPS_LN))
            gz_ev = P.go(lambda: g.memset(XIB[:], 0.0))

            xh_ev = [None] * NT
            ev_ev = [None] * NT
            vst_ev = [None] * NT
            tp_ev = [None] * NT
            sqA_ev = [None] * NT
            stb2 = P.cnt.get(id(s_st), 0)
            with ExitStack() as actx:
                tpA = _ps(actx, nc, "tpA", [128, 1024], BF16)
                tpB = _ps(actx, nc, "tpB", [128, 1024], BF16)
                sq2 = actx.enter_context(nc.sbuf_tensor("sq2", [128, D], F32))
                sqs2 = [sq, sq2]
                yts2 = [yt0, asm]

                def s2a_part1(i):
                    xb = xbufs[i % 2]
                    o = 8 * (i % 2)
                    sqb = sqs2[i % 2]
                    if i >= 2:
                        P.w(sy, xh_ev[i - 2])
                        P.w(sy, sqA_ev[i - 2])
                    ld_ev = P.dma(sy, xb[:], x[i * 128:(i + 1) * 128, :], lds[i % 2])
                    P.w(sc, ld_ev)
                    if i >= 2:
                        P.w(sc, vst_ev[i - 2])
                    sqA_ev[i] = P.so(lambda: sc.activation(out=sqb[:], in_=xb[:],
                                                           func=AF.Square))
                    P.w(v, ld_ev)
                    P.w(v, sqA_ev[i])
                    P.vo(lambda: v.reduce_sum(st8[:, o + 0:o + 1], xb[:], axis=AX.X))
                    P.vo(lambda: v.reduce_sum(st8[:, o + 1:o + 2], sqb[:], axis=AX.X))
                    P.vo(lambda: v.tensor_scalar_mul(st8[:, o + 2:o + 3],
                                                     st8[:, o + 0:o + 1], -1.0 / D))
                    P.vo(lambda: v.tensor_scalar_mul(st8[:, o + 3:o + 4],
                                                     st8[:, o + 1:o + 2], 1.0 / D))
                    P.vo(lambda: v.tensor_mul(st8[:, o + 4:o + 5], st8[:, o + 2:o + 3],
                                              st8[:, o + 2:o + 3]))
                    vst_ev[i] = P.vo(lambda: v.tensor_sub(st8[:, o + 3:o + 4],
                                                          st8[:, o + 3:o + 4],
                                                          st8[:, o + 4:o + 5]))

                def s2a_part2(i):
                    xb = xbufs[i % 2]
                    o = 8 * (i % 2)
                    yt = yts2[i % 2]
                    P.w(sc, vst_ev[i])
                    sd_ev = P.so(lambda: sc.activation(out=st8[:, o + 5:o + 6],
                                                       in_=st8[:, o + 3:o + 4],
                                                       func=AF.Sqrt,
                                                       bias=st8[:, o + 7:o + 8]))
                    P.w(v, sd_ev)
                    P.vo(lambda: v.reciprocal(st8[:, o + 6:o + 7], st8[:, o + 5:o + 6]))
                    if i >= 2:
                        P.w(v, tp_ev[i - 1])
                        P.wd(v, s_st, stb2 + 16 * (i - 1))
                    P.vo(lambda: v.tensor_scalar(out=yt[:], in0=xb[:],
                                                 scalar1=st8[:, o + 2:o + 3],
                                                 scalar2=st8[:, o + 6:o + 7],
                                                 op0=ALU.add, op1=ALU.mult))
                    xh_ev[i] = P.vo(lambda: v.tensor_copy(out=xhbt[:], in_=yt[:]))
                    P.w(sy, xh_ev[i])
                    P.dma(sy, xi[i * 128:(i + 1) * 128, :], yt[:], s_st)
                    tp = tpA if i % 2 == 0 else tpB
                    P.w(t, xh_ev[i])
                    if i == 0:
                        P.wd(t, s_w, W_ALL0)
                        P.w(t, gz_ev)
                    if i >= 2:
                        P.w(t, ev_ev[i - 2])
                    ti = None
                    for c in range(NCH):
                        ti = t.transpose(tp[:, c * 128:(c + 1) * 128],
                                         xhbt[:, c * 128:(c + 1) * 128], IDB[:])
                    tp_ev[i] = P.tlast(ti)
                    P.w(v, tp_ev[i])
                    for c in range(NCH):
                        ev_ev[i] = P.vo(lambda c=c, tp=tp, i=i: v.tensor_copy(
                            out=XIB[:, c * TP1 + 1 + i * 128: c * TP1 + 1 + (i + 1) * 128],
                            in_=tp[:, c * 128:(c + 1) * 128]))

                s2a_part1(0)
                for i in range(NT):
                    if i + 1 < NT:
                        s2a_part1(i + 1)
                    s2a_part2(i)
            P.fence(extra=[(s_ld, P.cnt.get(id(s_ld), 0)), (s_le, P.cnt.get(id(s_le), 0)),
                           (s_st, P.cnt.get(id(s_st), 0)), (s_w, P.cnt.get(id(s_w), 0))])

            for l in range(2):
                for vi in range(3):
                    P.dma_serial(g, VEC[:, vi * NCH:(vi + 1) * NCH],
                                 vec3[(l * 3 + vi) * 128:(l * 3 + vi + 1) * 128, :],
                                 s_w)
                P.dma_serial(g, B1A[:], b1p[(l * 2 + 0) * 128:(l * 2 + 1) * 128, :],
                             s_w)
                P.dma_serial(g, B1G[:], b1p[(l * 2 + 1) * 128:(l * 2 + 2) * 128, :],
                             s_w)
                qz_ev = P.go(lambda: g.memset(QS2[:], 0.0))
                P.w(g, qz_ev)
                P.dma_serial(g, QS2[0:1, 0:64],
                             qsks[(l * 2 + 0) * 64:(l * 2 + 0) * 64 + 64, :].rearrange("a b -> b a"), s_w)
                P.dma_serial(g, QS2[1:2, 64:128],
                             qsks[(l * 2 + 0) * 64:(l * 2 + 0) * 64 + 64, :].rearrange("a b -> b a"), s_w)
                P.dma_serial(g, QS2[0:1, 128:192],
                             qsks[(l * 2 + 1) * 64:(l * 2 + 1) * 64 + 64, :].rearrange("a b -> b a"), s_w)
                P.dma_serial(g, QS2[1:2, 192:256],
                             qsks[(l * 2 + 1) * 64:(l * 2 + 1) * 64 + 64, :].rearrange("a b -> b a"), s_w)
                vec_all = P.cnt[id(s_w)]

                with ExitStack() as bctx:
                    ATT2 = bctx.enter_context(nc.sbuf_tensor(f"ATT2_{l}", [128, 4 * T], BF16))
                    attctx = ExitStack()
                    QP = attctx.enter_context(nc.sbuf_tensor(f"QP_{l}", [128, T], BF16))
                    KP = attctx.enter_context(nc.sbuf_tensor(f"KP_{l}", [128, T], BF16))
                    VSTG = attctx.enter_context(nc.sbuf_tensor(f"VSTG_{l}", [128, 512], BF16))
                    VT = attctx.enter_context(nc.sbuf_tensor(f"VT_{l}", [128, NT * 130], BF16))
                    QR = attctx.enter_context(nc.sbuf_tensor(f"QR_{l}", [128, 512], BF16))
                    RT = attctx.enter_context(nc.sbuf_tensor(f"RT_{l}", [128, 512], BF16))
                    WP = attctx.enter_context(nc.sbuf_tensor(f"WP_{l}", [128, 3 * D], BF16))
                    PB = [attctx.enter_context(nc.sbuf_tensor(f"PB{i}_{l}", [128, 512], BF16))
                          for i in range(4)]
                    CMB = [attctx.enter_context(nc.sbuf_tensor(f"CMB{i}_{l}", [65, 512], F32))
                           for i in range(2)]
                    qsqb = attctx.enter_context(nc.sbuf_tensor(f"qsqb_{l}", [128, 512], F32))
                    COSQ = attctx.enter_context(nc.sbuf_tensor(f"COSQ_{l}", [128, T], BF16))
                    SINQ = attctx.enter_context(nc.sbuf_tensor(f"SINQ_{l}", [128, T], BF16))
                    COSK = attctx.enter_context(nc.sbuf_tensor(f"COSK_{l}", [128, T], BF16))
                    SINK = attctx.enter_context(nc.sbuf_tensor(f"SINK_{l}", [128, T], BF16))
                    AMGE = attctx.enter_context(nc.sbuf_tensor(f"AMGE_{l}", [128, 1024], BF16))
                    AMLE = attctx.enter_context(nc.sbuf_tensor(f"AMLE_{l}", [128, 1024], BF16))
                    for dst, src in ((COSQ, cosqc), (SINQ, sinqc), (COSK, coskc),
                                     (SINK, sinkc), (AMGE, amgec), (AMLE, amlec)):
                        P.dma_serial(g, dst[:], src.ap(), s_w)
                    tabs_all = P.cnt[id(s_w)]
                    pQ0 = _ps(bctx, nc, f"pQ0_{l}", [128, 512], F32)
                    pQ1 = _ps(bctx, nc, f"pQ1_{l}", [128, 512], F32)
                    pS0 = _ps(bctx, nc, f"pS0_{l}", [128, 512], F32)
                    pS1 = _ps(bctx, nc, f"pS1_{l}", [128, 512], F32)
                    pAV0 = _ps(bctx, nc, f"pAV0_{l}", [65, 512], F32)
                    pAV1 = _ps(bctx, nc, f"pAV1_{l}", [65, 512], F32)
                    pR0 = _ps(bctx, nc, f"pR0_{l}", [128, 512], F32)
                    pR1 = _ps(bctx, nc, f"pR1_{l}", [128, 512], F32)
                    pQs = [pQ0, pQ1]
                    pSs = [pS0, pS1]
                    pAVs = [pAV0, pAV1]

                    vt_ini = P.go(lambda: g.memset(VT[:], 1.0))
                    P.wd(v, s_w, vec_all)
                    P.vo(lambda: v.tensor_mul(CST[0:1, 0:64], QS2[0:1, 0:64],
                                              QS2[0:1, 128:192]))
                    P.vo(lambda: v.reduce_max(CST[0:1, 64:65], CST[0:1, 0:64], axis=AX.X,
                                              apply_absolute_value=True))
                    cmax_ev = P.vo(lambda: v.tensor_scalar_mul(CST[0:1, 64:65],
                                                               CST[0:1, 64:65], -8.0))
                    P.w(t, cmax_ev)
                    if P.last.get("s") is not None:
                        P.w(t, P.last["s"])
                    ti = t.matmul(pR0[:, 0:1], ONE1[:],
                                  CST[0:1, 64:65], start=True, stop=True)
                    cbm_ev = P.tlast(ti)
                    P.w(v, cbm_ev)
                    cbv_ev = P.vo(lambda: v.tensor_copy(out=CB[:, 0:1], in_=pR0[:, 0:1]))
                    prev_pair_fin = [None]
                    pair_last_exp2 = [None]
                    for hp in range(4):
                        if prev_pair_fin[0] is not None:
                            P.w(g, prev_pair_fin[0])
                        for mi in range(3):
                            tile = mi * 4 + hp
                            P.dma_serial(
                                g, WP[:, mi * D:(mi + 1) * D],
                                wqkv[(l * 12 + tile) * 128:
                                     (l * 12 + tile + 1) * 128, :], s_wp)
                        wpt = P.cnt[id(s_wp)]
                        evq = [None] * 24
                        vt_ev = [None] * NT
                        cst_ev = {}
                        cst_last = None
                        qsq_last = None
                        prev_evq = None
                        for mi, DF in enumerate((QP, KP, None)):
                            for trg in range(8):
                                kk = mi * 8 + trg
                                pQ = pQs[kk % 2]
                                if kk == 0:
                                    P.wd(t, s_wp, wpt)
                                    if prev_pair_fin[0] is not None:
                                        P.w(t, prev_pair_fin[0])
                                    if pair_last_exp2[0] is not None:
                                        P.w(t, pair_last_exp2[0])
                                    if hp == 0:
                                        P.w(t, ev_ev[NT - 1])
                                if kk >= 2:
                                    P.w(t, evq[kk - 2])
                                ti = None
                                for c in range(NCH):
                                    off = c * TP1 + (1 if c < 4 else 0)
                                    ti = t.matmul(
                                        pQ[:],
                                        WP[:, mi * D + c * 128: mi * D + (c + 1) * 128],
                                        XIB[:, off + trg * 512: off + (trg + 1) * 512],
                                        start=(c == 0), stop=(c == NCH - 1))
                                mm_ev = P.tlast(ti)
                                if mi == 2:
                                    P.w(sc, mm_ev)
                                    if trg >= 1:
                                        P.w(sc, vt_ev[trg * 4 - 1])
                                    evq[kk] = P.so(lambda: sc.activation(
                                        out=VSTG[:], in_=pQ[:], func=AF.Identity))
                                    for j in range(4):
                                        i2 = trg * 4 + j
                                        pT = pSs[i2 % 2]
                                        P.w(t, evq[kk])
                                        if i2 >= 2:
                                            P.w(t, vt_ev[i2 - 2])
                                        ti = t.matmul(pT[:, 0:128],
                                                      VSTG[:, j * 128:(j + 1) * 128],
                                                      IDB[:], start=True, stop=True)
                                        tt_ev = P.tlast(ti)
                                        P.w(v, tt_ev)
                                        if hp == 0 and i2 == 0:
                                            P.w(v, vt_ini)
                                        P.vo(lambda i2=i2: v.tensor_copy(
                                            out=VT[:, i2 * 130: i2 * 130 + 64],
                                            in_=pT[:, 0:64]))
                                        vt_ev[i2] = P.vo(lambda i2=i2: v.tensor_copy(
                                            out=VT[:, i2 * 130 + 65: i2 * 130 + 129],
                                            in_=pT[:, 64:128]))
                                    continue
                                P.w(sc, mm_ev)
                                if qsq_last is not None:
                                    P.w(sc, qsq_last)
                                sqv = P.so(lambda: sc.activation(out=qsqb[:], in_=pQ[:],
                                                                 func=AF.Square))
                                P.w(t, sqv)
                                if cst_last is not None:
                                    P.w(t, cst_last)
                                ti = t.matmul(pAV0[0:2, :], ONES2[:], qsqb[:],
                                              start=True, stop=True)
                                nm_ev = P.tlast(ti)
                                qsq_last = nm_ev
                                P.w(sc, nm_ev)
                                sd = P.so(lambda: sc.activation(out=qsqb[0:2, 0:512],
                                                                in_=pAV0[0:2, :],
                                                                func=AF.Sqrt))
                                cst_last = sd
                                P.w(v, sd)
                                P.vo(lambda: v.tensor_scalar_max(qsqb[0:2, 0:512],
                                                                 qsqb[0:2, 0:512], 1e-12))
                                rstd_ev = P.vo(lambda: v.reciprocal(qsqb[0:2, 0:512],
                                                                    qsqb[0:2, 0:512]))
                                P.w(t, rstd_ev)
                                if prev_evq is not None:
                                    P.w(t, prev_evq)
                                qoff = 0 if mi == 0 else 128
                                ti = t.matmul(pR0[:], QS2[:, qoff:qoff + 128],
                                              qsqb[0:2, 0:512], start=True, stop=True)
                                bc_ev = P.tlast(ti)
                                P.w(sc, mm_ev)
                                if prev_evq is not None:
                                    P.w(sc, prev_evq)
                                ev_raw = P.so(lambda: sc.activation(out=QR[:], in_=pQ[:],
                                                                    func=AF.Identity))
                                P.w(v, ev_raw)
                                P.w(v, bc_ev)
                                qh_ev = P.vo(lambda: v.tensor_mul(RT[:], QR[:], pR0[:]))
                                P.w(t, qh_ev)
                                ti = t.matmul(pR1[:], PERM[:], RT[:], start=True, stop=True)
                                rot_ev = P.tlast(ti)
                                COS = COSQ if mi == 0 else COSK
                                SIN = SINQ if mi == 0 else SINK
                                P.w(v, rot_ev)
                                P.wd(v, s_w, tabs_all)
                                P.vo(lambda: v.tensor_mul(
                                    QR[:, 0:512], RT[:, 0:512],
                                    COS[:, trg * 512:(trg + 1) * 512]))
                                P.vo(lambda: v.tensor_mul(
                                    RT[:, 0:512], pR1[:, 0:512],
                                    SIN[:, trg * 512:(trg + 1) * 512]))
                                evq[kk] = P.vo(lambda: v.tensor_add(
                                    DF[:, trg * 512:(trg + 1) * 512], QR[:, 0:512],
                                    RT[:, 0:512]))
                                prev_evq = evq[kk]
                        vt_done = vt_ev[NT - 1]

                        last_fin = None
                        for hh in range(2):
                            rbase = 64 * hh
                            spool = [pSs[hh], pQs[hh]]
                            sfree = [[vt_done], [evq[23]]]
                            sidx = 0
                            pAVh = pAVs[hh]
                            comb = CMB[hh]
                            prev_fin = None
                            prev_exp = None
                            pb_av = [None, None]
                            for w in range(8):
                                q0 = w * 512
                                chunks = list(range(0 if w > 0 else 4, 8))
                                nch_w = len(chunks)
                                base = sidx
                                sidx += nch_w + 1
                                av_ev = None
                                smm = [None] * nch_w

                                def emit_S(k, w=w, q0=q0, chunks=chunks):
                                    ci = chunks[k]
                                    c0 = (w - 1) * 512 + ci * 128
                                    sl = (base + k) % 2
                                    pS_ = spool[sl]
                                    if k == 0 and hh == 0 and w == 0:
                                        P.w(t, cbv_ev)
                                    for evf in sfree[sl]:
                                        P.w(t, evf)
                                    sfree[sl] = []
                                    ti = t.matmul(pS_[:, 0:512],
                                                  KP[rbase:rbase + 64, c0:c0 + 128],
                                                  QP[rbase:rbase + 64, q0:q0 + 512],
                                                  start=True, stop=True)
                                    smm[k] = P.tlast(ti)

                                emit_S(0)
                                if hh == 0 and w == 0:
                                    P.w(sc, cbv_ev)
                                    P.wd(v, s_w, tabs_all)
                                for k in range(nch_w):
                                    if k + 1 < nch_w:
                                        emit_S(k + 1)
                                    sl = (base + k) % 2
                                    pS_ = spool[sl]
                                    pbslot = k % 2
                                    pb = PB[hh * 2 + pbslot]
                                    ci = chunks[k]
                                    c0 = (w - 1) * 512 + ci * 128
                                    if ci < 4:
                                        M = AMGE[:, 512 - 128 * ci: 1024 - 128 * ci]
                                    else:
                                        M = AMLE[:, 512 - 128 * (ci - 4):
                                                 1024 - 128 * (ci - 4)]
                                    P.w(sc, smm[k])
                                    if pb_av[pbslot] is not None:
                                        P.w(sc, pb_av[pbslot])
                                    e_ev = P.so(lambda: sc.activation(
                                        out=pb[:, 0:512], in_=pS_[:, 0:512], func=AF.Exp,
                                        bias=CB[:, 0:1], scale=8.0))
                                    sfree[sl].append(e_ev)
                                    prev_exp = e_ev
                                    P.w(v, e_ev)
                                    mask_ev = P.vo(lambda: v.tensor_mul(
                                        pb[:, 0:512], pb[:, 0:512], M))
                                    P.w(t, mask_ev)
                                    if k == 0 and prev_fin is not None:
                                        P.w(t, prev_fin)
                                    ti = t.matmul(pAVh[:, 0:512],
                                                  VT[:, (c0 // 128) * 130 + 65 * hh:
                                                     (c0 // 128) * 130 + 65 * hh + 65],
                                                  pb[:, 0:512], start=(k == 0),
                                                  stop=(k == nch_w - 1),
                                                  skip_group_check=True)
                                    av_ev = P.tlast(ti)
                                    pb_av[pbslot] = av_ev
                                P.w(v, av_ev)
                                P.vo(lambda: v.tensor_copy(out=comb[:, 0:512],
                                                           in_=pAVh[:, 0:512]))
                                rc_ev = P.vo(lambda: v.reciprocal(comb[64:65, 0:512],
                                                                  comb[64:65, 0:512]))
                                rbslot = (base + nch_w) % 2
                                pS_ = spool[rbslot]
                                P.w(t, rc_ev)
                                for evf in sfree[rbslot]:
                                    P.w(t, evf)
                                sfree[rbslot] = []
                                ti = t.matmul(pS_[0:64, 0:512], SEL65[:], comb[:, 0:512],
                                              start=True, stop=True)
                                rbc_ev = P.tlast(ti)
                                P.w(v, rbc_ev)
                                last_fin = P.vo(lambda pS_=pS_: v.tensor_mul(
                                    ATT2[rbase:rbase + 64, hp * T + q0: hp * T + q0 + 512],
                                    comb[0:64, 0:512], pS_[0:64, 0:512]))
                                sfree[rbslot].append(last_fin)
                                prev_fin = last_fin
                        prev_pair_fin[0] = last_fin
                        pair_last_exp2[0] = prev_exp
                    att_done = prev_pair_fin[0]
                    P.fence(extra=[(s_wp, P.cnt.get(id(s_wp), 0)),
                                   (s_w, P.cnt.get(id(s_w), 0))])
                    attctx.close()
                    OUTF = bctx.enter_context(nc.sbuf_tensor(f"OUTF_{l}", [128, NCH * 512], F32))
                    WPO = bctx.enter_context(nc.sbuf_tensor(f"WPO_{l}", [128, 4 * D], BF16))

                    for c in range(4):
                        for fo in range(NCH):
                            P.dma_serial(g, WPO[:, c * D + fo * 128: c * D + (fo + 1) * 128],
                                         wol[l * 512 + c * 128: l * 512 + (c + 1) * 128,
                                             fo * 128:(fo + 1) * 128], s_wp)
                    wot = P.cnt[id(s_wp)]
                    of_ev = [None] * (8 * NCH)
                    ty_ev = [None] * NT
                    ev2_ev = [None] * NT
                    fin_ev = [None] * NT
                    vstc_ev = [None] * NT
                    xfe_ev = [None] * NT
                    rb_z = P.vo(lambda: v.memset(rowb[:], 0.0))
                    sh_prev = [None]
                    for tr in range(8):
                        for do in range(NCH):
                            k = tr * NCH + do
                            pU = pQs[k % 2]
                            if k == 0:
                                P.w(t, att_done)
                                if pair_last_exp2[0] is not None:
                                    P.w(t, pair_last_exp2[0])
                                P.wd(t, s_wp, wot)
                            if k >= 2:
                                P.w(t, of_ev[k - 2])
                            ti = None
                            for c in range(4):
                                ti = t.matmul(pU[:],
                                              WPO[:, c * D + do * 128: c * D + (do + 1) * 128],
                                              ATT2[:, c * T + tr * 512:
                                                   c * T + (tr + 1) * 512],
                                              start=(c == 0), stop=(c == 3))
                            mm_ev = P.tlast(ti)
                            P.w(sc, mm_ev)
                            if tr > 0:
                                P.w(sc, ty_ev[tr * 4 - 1])
                            of_ev[k] = P.so(lambda: sc.activation(
                                out=OUTF[:, do * 512:(do + 1) * 512], in_=pU[:],
                                func=AF.Identity))
                        for i in range(4):
                            gidx = tr * 4 + i
                            pa, pb2 = (pR0, pR1) if gidx % 2 == 0 else (pS0, pS1)
                            P.w(t, of_ev[tr * NCH + NCH - 1])
                            if gidx >= 2:
                                P.w(t, ev2_ev[gidx - 2])
                            ti = None
                            for do in range(NCH):
                                src = OUTF[:, do * 512 + i * 128: do * 512 + (i + 1) * 128]
                                dst = (pa if do < 4 else pb2)[:, (do % 4) * 128:
                                                              (do % 4 + 1) * 128]
                                ti = t.transpose(dst, src, IDF[:])
                            ty_ev[gidx] = P.tlast(ti)
                            if gidx >= 2:
                                P.w(sy, fin_ev[gidx - 2])
                            P.wd(sy, s_st, 16 * NT * (l + 1) if False else P.cnt[id(s_st)])
                            ld_ev = P.dma(sy, xbufs[gidx % 2][:],
                                          xi[gidx * 128:(gidx + 1) * 128, :], lds[gidx % 2])
                            P.w(v, ty_ev[gidx])
                            if gidx >= 1:
                                P.w(v, vstc_ev[gidx - 1] or ty_ev[gidx])
                            if sh_prev[0] is not None:
                                P.w(v, sh_prev[0])
                            P.wd(v, s_st, P.cnt[id(s_st)])
                            P.vo(lambda: v.tensor_copy(out=yt0[:, 0:512], in_=pa[:]))
                            ev2_ev[gidx] = P.vo(lambda: v.tensor_copy(out=yt0[:, 512:1024],
                                                                      in_=pb2[:]))
                            P.w(v, ld_ev)
                            xm_ev = P.vo(lambda: v.tensor_add(yt0[:], xbufs[gidx % 2][:],
                                                              yt0[:]))
                            P.w(sy, xm_ev)
                            st_ev = P.dma(sy, xi[gidx * 128:(gidx + 1) * 128, :], yt0[:],
                                          s_st)
                            if gidx >= 1:
                                P.wd(g, s_st, P.cnt[id(s_st)] - 16)
                                if sh_prev[0] is not None:
                                    P.w(g, sh_prev[0])
                                rb_ev = P.dma(g, rowb[:],
                                              xi[gidx * 128 - 1: gidx * 128, 512:1024],
                                              s_sr)
                            else:
                                rb_ev = None
                            psh = pb2 if gidx % 2 == 0 else pR1
                            psh = pQs[(tr * NCH) % 2] if False else psh
                            P.w(t, ev2_ev[gidx])
                            P.w(t, xm_ev)
                            ti = t.matmul(pa[:, 0:512], SH1[:], yt0[:, 512:1024],
                                          start=True, stop=False)
                            if rb_ev is not None:
                                P.wd(t, s_sr, rb_ev[1])
                            else:
                                P.w(t, rb_z)
                            ti = t.matmul(pa[:, 0:512], E0[:], rowb[0:1, 0:512],
                                          start=False, stop=True)
                            sh_ev = P.tlast(ti)
                            sh_prev[0] = sh_ev
                            P.w(v, sh_ev)
                            P.vo(lambda: v.tensor_copy(out=asm[:, 0:512], in_=yt0[:, 0:512]))
                            asm_ev = P.vo(lambda: v.tensor_copy(out=asm[:, 512:1024],
                                                                in_=pa[:, 0:512]))
                            P.w(sc, asm_ev)
                            if gidx >= 1:
                                P.w(sc, vstc_ev[gidx - 1])
                            vstc_ev[gidx], _sqe = ln_stats(asm, sq)
                            xh2 = P.vo(lambda: v.tensor_scalar(
                                out=xhbt[:], in0=asm[:], scalar1=st8[:, 2:3],
                                scalar2=st8[:, 6:7], op0=ALU.add, op1=ALU.mult))
                            P.w(t, xh2)
                            if gidx >= 2:
                                P.w(t, xfe_ev[gidx - 2])
                            tpx = pb2
                            ti = None
                            for c in range(NCH):
                                ti = t.matmul(tpx[:, 0:128] if False else tpx[:, (c % 4) * 128:(c % 4 + 1) * 128],
                                              xhbt[:, c * 128:(c + 1) * 128], IDB[:],
                                              start=True, stop=True)
                                if c % 4 == 3 or c == NCH - 1:
                                    tx_ev = P.tlast(ti)
                                    P.w(v, tx_ev)
                                    for c2 in range(c - c % 4, c + 1):
                                        xfe_ev[gidx] = P.vo(lambda c2=c2: v.tensor_copy(
                                            out=XIB[:, c2 * TP1 + 1 + gidx * 128:
                                                    c2 * TP1 + 1 + (gidx + 1) * 128],
                                            in_=tpx[:, (c2 % 4) * 128:(c2 % 4 + 1) * 128]))
                                    if c == 3:
                                        P.w(t, xfe_ev[gidx])
                            fin_ev[gidx] = xfe_ev[gidx]
                    aff_ev = None
                    P.wd(v, s_w, vec_all)
                    for c in range(NCH):
                        P.w(v, fin_ev[NT - 1])
                        aff_ev = P.vo(lambda c=c: v.tensor_scalar(
                            out=XIB[:, c * TP1 + 1: c * TP1 + 1 + T],
                            in0=XIB[:, c * TP1 + 1: c * TP1 + 1 + T],
                            scalar1=VEC[:, 0 * NCH + c: 0 * NCH + c + 1],
                            scalar2=VEC[:, 1 * NCH + c: 1 * NCH + c + 1],
                            op0=ALU.mult, op1=ALU.add))
                    P.fence(extra=[(s_ld, P.cnt.get(id(s_ld), 0)),
                                   (s_le, P.cnt.get(id(s_le), 0)),
                                   (s_st, P.cnt.get(id(s_st), 0)),
                                   (s_sr, P.cnt.get(id(s_sr), 0)),
                                   (s_wp, P.cnt.get(id(s_wp), 0)),
                                   (s_w, P.cnt.get(id(s_w), 0))])

                with ExitStack() as fctx:
                    GU = fctx.enter_context(nc.sbuf_tensor(f"GU_{l}", [128, NF2 * 1024], BF16))
                    W2B = fctx.enter_context(nc.sbuf_tensor(f"W2B_{l}", [128, NCH * NF2 * 128],
                                                            BF16))
                    W1B = [fctx.enter_context(nc.sbuf_tensor(f"W1B{i}_{l}", [128, D], BF16))
                           for i in range(4)]
                    AH = fctx.enter_context(nc.sbuf_tensor(f"AH_{l}", [128, 512], BF16))
                    GH = fctx.enter_context(nc.sbuf_tensor(f"GH_{l}", [128, 512], BF16))
                    O2F = fctx.enter_context(nc.sbuf_tensor(f"O2F_{l}", [128, NCH * 512], F32))
                    pA = _ps(fctx, nc, f"pA_{l}", [128, 512], F32)
                    pG = _ps(fctx, nc, f"pG_{l}", [128, 512], F32)
                    pW0 = _ps(fctx, nc, f"pW0_{l}", [128, 512], F32)
                    pW1 = _ps(fctx, nc, f"pW1_{l}", [128, 512], F32)
                    pX0 = _ps(fctx, nc, f"pX0_{l}", [128, 512], F32)
                    pX1 = _ps(fctx, nc, f"pX1_{l}", [128, 512], F32)
                    pY0 = _ps(fctx, nc, f"pY0_{l}", [128, 512], F32)
                    pY1 = _ps(fctx, nc, f"pY1_{l}", [128, 512], F32)
                    pWs = [pW0, pW1]

                    gu_last = [None] * NF2
                    fin2_ev = [None] * NT
                    w1_dm = [None] * 4
                    w1_cons = [None] * 4
                    for tr2 in range(4):
                        t0 = tr2 * 1024
                        w2base = P.cnt.get(id(s_wg), 0)
                        if tr2 > 0 or l > 0:
                            P.w(g, P.last.get("t", (P.SG, 0)))
                        for fo in range(NCH):
                            P.dma_serial(
                                g, W2B[:, fo * NF2 * 128:(fo + 1) * NF2 * 128],
                                w2t[(l * NCH + fo) * 128:(l * NCH + fo + 1) * 128, :],
                                s_wg)
                        w2t_all = P.cnt[id(s_wg)]

                        def load_w1(ft):
                            sem2 = s_wf if ft % 2 == 0 else s_sr
                            for hf in range(2):
                                slot = (ft % 2) * 2 + hf
                                if w1_cons[slot] is not None:
                                    P.w(g, w1_cons[slot])
                                w1_dm[slot] = P.dma_serial(
                                    g, W1B[slot][:],
                                    w1t[((l * 2 + hf) * NF2 + ft) * 128:
                                        ((l * 2 + hf) * NF2 + ft + 1) * 128, :],
                                    sem2)

                        load_w1(0)
                        guev = None
                        for ft in range(NF2):
                            if ft + 1 < NF2:
                                load_w1(ft + 1)
                            for sb in range(2):
                                for hf, (pDst, BUF, BV) in enumerate(
                                        (((pA, AH, B1A)), ((pG, GH, B1G)))):
                                    slot = (ft % 2) * 2 + hf
                                    P.w(t, w1_dm[slot])
                                    if ft == 0 and sb == 0 and hf == 0 and tr2 == 0:
                                        P.w(t, aff_ev)
                                    if guev is not None:
                                        P.w(t, guev)
                                    ti = None
                                    for c in range(NCH):
                                        off = c * TP1 + 1
                                        ti = t.matmul(
                                            pDst[:],
                                            W1B[slot][:, c * 128:(c + 1) * 128],
                                            XIB[:, off + t0 + sb * 512:
                                                off + t0 + (sb + 1) * 512],
                                            start=(c == 0), stop=(c == NCH - 1))
                                    mmev = P.tlast(ti)
                                    P.w(sc, mmev)
                                    fn = AF.Identity if hf == 0 else AF.Gelu
                                    ev = P.so(lambda fn=fn, BUF=BUF, BV=BV, pDst=pDst, ft=ft:
                                              sc.activation(out=BUF[:], in_=pDst[:],
                                                            func=fn,
                                                            bias=BV[:, ft:ft + 1]))
                                P.w(v, ev)
                                guev = P.vo(lambda ft=ft, sb=sb: v.tensor_mul(
                                    GU[:, ft * 1024 + sb * 512: ft * 1024 + (sb + 1) * 512],
                                    AH[:], GH[:]))
                                w1_cons[(ft % 2) * 2 + 0] = guev
                                w1_cons[(ft % 2) * 2 + 1] = guev
                        gu_done = guev
                        o2_ev = [None] * (2 * NCH)
                        for sb in range(2):
                            for fo in range(NCH):
                                k = sb * NCH + fo
                                pU = pWs[k % 2]
                                if k == 0:
                                    P.w(t, gu_done)
                                    P.wd(t, s_wg, w2t_all)
                                if k >= 2:
                                    P.w(t, o2_ev[k - 2])
                                ti = None
                                for c in range(NF2):
                                    ti = t.matmul(
                                        pU[:],
                                        W2B[:, fo * NF2 * 128 + c * 128:
                                            fo * NF2 * 128 + (c + 1) * 128],
                                        GU[:, c * 1024 + sb * 512: c * 1024 + (sb + 1) * 512],
                                        start=(c == 0), stop=(c == NF2 - 1))
                                mm2 = P.tlast(ti)
                                P.w(sc, mm2)
                                if sb > 0 or tr2 > 0:
                                    pass
                                P.w(sc, P.last["v"])
                                o2_ev[k] = P.so(lambda fo=fo, pU=pU: sc.activation(
                                    out=O2F[:, fo * 512:(fo + 1) * 512], in_=pU[:],
                                    func=AF.Identity,
                                    bias=VEC[:, 2 * NCH + fo: 2 * NCH + fo + 1]))
                            for i in range(4):
                                gidx = tr2 * 8 + sb * 4 + i
                                pa2, pb3 = (pX0, pX1) if gidx % 2 == 0 else (pY0, pY1)
                                P.w(t, o2_ev[sb * NCH + NCH - 1])
                                if fin2_ev[gidx - 2] is not None if gidx >= 2 else False:
                                    P.w(t, fin2_ev[gidx - 2])
                                ti = None
                                for do in range(NCH):
                                    src = O2F[:, do * 512 + i * 128: do * 512 + (i + 1) * 128]
                                    dst = (pa2 if do < 4 else pb3)[:, (do % 4) * 128:
                                                                   (do % 4 + 1) * 128]
                                    ti = t.transpose(dst, src, IDF[:])
                                tyv = P.tlast(ti)
                                if gidx >= 2:
                                    P.w(sy, fin2_ev[gidx - 2])
                                P.wd(sy, s_st, P.cnt[id(s_st)])
                                ldv = P.dma(sy, xbufs[gidx % 2][:],
                                            xi[gidx * 128:(gidx + 1) * 128, :],
                                            lds[gidx % 2])
                                P.w(v, tyv)
                                P.wd(v, s_st, P.cnt[id(s_st)])
                                P.vo(lambda: v.tensor_copy(out=yt0[:, 0:512], in_=pa2[:]))
                                P.vo(lambda: v.tensor_copy(out=yt0[:, 512:1024], in_=pb3[:]))
                                P.w(v, ldv)
                                fin2_ev[gidx] = P.vo(lambda: v.tensor_add(
                                    yt0[:], xbufs[gidx % 2][:], yt0[:]))
                                P.w(sy, fin2_ev[gidx])
                                P.dma(sy, xi[gidx * 128:(gidx + 1) * 128, :], yt0[:], s_st)
                    P.fence(extra=[(s_ld, P.cnt.get(id(s_ld), 0)),
                                   (s_le, P.cnt.get(id(s_le), 0)),
                                   (s_st, P.cnt.get(id(s_st), 0)),
                                   (s_sr, P.cnt.get(id(s_sr), 0)),
                                   (s_wf, P.cnt.get(id(s_wf), 0)),
                                   (s_wg, P.cnt.get(id(s_wg), 0))])

                with ExitStack() as rctx:
                    tpA = _ps(rctx, nc, f"tpR0_{l}", [128, 1024], BF16)
                    tpB = _ps(rctx, nc, f"tpR1_{l}", [128, 1024], BF16)
                    rtp = [tpA, tpB]
                    rxh = [None] * NT
                    rtp_ev = [None] * NT
                    rev = [None] * NT
                    rvst = [None] * NT
                    rfin = [None] * NT
                    for i in range(NT):
                        xb = xbufs[i % 2]
                        if i >= 2:
                            P.w(sy, rxh[i - 2])
                        ld_ev = P.dma(sy, xb[:], xi[i * 128:(i + 1) * 128, :], lds[i % 2])
                        if l == 0:
                            P.w(v, ld_ev)
                            if i >= 2:
                                P.w(v, rtp_ev[i - 1])
                            rxh[i] = P.vo(lambda: v.tensor_copy(out=xhbt[:], in_=xb[:]))
                            tp = rtp[i % 2]
                            P.w(t, rxh[i])
                            if i >= 2:
                                P.w(t, rev[i - 2])
                            ti = None
                            for c in range(NCH):
                                ti = t.transpose(tp[:, c * 128:(c + 1) * 128],
                                                 xhbt[:, c * 128:(c + 1) * 128], IDB[:])
                            rtp_ev[i] = P.tlast(ti)
                            P.w(v, rtp_ev[i])
                            for c in range(NCH):
                                rev[i] = P.vo(lambda c=c: v.tensor_copy(
                                    out=XIB[:, c * TP1 + 1 + i * 128:
                                            c * TP1 + 1 + (i + 1) * 128],
                                    in_=tp[:, c * 128:(c + 1) * 128]))
                        else:
                            P.w(sc, ld_ev)
                            if i >= 1:
                                P.w(sc, rvst[i - 1])
                            P.w(v, ld_ev)
                            if i >= 2:
                                P.w(v, rfin[i - 2])
                            rvst[i], _sqe = ln_stats(xb, sq)
                            if i >= 1:
                                P.wd(v, s_st, P.cnt[id(s_st)])
                            P.vo(lambda: v.tensor_scalar(out=yt0[:], in0=xb[:],
                                                         scalar1=st8[:, 2:3],
                                                         scalar2=st8[:, 6:7],
                                                         op0=ALU.add, op1=ALU.mult))
                            if i >= 1:
                                P.w(sy, rfin[i - 1])
                            ld2 = P.dma(sy, asm[:], x[i * 128:(i + 1) * 128, :], s_sx)
                            P.wd(v, s_sx, ld2[1])
                            rxh[i] = rfin[i] = P.vo(lambda: v.tensor_add(yt0[:], asm[:],
                                                                         yt0[:]))
                            P.w(sy, rfin[i])
                            P.dma(sy, y[i * 128:(i + 1) * 128, :], yt0[:], s_st)
                    if l == 0:
                        P.fence(extra=[(s_ld, P.cnt.get(id(s_ld), 0)),
                                       (s_le, P.cnt.get(id(s_le), 0))])
            P.wd(sy, s_st, P.cnt[id(s_st)])
        emit(None)
        _stage_fence(P, env)


def _stage2_inputs(inputs):
    wqkv = np.asarray(inputs["L0_ffn_wqkv"], np.float32)
    wqkvp = np.ascontiguousarray(
        wqkv.reshape(2, NCH, 128, 12, 128).transpose(0, 3, 2, 1, 4)).reshape(
            2 * 12 * 128, D)
    wo = np.asarray(inputs["L0_ffn_wo"], np.float32).reshape(2 * 512, D)
    wo = np.ascontiguousarray(wo)
    w1 = np.asarray(inputs["L0_ffn_w1"], np.float32)
    w1a = np.zeros((2, D, HID2), np.float32)
    w1g = np.zeros((2, D, HID2), np.float32)
    w1a[:, :, :2730] = w1[:, :, :2730]
    w1g[:, :, :2730] = w1[:, :, 2730:]
    w1s = np.stack([w1a, w1g], axis=1)
    w1t = np.ascontiguousarray(
        w1s.reshape(2, 2, NCH, 128, NF2, 128).transpose(0, 1, 4, 3, 2, 5)).reshape(
            2 * 2 * NF2 * 128, D)
    w2 = np.asarray(inputs["L0_ffn_w2"], np.float32)
    w2p = np.zeros((2, HID2, D), np.float32)
    w2p[:, :2730, :] = w2
    w2t = np.ascontiguousarray(
        w2p.reshape(2, NF2, 128, NCH, 128).transpose(0, 3, 2, 1, 4)).reshape(
            2 * NCH * 128, NF2 * 128)
    b1 = np.asarray(inputs["L0_ffn_b1"], np.float32)
    b1a = np.zeros((2, HID2), np.float32)
    b1g = np.zeros((2, HID2), np.float32)
    b1a[:, :2730] = b1[:, :2730]
    b1g[:, :2730] = b1[:, 2730:]
    b1p = np.ascontiguousarray(
        np.stack([b1a, b1g], axis=1).reshape(2, 2, NF2, 128).transpose(
            0, 1, 3, 2)).reshape(2 * 2 * 128, NF2)
    qs = np.asarray(inputs["L0_ffn_q_scale"], np.float32)
    ks = np.asarray(inputs["L0_ffn_k_scale"], np.float32)
    qsks = np.ascontiguousarray(
        np.stack([qs, ks], axis=1).reshape(2 * 2 * 64, 1))
    vec3 = np.stack([np.asarray(inputs["L0_ffn_ln_g"], np.float32),
                     np.asarray(inputs["L0_ffn_ln_b"], np.float32),
                     np.asarray(inputs["L0_ffn_b2"], np.float32)],
                    axis=1)
    vec3 = np.ascontiguousarray(
        vec3.reshape(2, 3, NCH, 128).transpose(0, 1, 3, 2)).reshape(
            2 * 3 * 128, NCH)
    return {"s2_wqkvp": wqkvp, "s2_wol": wo, "s2_w1t": w1t, "s2_w2t": w2t,
            "s2_vec3": vec3, "s2_qsks": qsks, "s2_b1p": b1p}



ECAP = 4096
NE = 8
NA = 2 * NCORES * T
NTB = 36864
TBW = 18432
GROW = 16384.0


def emit_stage4(nc, P, env):
    x = env["x3"]
    wr = env["s4_wr"]
    wgt = env["s4_wgt"]
    wut = env["s4_wut"]
    wdt = env["s4_wdt"]
    fwgt = env["s4_fwgt"]
    fwut = env["s4_fwut"]
    fwdt = env["s4_fwdt"]
    vin = env["s4_vecs"]
    tokc = env["s4_tokc"]
    coff = env["s4_coff"]
    y = env["y"]
    hsb = env["s4_hsb"]
    lgd = env["s4_lgd"]
    hs_all = env["s4_hs_all"]
    lg_all = env["s4_lg_all"]
    esd = env["s4_esd"]
    paf = env["s4_paf"]
    pafR = env["s4_pafR"]
    esdR = env["s4_esdR"]
    w12d = env["s4_w12d"]
    itd = env["s4_itd"]
    it_all = env["s4_it_all"]
    fod = env["s4_fod"]
    fo_all = env["s4_fo_all"]

    idb = env["idb"]
    idf = env["idf"]
    iotaec = env["s4_iotaec"]
    iota8c = env["s4_iota8c"]
    rg = [list(range(8))]
    ue = "p (u e) -> p u e"
    uo = "p (u o) -> p u o"

    ctx = ExitStack()
    with ctx:
        IDB = _sb(ctx, nc, "IDB", [128, 128], BF16)
        IDF = _sb(ctx, nc, "IDF", [128, 128], F32)
        WRS = _sb(ctx, nc, "WRS", [128, NCH * NE], F32)
        NRM = _sb(ctx, nc, "NRM", [128, NCH], F32)
        FNM = _sb(ctx, nc, "FNM", [128, NCH], F32)
        STM = _sb(ctx, nc, "STM", [128, NT], F32)
        STR = _sb(ctx, nc, "STR", [128, NT], F32)
        st8 = _sb(ctx, nc, "st8", [128, 12], F32)
        xb0 = _sb(ctx, nc, "xb0", [128, D], F32)
        xb1 = _sb(ctx, nc, "xb1", [128, D], F32)
        sq = _sb(ctx, nc, "sq", [128, D], F32)
        xhbt = _sb(ctx, nc, "xhbt", [128, D], BF16)
        hsf = _sb(ctx, nc, "hsf", [128, D], F32)
        yt0 = _sb(ctx, nc, "yt0", [128, D], F32)
        W12 = _sb(ctx, nc, "W12", [128, 512], F32)
        POSL = _sb(ctx, nc, "POSL", [128, 64], F32)
        ESL = _sb(ctx, nc, "ESL", [128, 64], F32)
        POSTL = _sb(ctx, nc, "POSTL", [128, 64], F32)
        ESTL = _sb(ctx, nc, "ESTL", [128, 64], F32)
        KWT = _sb(ctx, nc, "KWT", [128, 64], F32)
        KPT = _sb(ctx, nc, "KPT", [128, 64], F32)
        TOKS = _sb(ctx, nc, "TOKS", [128, 64], F32)
        OVT = _sb(ctx, nc, "OVT", [128, 32], F32)
        ITS = _sb(ctx, nc, "ITS", [128, 64], F32)
        ITSC = _sb(ctx, nc, "ITSC", [128, 32], F32)
        WCG = _sb(ctx, nc, "WCG", [128, 512], I16)
        WEG = _sb(ctx, nc, "WEG", [128, 256], I16)
        W16F = _sb(ctx, nc, "W16F", [16, 512], F32)
        s_w = env["s_w"]
        s_ld = env["s_ld"]
        s_le = env["s_le"]
        s_st = env["s_st"]
        s_cc = env["s_cc"]
        s_sc = env["s_sc"]
        s_g2 = env["s_g2"]
        s_wf = env["s_wf"]

        xbufs = [xb0, xb1]
        lds = [s_ld, s_le]

        def emit(_):
            v, sc, t, sy, g = P.v, P.s, P.t, P.sy, P.g

            def ln_stats(src, sqbuf):
                if "v" in P.last:
                    P.w(sc, P.last["v"])
                sq_ev = P.so(lambda: sc.activation(out=sqbuf[:], in_=src[:],
                                                   func=AF.Square))
                P.w(v, sq_ev)
                P.vo(lambda: v.reduce_sum(st8[:, 0:1], src[:], axis=AX.X))
                P.vo(lambda: v.reduce_sum(st8[:, 1:2], sqbuf[:], axis=AX.X))
                P.vo(lambda: v.tensor_scalar_mul(st8[:, 2:3], st8[:, 0:1], -1.0 / D))
                P.vo(lambda: v.tensor_scalar_mul(st8[:, 3:4], st8[:, 1:2], 1.0 / D))
                P.vo(lambda: v.tensor_mul(st8[:, 4:5], st8[:, 2:3], st8[:, 2:3]))
                vst = P.vo(lambda: v.tensor_sub(st8[:, 3:4], st8[:, 3:4], st8[:, 4:5]))
                P.w(sc, vst)
                sd = P.so(lambda: sc.activation(out=st8[:, 5:6], in_=st8[:, 3:4],
                                                func=AF.Sqrt, bias=st8[:, 7:8]))
                P.w(v, sd)
                return P.vo(lambda: v.reciprocal(st8[:, 6:7], st8[:, 5:6]))

            def rms_stats(src):
                if "v" in P.last:
                    P.w(sc, P.last["v"])
                sqv = P.so(lambda: sc.activation(out=sq[:], in_=src[:],
                                                 func=AF.Square))
                P.w(v, sqv)
                rs = P.vo(lambda: v.reduce_sum(st8[:, 0:1], sq[:], axis=AX.X))
                P.w(sc, rs)
                sd = P.so(lambda: sc.activation(out=st8[:, 5:6], in_=st8[:, 0:1],
                                                func=AF.Sqrt, bias=st8[:, 11:12],
                                                scale=1.0 / D))
                P.w(v, sd)
                return P.vo(lambda: v.reciprocal(st8[:, 6:7], st8[:, 5:6]))

            def wbuild(srcap, C, pbank, wi):
                P.w(t, P.last["v"])
                ti = None
                for pp in range(8):
                    ti = t.matmul(pbank[0:16, pp * C:(pp + 1) * C],
                                  IDF[:, pp * 16:(pp + 1) * 16], srcap,
                                  start=True, stop=True)
                me = P.tlast(ti)
                P.w(v, me)
                for pp in range(8):
                    P.vo(lambda pp=pp: v.tensor_copy(
                        out=W16F[0:16, 0:8 * C].rearrange(
                            "q (c pp) -> q c pp", pp=8)[:, :, pp:pp + 1],
                        in_=pbank[0:16, pp * C:(pp + 1) * C].rearrange(
                            "q (c o) -> q c o", o=1)))
                cast = P.vo(lambda: v.tensor_copy(out=wi[0:16, 0:8 * C],
                                                  in_=W16F[0:16, 0:8 * C]))
                P.w(g, cast)
                ev = P.dma(g, wi[16:32, 0:8 * C], wi[0:16, 0:8 * C], s_sc)
                P.wd(g, s_sc, ev[1])
                ev = P.dma(g, wi[32:64, 0:8 * C], wi[0:32, 0:8 * C], s_sc)
                P.wd(g, s_sc, ev[1])
                ev = P.dma(g, wi[64:128, 0:8 * C], wi[0:64, 0:8 * C], s_sc)
                return ev

            for dst, src in ((IDB, idb), (IDF, idf), (TOKS, tokc)):
                P.dma(g, dst[:], src.ap(), s_w)
            for c in range(NCH):
                P.dma(g, WRS[:, c * NE:(c + 1) * NE], wr[c * 128:(c + 1) * 128, :], s_w)
                P.dma(g, NRM[:, c:c + 1], vin[c * 128:(c + 1) * 128, :], s_w)
                P.dma(g, FNM[:, c:c + 1], vin[D + c * 128: D + (c + 1) * 128, :], s_w)
            W_ALL = P.cnt[id(s_w)]
            P.wd(v, s_w, W_ALL)
            P.vo(lambda: v.memset(st8[:, 7:8], EPS_LN))
            P.vo(lambda: v.memset(st8[:, 11:12], 1e-6))

            with ExitStack() as actx:
                tpF = _ps(actx, nc, "tpF", [128, 1024], F32)
                tpG = _ps(actx, nc, "tpG", [128, 1024], F32)
                pL0 = _ps(actx, nc, "pL0", [128, NE], F32)
                pL1 = _ps(actx, nc, "pL1", [128, NE], F32)
                hst = actx.enter_context(nc.sbuf_tensor("hst", [128, D], F32))
                lgs = actx.enter_context(nc.sbuf_tensor("lgs", [128, 2 * NE], F32))
                tp_cons = [None, None]
                lg_ev = [None] * NT
                hs_ev = [None] * NT
                lme_prev = None
                for i in range(NT):
                    xb = xbufs[i % 2]
                    if i >= 2:
                        P.w(sy, hs_ev[i - 2])
                    ld_ev = P.dma(sy, xb[:], x[i * 128:(i + 1) * 128, :], lds[i % 2])
                    P.w(sc, ld_ev)
                    P.w(v, ld_ev)
                    ln_stats(xb, sq)
                    P.vo(lambda i=i: v.tensor_copy(out=STM[:, i:i + 1], in_=st8[:, 2:3]))
                    P.vo(lambda i=i: v.tensor_copy(out=STR[:, i:i + 1], in_=st8[:, 6:7]))
                    hs_ev[i] = P.vo(lambda i=i: v.tensor_scalar(
                        out=hsf[:], in0=xb[:], scalar1=st8[:, 2:3], scalar2=st8[:, 6:7],
                        op0=ALU.add, op1=ALU.mult))
                    P.wd(v, s_st, P.cnt.get(id(s_st), 0))
                    bf_ev = P.vo(lambda: v.tensor_copy(out=xhbt[:], in_=hsf[:]))
                    P.w(sy, bf_ev)
                    P.dma(sy, hsb[i * 128:(i + 1) * 128, :], xhbt[:], s_st)
                    tp = tpF if i % 2 == 0 else tpG
                    P.w(t, hs_ev[i])
                    if i == 0:
                        P.wd(t, s_w, W_ALL)
                    if tp_cons[i % 2] is not None:
                        P.w(t, tp_cons[i % 2])
                    ti = None
                    for c in range(NCH):
                        ti = t.transpose(tp[:, c * 128:(c + 1) * 128],
                                         hsf[:, c * 128:(c + 1) * 128], IDF[:])
                    tpe = P.tlast(ti)
                    P.w(v, tpe)
                    if lme_prev is not None:
                        P.w(v, lme_prev)
                    hev = None
                    for c in range(NCH):
                        hev = P.vo(lambda c=c: v.tensor_copy(
                            out=hst[:, c * 128:(c + 1) * 128],
                            in_=tp[:, c * 128:(c + 1) * 128]))
                    tp_cons[i % 2] = hev
                    pL = pL0 if i % 2 == 0 else pL1
                    P.w(t, hev)
                    if i >= 2:
                        P.w(t, lg_ev[i - 2])
                    ti = None
                    for c in range(NCH):
                        ti = t.matmul(pL[:], hst[:, c * 128:(c + 1) * 128],
                                      WRS[:, c * NE:(c + 1) * NE],
                                      start=(c == 0), stop=(c == NCH - 1))
                    lme = P.tlast(ti)
                    lme_prev = lme
                    P.w(v, lme)
                    if i >= 2:
                        P.w(v, lg_ev[i - 2])
                    lgc = P.vo(lambda i=i, pL=pL: v.tensor_copy(
                        out=lgs[:, (i % 2) * NE:(i % 2 + 1) * NE], in_=pL[:]))
                    P.w(sy, lgc)
                    lg_ev[i] = P.dma(sy, lgd[i * 128:(i + 1) * 128, :],
                                     lgs[:, (i % 2) * NE:(i % 2 + 1) * NE], s_st)
            P.fence(extra=[(s_ld, P.cnt.get(id(s_ld), 0)), (s_le, P.cnt.get(id(s_le), 0)),
                           (s_st, P.cnt.get(id(s_st), 0))])

            cc1 = P.go(lambda: g.collective_compute(
                "AllGather", ALU.bypass, rg, ins=[hsb.ap()], outs=[hs_all.ap()]))
            cc2 = P.go(lambda: g.collective_compute(
                "AllGather", ALU.bypass, rg, ins=[lgd.ap()], outs=[lg_all.ap()]))

            with ExitStack() as rctx:
                LG = rctx.enter_context(nc.sbuf_tensor("LG", [128, 2048], F32))
                IOTAE = rctx.enter_context(nc.sbuf_tensor("IOTAE", [128, 2048], F32))
                IOTA8 = rctx.enter_context(nc.sbuf_tensor("IOTA8", [8, 1], F32))
                SM = rctx.enter_context(nc.sbuf_tensor("SM", [128, 256], F32))
                M1 = rctx.enter_context(nc.sbuf_tensor("M1", [128, 256], F32))
                EQ = rctx.enter_context(nc.sbuf_tensor("EQ", [128, 2048], F32))
                CU = rctx.enter_context(nc.sbuf_tensor("CU", [128, 2048], F32))
                E12 = rctx.enter_context(nc.sbuf_tensor("E12", [128, 512], F32))
                POSA = rctx.enter_context(nc.sbuf_tensor("POSA", [128, 512], F32))
                ESA = rctx.enter_context(nc.sbuf_tensor("ESA", [128, 512], F32))
                POSAT = rctx.enter_context(nc.sbuf_tensor("POSAT", [128, 512], F32))
                ZT = rctx.enter_context(nc.sbuf_tensor("ZT", [128, 256], F32))
                pE0 = _ps(rctx, nc, "pE0", [128, 512], F32)
                pE1 = _ps(rctx, nc, "pE1", [128, 512], F32)
                pF0 = _ps(rctx, nc, "pF0", [128, 512], F32)
                pF1 = _ps(rctx, nc, "pF1", [128, 512], F32)
                pW16 = _ps(rctx, nc, "pW16", [16, 512], F32)
                sctx = ExitStack()
                EB = sctx.enter_context(nc.sbuf_tensor("EB", [8, 8192], F32))
                EB2 = sctx.enter_context(nc.sbuf_tensor("EB2", [8, 8192], F32))
                Z8 = sctx.enter_context(nc.sbuf_tensor("Z8", [8, 8192], F32))
                PS = sctx.enter_context(nc.sbuf_tensor("PS", [8, 8192], F32))

                P.dma(g, IOTAE[:], iotaec.ap(), s_w)
                P.dma(g, IOTA8[:], iota8c.ap(), s_w)
                IOTA_ALL = P.cnt[id(s_w)]
                zt_ev = P.vo(lambda: v.memset(ZT[:], 0.0))
                z8_ev = P.vo(lambda: v.memset(Z8[:], 0.0))

                P.w(sy, cc2)
                ldL = P.dma(sy, LG[:].rearrange(ue, e=8),
                            lg_all.ap().rearrange("(u p) e -> p u e", p=128), s_cc)
                P.wd(sc, s_cc, ldL[1])
                ex_ev = P.so(lambda: sc.activation(out=LG[:], in_=LG[:], func=AF.Exp))
                P.w(v, ex_ev)
                P.wd(v, s_w, IOTA_ALL)
                P.vo(lambda: v.reduce_sum(SM[:].rearrange(uo, o=1),
                                          LG[:].rearrange(ue, e=8), axis=AX.X))
                P.vo(lambda: v.reciprocal(SM[:], SM[:]))
                P.vo(lambda: v.tensor_tensor(
                    LG[:].rearrange(ue, e=8), LG[:].rearrange(ue, e=8),
                    SM[:].rearrange(uo, o=1).to_broadcast([128, 256, 8]),
                    op=ALU.mult))

                def top1(dst_e, dst_w):
                    P.vo(lambda: v.reduce_max(M1[:].rearrange(uo, o=1),
                                              LG[:].rearrange(ue, e=8), axis=AX.X))
                    P.vo(lambda: v.tensor_tensor(
                        EQ[:].rearrange(ue, e=8), LG[:].rearrange(ue, e=8),
                        M1[:].rearrange(uo, o=1).to_broadcast([128, 256, 8]),
                        op=ALU.is_equal))
                    P.vo(lambda: v.tensor_copy(out=CU[:], in_=EQ[:]))
                    for e in range(1, 8):
                        P.vo(lambda e=e: v.tensor_add(
                            CU[:].rearrange(ue, e=8)[:, :, e:e + 1],
                            CU[:].rearrange(ue, e=8)[:, :, e - 1:e],
                            CU[:].rearrange(ue, e=8)[:, :, e:e + 1]))
                    P.vo(lambda: v.tensor_single_scalar(CU[:], CU[:], 1.0,
                                                        op=ALU.is_equal))
                    P.vo(lambda: v.tensor_mul(EQ[:], EQ[:], CU[:]))
                    P.vo(lambda: v.tensor_mul(CU[:], EQ[:], IOTAE[:]))
                    P.vo(lambda: v.reduce_sum(dst_e.rearrange(uo, o=1),
                                              CU[:].rearrange(ue, e=8), axis=AX.X))
                    return P.vo(lambda: v.tensor_copy(out=dst_w, in_=M1[:]))

                top1(E12[:, 0:256], W12[:, 0:256])
                P.vo(lambda: v.scalar_tensor_tensor(
                    out=LG[:], in0=EQ[:], scalar=-1e30, in1=LG[:],
                    op0=ALU.mult, op1=ALU.add))
                w2_ev = top1(E12[:, 256:512], W12[:, 256:512])

                for j in range(2):
                    P.vo(lambda j=j: v.tensor_copy(
                        out=CU[:, 0:512].rearrange("p (u j) -> p u j", j=2)[:, :, j:j + 1],
                        in_=E12[:, j * 256:(j + 1) * 256].rearrange(uo, o=1)))
                ei_ev = P.last["v"]
                P.w(sy, ei_ev)
                P.dma(sy, esd.ap().rearrange("(u p j) o -> p u (j o)", p=128, j=2),
                      CU[:, 0:512].rearrange("p (u j) -> p u j", j=2), s_cc)
                P.w(sy, w2_ev)
                P.dma(sy, w12d.ap(), W12[:], s_cc)
                ES_W = P.cnt[id(s_cc)]

                prd_ev = None
                prd2_ev = None
                ext_ev = [None] * 8
                ext2_ev = [None] * 8
                red_ev = [None] * 8
                red2_ev = [None] * 8
                for r in range(8):
                    a0 = r * 8192
                    P.wd(sy, s_cc, P.cnt[id(s_cc)])
                    if prd_ev is not None:
                        P.w(sy, prd_ev)
                    if r >= 1:
                        P.w(sy, ext_ev[r - 1])
                    for p8 in range(8):
                        P.dma(sy, EB[p8:p8 + 1, :],
                              esd[a0:a0 + 8192, :].rearrange("a o -> o a"), s_cc)
                    P.wd(v, s_cc, P.cnt[id(s_cc)])
                    P.vo(lambda: v.tensor_single_scalar(EB[:], EB[:], IOTA8[:],
                                                        op=ALU.is_equal))
                    if r >= 1:
                        P.w(v, ext2_ev[r - 1])
                    prd2_ev = P.vo(lambda: v.tensor_single_scalar(
                        EB2[:], EB[:], IOTA8[:], op=ALU.mult))
                    ini = 0.0 if r == 0 else PS[:, 8191:8192]
                    P.vo(lambda ini=ini: v.tensor_tensor_scan(
                        out=PS[:], data0=Z8[:], data1=EB[:], initial=ini,
                        op0=ALU.add, op1=ALU.add))
                    prd_ev = P.vo(lambda: v.tensor_mul(EB[:], EB[:], PS[:]))
                    pE = pE0 if r % 2 == 0 else pE1
                    pF = pF0 if r % 2 == 0 else pF1
                    P.w(t, prd_ev)
                    if r >= 2:
                        P.w(t, red_ev[r - 2])
                        P.w(t, red2_ev[r - 2])
                    ti = None
                    for at in range(64):
                        ti = t.matmul(pE[:, at * 8:(at + 1) * 8],
                                      EB[:, at * 128:(at + 1) * 128],
                                      IDF[0:8, 0:8], start=True, stop=True)
                    ext_ev[r] = P.tlast(ti)
                    ti = None
                    for at in range(64):
                        ti = t.matmul(pF[:, at * 8:(at + 1) * 8],
                                      EB2[:, at * 128:(at + 1) * 128],
                                      IDF[0:8, 0:8], start=True, stop=True)
                    ext2_ev[r] = P.tlast(ti)
                    P.w(v, ext_ev[r])
                    red_ev[r] = P.vo(lambda r=r, pE=pE: v.reduce_sum(
                        POSA[:, r * 64:(r + 1) * 64].rearrange(uo, o=1),
                        pE[:].rearrange("p (a e) -> p a e", e=8), axis=AX.X))
                    P.w(v, ext2_ev[r])
                    red2_ev[r] = P.vo(lambda r=r, pF=pF: v.reduce_sum(
                        ESA[:, r * 64:(r + 1) * 64].rearrange(uo, o=1),
                        pF[:].rearrange("p (a e) -> p a e", e=8), axis=AX.X))

                P.w(sy, red_ev[7])
                P.dma(sy, pafR.ap(), POSA[:], s_cc)
                P.w(sy, red2_ev[7])
                P.dma(sy, esdR.ap(), ESA[:], s_cc)
                P.w(t, red_ev[7])
                P.w(t, red_ev[6])
                ti = None
                for q in range(4):
                    ti = t.transpose(pE0[:, q * 128:(q + 1) * 128],
                                     POSA[:, q * 128:(q + 1) * 128], IDF[:])
                tpa = P.tlast(ti)
                P.w(v, tpa)
                pat_ev = P.vo(lambda: v.tensor_copy(out=POSAT[:], in_=pE0[:]))
                P.w(sy, pat_ev)
                P.dma(sy, paf.ap().rearrange("(q k p) o -> k q (p o)", q=4, k=128),
                      POSAT[:].rearrange("k (q p) -> k q p", p=128), s_cc)
                PAF_W = P.cnt[id(s_cc)]
                P.fence(extra=[(s_cc, PAF_W)])
                sctx.close()

                P.wd(g, s_cc, PAF_W)
                with g.register("rA") as regA, g.register("rB") as regB, \
                        g.register("rC") as regC:
                    g.reg_load(regA, coff.ap()[0:1, 0:1])
                    g.reg_load(regB, coff.ap()[0:1, 2:3])
                    g.reg_load(regC, coff.ap()[0:1, 3:4])
                    offA = g.snap(regA)
                    offB = g.snap(regB)
                    offC = g.snap(regC)
                    tm_ = "(i p j) o -> p i (j o)"
                    P.dma(g, POSL[:], pafR.ap()[:, bass.ds(offA, 64)], s_sc)
                    P.dma(g, ESL[:], esdR.ap()[:, bass.ds(offA, 64)], s_sc)
                    P.dma(g, POSTL[:].rearrange("p (i j) -> p i j", j=2),
                          paf.ap().rearrange(tm_, p=128, j=2)[:, bass.ds(offB, 32), :],
                          s_sc)
                    P.dma(g, ESTL[:].rearrange("p (i j) -> p i j", j=2),
                          esd.ap().rearrange(tm_, p=128, j=2)[:, bass.ds(offB, 32), :],
                          s_sc)
                    P.dma(g, KWT[:, 0:32], w12d.ap()[:, bass.ds(offB, 32)], s_sc)
                    P.dma(g, KWT[:, 32:64], w12d.ap()[:, bass.ds(offC, 32)], s_sc)
                SEL_ALL = P.cnt[id(s_sc)]

                with ExitStack() as pctx:
                    ZR = pctx.enter_context(nc.sbuf_tensor("ZRp", [128, 2048], F32))
                    WSA = pctx.enter_context(nc.sbuf_tensor("WSA", [128, 512], I16))
                    WSB = pctx.enter_context(nc.sbuf_tensor("WSB", [128, 512], I16))
                    SELA = pctx.enter_context(nc.sbuf_tensor("SELA", [128, 64], F32))
                    KPA = pctx.enter_context(nc.sbuf_tensor("KPA", [128, 64], F32))
                    SCIF = pctx.enter_context(nc.sbuf_tensor("SCIF", [128, 64], F32))
                    TMP = pctx.enter_context(nc.sbuf_tensor("TMPp", [128, 64], F32))
                    PAYV = pctx.enter_context(nc.sbuf_tensor("PAYV", [128, 64], F32))
                    PAYW = pctx.enter_context(
                        nc.sbuf_tensor("PAYW", [128, 64, 64], F32))

                    P.wd(v, s_sc, SEL_ALL)
                    P.vo(lambda: v.tensor_scalar_add(POSL[:], POSL[:], -1.0))
                    P.vo(lambda: v.tensor_single_scalar(KPA[:], POSL[:], float(ECAP),
                                                        op=ALU.is_lt))
                    P.vo(lambda: v.scalar_tensor_tensor(
                        out=SCIF[:], in0=ESL[:], scalar=float(ECAP), in1=POSL[:],
                        op0=ALU.mult, op1=ALU.add))
                    P.vo(lambda: v.tensor_single_scalar(SELA[:], ESL[:], 3.5,
                                                        op=ALU.is_lt))
                    P.vo(lambda: v.tensor_mul(SELA[:], SELA[:], KPA[:]))
                    P.vo(lambda: v.tensor_scalar_add(TMP[:], SCIF[:], -GROW))
                    P.vo(lambda: v.tensor_mul(TMP[:], TMP[:], SELA[:]))
                    P.vo(lambda: v.tensor_scalar_add(TMP[:], TMP[:], GROW))
                    wbuild(TMP[:], 64, pW16, WSA)
                    P.vo(lambda: v.tensor_sub(SELA[:], KPA[:], SELA[:]))
                    P.vo(lambda: v.tensor_scalar_add(TMP[:], SCIF[:],
                                                     -GROW - 4.0 * ECAP))
                    P.vo(lambda: v.tensor_mul(TMP[:], TMP[:], SELA[:]))
                    P.vo(lambda: v.tensor_scalar_add(TMP[:], TMP[:], GROW))
                    wbuild(TMP[:], 64, pW16, WSB)
                    P.vo(lambda: v.tensor_scalar_add(PAYV[:], TOKS[:], 1.0))
                    P.vo(lambda: v.memset(PAYW[:], 0.0))
                    payw_ev = P.vo(lambda: v.tensor_copy(
                        out=PAYW[:, :, 0:1],
                        in_=PAYV[:].rearrange("p (c o) -> p c o", o=1)))
                    zr_ev = P.vo(lambda: v.memset(ZR[:], 0.0))

                    P.vo(lambda: v.tensor_scalar_add(POSTL[:], POSTL[:], -1.0))
                    P.vo(lambda: v.tensor_single_scalar(KPT[:], POSTL[:],
                                                        float(ECAP), op=ALU.is_lt))
                    P.vo(lambda: v.tensor_single_scalar(POSTL[:], POSTL[:],
                                                        float(ECAP - 1), op=ALU.min))
                    P.vo(lambda: v.scalar_tensor_tensor(
                        out=POSTL[:], in0=ESTL[:], scalar=float(ECAP), in1=POSTL[:],
                        op0=ALU.mult, op1=ALU.add))
                    wcg_ev = wbuild(POSTL[:], 64, pW16, WCG)
                    k3 = KPT[:].rearrange("p (i j) -> p i j", j=2)
                    for j in range(2):
                        P.vo(lambda j=j: v.tensor_tensor(
                            KWT[:, j * 32:(j + 1) * 32].rearrange(uo, o=1),
                            KWT[:, j * 32:(j + 1) * 32].rearrange(uo, o=1),
                            k3[:, :, j:j + 1], op=ALU.mult))
                    kw_ev = P.last["v"]
                    P.vo(lambda: v.tensor_tensor(
                        OVT[:].rearrange(uo, o=1), k3[:, :, 0:1], k3[:, :, 1:2],
                        op=ALU.mult))
                    ov_ev = P.vo(lambda: v.tensor_single_scalar(OVT[:], OVT[:], 0.5,
                                                                op=ALU.is_lt))

                    P.w(g, zr_ev)
                    for q in range(9):
                        P.dma(g, itd.ap().rearrange("(q p c) o -> q p (c o)",
                                                    q=9, p=128)[q],
                              ZR[:], s_sc)
                    P.w(g, payw_ev)
                    P.wd(g, s_sc, P.cnt[id(s_sc)])
                    for w0, wi_ in ((0, WSA), (TBW, WSB)):
                        for sp in range(2):
                            ev = P._mark(None, g.dma_scatter_add(
                                itd.ap()[w0:w0 + TBW, :],
                                PAYW[:, sp * 32:(sp + 1) * 32, :],
                                wi_[:, sp * 256:(sp + 1) * 256],
                                4096, 4096, 64, elem_step=64), s_g2, 16)
                            P.wd(g, s_g2, ev[1])
                    cc3 = P.go(lambda: g.collective_compute(
                        "AllReduce", ALU.add, rg, ins=[itd.ap()],
                        outs=[it_all.ap()]))
            P.fence(extra=[(s_cc, P.cnt.get(id(s_cc), 0)),
                           (s_sc, P.cnt.get(id(s_sc), 0)),
                           (s_g2, P.cnt.get(id(s_g2), 0)),
                           (s_w, P.cnt.get(id(s_w), 0))])

            with ExitStack() as ectx:
                WGB = ectx.enter_context(nc.sbuf_tensor("WGB", [128, 16 * 1024], BF16))
                WUB = ectx.enter_context(nc.sbuf_tensor("WUB", [128, 16 * 1024], BF16))
                WDB = ectx.enter_context(nc.sbuf_tensor("WDB", [128, NCH * 2048], BF16))
                XE = ectx.enter_context(nc.sbuf_tensor("XE", [128, NCH * 1024], BF16))
                GUE = ectx.enter_context(nc.sbuf_tensor("GUE", [128, 16 * 1024], BF16))
                EG = ectx.enter_context(nc.sbuf_tensor("EG", [128, 8, 1024], BF16))
                XH2 = ectx.enter_context(nc.sbuf_tensor("XH2", [128, 2 * 1024], BF16))
                AH = ectx.enter_context(nc.sbuf_tensor("AHe", [128, 512], BF16))
                GH = ectx.enter_context(nc.sbuf_tensor("GHe", [128, 512], BF16))
                O2F = ectx.enter_context(nc.sbuf_tensor("O2Fe", [128, NCH * 512],
                                                         BF16))
                pA = _ps(ectx, nc, "pAe", [128, 512], F32)
                pG = _ps(ectx, nc, "pGe", [128, 512], F32)
                pX0 = _ps(ectx, nc, "pX0e", [128, 1024], F32)
                pX1 = _ps(ectx, nc, "pX1e", [128, 1024], F32)
                pW0 = _ps(ectx, nc, "pW0e", [128, 512], F32)
                pW1 = _ps(ectx, nc, "pW1e", [128, 512], F32)

                for ftl in range(16):
                    P.dma(g, WGB[:, ftl * 1024:(ftl + 1) * 1024],
                          wgt[ftl * 128:(ftl + 1) * 128, :], s_wf)
                    P.dma(g, WUB[:, ftl * 1024:(ftl + 1) * 1024],
                          wut[ftl * 128:(ftl + 1) * 128, :], s_wf)
                for fo in range(NCH):
                    P.dma(g, WDB[:, fo * 2048:(fo + 1) * 2048],
                          wdt[fo * 128:(fo + 1) * 128, :], s_wf)
                WEXP = P.cnt[id(s_wf)]
                P.wd(v, s_wf, WEXP)
                nw_ev = None
                for ftl in range(16):
                    for c in range(NCH):
                        P.vo(lambda ftl=ftl, c=c: v.tensor_scalar_mul(
                            WGB[:, ftl * 1024 + c * 128: ftl * 1024 + (c + 1) * 128],
                            WGB[:, ftl * 1024 + c * 128: ftl * 1024 + (c + 1) * 128],
                            NRM[:, c:c + 1]))
                        nw_ev = P.vo(lambda ftl=ftl, c=c: v.tensor_scalar_mul(
                            WUB[:, ftl * 1024 + c * 128: ftl * 1024 + (c + 1) * 128],
                            WUB[:, ftl * 1024 + c * 128: ftl * 1024 + (c + 1) * 128],
                            NRM[:, c:c + 1]))

                P.w(g, cc3)
                with g.register("rE") as regE:
                    g.reg_load(regE, coff.ap()[0:1, 1:2])
                    offE = g.snap(regE)
                    its_ev = P.dma(g, ITS[:].rearrange("p (c j) -> p c j", j=2),
                                   it_all.ap()[:, 0:2].rearrange(
                                       "(c p) o -> p c o", p=128)
                                   [:, bass.ds(offE, 32), :], s_sc)
                P.wd(v, s_sc, its_ev[1])
                P.vo(lambda: v.tensor_copy(
                    out=ITSC[:].rearrange("p (c o) -> p c o", o=1),
                    in_=ITS[:].rearrange("p (c j) -> p c j", j=2)[:, :, 0:1]))
                P.vo(lambda: v.tensor_scalar_add(ITSC[:], ITSC[:], -1.0))
                P.vo(lambda: v.tensor_single_scalar(ITSC[:], ITSC[:], 0.0,
                                                    op=ALU.max))
                weg_ev = wbuild(ITSC[:], 32, pA, WEG)

                wb_last = None
                eg_cons = None
                for sr in range(4):
                    if sr == 0:
                        P.wd(g, s_sc, weg_ev[1])
                        P.w(g, cc1)
                    else:
                        P.w(g, eg_cons)
                    gat = P._mark(None, g.dma_gather(
                        EG[:], hs_all.ap(), WEG[:, sr * 64:(sr + 1) * 64],
                        1024, 1024, 1024), s_g2, 16)
                    GT = gat[1]

                    def egs(i):
                        return EG[:, i:i + 1, :].rearrange("p o e -> p (o e)")

                    xe_ev = [None] * NCH
                    txe_ev = [None, None]
                    for i in range(NCH):
                        P.wd(sc, s_g2, GT)
                        P.wd(v, s_g2, GT)
                        rms_stats(egs(i))
                        xh = XH2[:, (i % 2) * 1024:(i % 2 + 1) * 1024]
                        if txe_ev[i % 2] is not None:
                            P.w(v, txe_ev[i % 2])
                        xh_ev = P.vo(lambda i=i, xh=xh: v.tensor_scalar_mul(
                            xh, egs(i), st8[:, 6:7]))
                        pX = pX0 if i % 2 == 0 else pX1
                        P.w(t, xh_ev)
                        if i >= 2:
                            P.w(t, xe_ev[i - 2])
                        elif sr > 0:
                            P.w(t, wb_last)
                        elif sr == 0 and i == 0:
                            P.w(t, P.last["v"])
                        ti = None
                        for c in range(NCH):
                            ti = t.matmul(pX[:, c * 128:(c + 1) * 128],
                                          xh[:, c * 128:(c + 1) * 128], IDB[:],
                                          start=True, stop=True)
                        txe_ev[i % 2] = P.tlast(ti)
                        P.w(v, txe_ev[i % 2])
                        ev = None
                        for c in range(NCH):
                            ev = P.vo(lambda c=c, i=i, pX=pX: v.tensor_copy(
                                out=XE[:, c * 1024 + i * 128: c * 1024 + (i + 1) * 128],
                                in_=pX[:, c * 128:(c + 1) * 128]))
                        xe_ev[i] = ev
                    guev = None
                    ev = None
                    for ftl in range(16):
                        for sb in range(2):
                            for hf in range(2):
                                pDst = pA if hf == 0 else pG
                                BUF = GH if hf == 0 else AH
                                WB = WGB if hf == 0 else WUB
                                if ftl == 0 and sb == 0 and hf == 0:
                                    P.w(t, xe_ev[NCH - 1])
                                    P.w(t, xe_ev[NCH - 2])
                                    P.w(t, nw_ev)
                                    if sr > 0:
                                        P.w(t, wb_last)
                                if guev is not None:
                                    P.w(t, guev)
                                ti = None
                                for c in range(NCH):
                                    ti = t.matmul(
                                        pDst[:],
                                        WB[:, ftl * 1024 + c * 128:
                                           ftl * 1024 + (c + 1) * 128],
                                        XE[:, c * 1024 + sb * 512:
                                           c * 1024 + (sb + 1) * 512],
                                        start=(c == 0), stop=(c == NCH - 1))
                                mmev = P.tlast(ti)
                                P.w(sc, mmev)
                                if guev is not None:
                                    P.w(sc, guev)
                                fn = AF.Gelu if hf == 0 else AF.Identity
                                ev = P.so(lambda fn=fn, BUF=BUF, pDst=pDst:
                                          sc.activation(out=BUF[:], in_=pDst[:],
                                                        func=fn))
                            P.w(v, ev)
                            guev = P.vo(lambda ftl=ftl, sb=sb: v.tensor_mul(
                                GUE[:, ftl * 1024 + sb * 512:
                                    ftl * 1024 + (sb + 1) * 512],
                                GH[:], AH[:]))
                    pWs = [pW0, pW1]
                    for sb in range(2):
                        o2_ev = [None] * NCH
                        for fo in range(NCH):
                            k = sb * NCH + fo
                            pU = pWs[k % 2]
                            if k == 0:
                                P.w(t, guev)
                            if k >= 2:
                                P.w(t, o2_ev[fo - 2] if fo >= 2 else wb_last or guev)
                            ti = None
                            for cc in range(16):
                                ti = t.matmul(
                                    pU[:],
                                    WDB[:, fo * 2048 + cc * 128:
                                        fo * 2048 + (cc + 1) * 128],
                                    GUE[:, cc * 1024 + sb * 512:
                                        cc * 1024 + (sb + 1) * 512],
                                    start=(cc == 0), stop=(cc == 15))
                            mm2 = P.tlast(ti)
                            P.w(sc, mm2)
                            if sb == 1 and fo == 0:
                                P.w(sc, wb_last or mm2)
                            o2_ev[fo] = P.so(lambda fo=fo, pU=pU: sc.activation(
                                out=O2F[:, fo * 512:(fo + 1) * 512], in_=pU[:],
                                func=AF.Identity))
                        for i in range(4):
                            gidx = sr * 8 + sb * 4 + i
                            tlo = sb * 4 + i
                            P.w(t, o2_ev[NCH - 1])
                            if wb_last is not None:
                                P.w(t, wb_last)
                            ti = None
                            for do in range(NCH):
                                src = O2F[:, do * 512 + i * 128:
                                          do * 512 + (i + 1) * 128]
                                dst = (pA if do < 4 else pG)[:, (do % 4) * 128:
                                                             (do % 4 + 1) * 128]
                                ti = t.matmul(dst, src, IDB[:], start=True,
                                              stop=True)
                            tyv = P.tlast(ti)
                            P.w(v, tyv)
                            P.vo(lambda: v.tensor_copy(out=yt0[:, 0:512], in_=pA[:]))
                            P.vo(lambda: v.tensor_copy(out=yt0[:, 512:1024],
                                                       in_=pG[:]))
                            P.vo(lambda tlo=tlo: v.tensor_copy(out=sq[:],
                                                               in_=egs(tlo)))
                            eg_c = P.vo(lambda: v.tensor_add(yt0[:], yt0[:], sq[:]))
                            if tlo == NCH - 1:
                                eg_cons = eg_c
                            P.wd(v, s_st, P.cnt.get(id(s_st), 0))
                            wb_last = P.vo(lambda: v.tensor_copy(out=xhbt[:],
                                                                 in_=yt0[:]))
                            P.w(sy, wb_last)
                            P.dma(sy, fod[gidx * 128:(gidx + 1) * 128, :], xhbt[:],
                                  s_st)
            P.fence(extra=[(s_st, P.cnt.get(id(s_st), 0)),
                           (s_g2, P.cnt.get(id(s_g2), 0)),
                           (s_sc, P.cnt.get(id(s_sc), 0)),
                           (s_wf, P.cnt.get(id(s_wf), 0))])

            cc4 = P.go(lambda: g.collective_compute(
                "AllGather", ALU.bypass, rg, ins=[fod.ap()], outs=[fo_all.ap()]))

            with ExitStack() as fctx:
                FH = fctx.enter_context(nc.sbuf_tensor("FH", [128, NCH * T], BF16))
                FBG = fctx.enter_context(nc.sbuf_tensor("FBG", [128, 4 * T], BF16))
                FWG = fctx.enter_context(nc.sbuf_tensor("FWG", [128, 4 * 1024], BF16))
                FWU = fctx.enter_context(nc.sbuf_tensor("FWU", [128, 4 * 1024], BF16))
                FWD = fctx.enter_context(nc.sbuf_tensor("FWD", [128, NCH * 512], BF16))
                AH2 = fctx.enter_context(nc.sbuf_tensor("AH2", [128, 512], BF16))
                GH2 = fctx.enter_context(nc.sbuf_tensor("GH2", [128, 512], BF16))
                RG = fctx.enter_context(nc.sbuf_tensor("RG", [128, 8, 1024], BF16))
                CMBV = fctx.enter_context(nc.sbuf_tensor("CMBV", [128, D], F32))
                FBT = fctx.enter_context(nc.sbuf_tensor("FBT", [128, D], F32))
                XH3 = fctx.enter_context(nc.sbuf_tensor("XH3", [128, 2 * 1024], BF16))
                OUTC = fctx.enter_context(nc.sbuf_tensor("OUTC", [128, NCH * 512],
                                                         F32))
                pA2 = _ps(fctx, nc, "pA2f", [128, 512], F32)
                pG2 = _ps(fctx, nc, "pG2f", [128, 512], F32)
                pX2 = _ps(fctx, nc, "pX2f", [128, 1024], F32)
                pX3 = _ps(fctx, nc, "pX3f", [128, 1024], F32)
                pW2 = _ps(fctx, nc, "pW2f", [128, 512], F32)
                pW3 = _ps(fctx, nc, "pW3f", [128, 512], F32)

                for ftl in range(4):
                    P.dma(g, FWG[:, ftl * 1024:(ftl + 1) * 1024],
                          fwgt[ftl * 128:(ftl + 1) * 128, :], s_wf)
                    P.dma(g, FWU[:, ftl * 1024:(ftl + 1) * 1024],
                          fwut[ftl * 128:(ftl + 1) * 128, :], s_wf)
                for fo in range(NCH):
                    P.dma(g, FWD[:, fo * 512:(fo + 1) * 512],
                          fwdt[fo * 128:(fo + 1) * 128, :], s_wf)
                WFB = P.cnt[id(s_wf)]
                P.wd(v, s_wf, WFB)
                fw_ev = None
                for ftl in range(4):
                    for c in range(NCH):
                        P.vo(lambda ftl=ftl, c=c: v.tensor_scalar_mul(
                            FWG[:, ftl * 1024 + c * 128: ftl * 1024 + (c + 1) * 128],
                            FWG[:, ftl * 1024 + c * 128: ftl * 1024 + (c + 1) * 128],
                            FNM[:, c:c + 1]))
                        fw_ev = P.vo(lambda ftl=ftl, c=c: v.tensor_scalar_mul(
                            FWU[:, ftl * 1024 + c * 128: ftl * 1024 + (c + 1) * 128],
                            FWU[:, ftl * 1024 + c * 128: ftl * 1024 + (c + 1) * 128],
                            FNM[:, c:c + 1]))

                fh_ev = [None] * NT
                hs2_ev = [None] * NT
                txf_ev = [None, None]
                for i in range(NT):
                    xb = xbufs[i % 2]
                    if i >= 2:
                        P.w(sy, hs2_ev[i - 2])
                    ld_ev = P.dma(sy, xb[:], x[i * 128:(i + 1) * 128, :], lds[i % 2])
                    P.w(v, ld_ev)
                    hs2_ev[i] = P.vo(lambda i=i: v.tensor_scalar(
                        out=hsf[:], in0=xb[:], scalar1=STM[:, i:i + 1],
                        scalar2=STR[:, i:i + 1], op0=ALU.add, op1=ALU.mult))
                    rms_stats(hsf)
                    xh = XH3[:, (i % 2) * 1024:(i % 2 + 1) * 1024]
                    if txf_ev[i % 2] is not None:
                        P.w(v, txf_ev[i % 2])
                    xh_ev = P.vo(lambda xh=xh: v.tensor_scalar_mul(xh, hsf[:],
                                                                   st8[:, 6:7]))
                    pX = pX2 if i % 2 == 0 else pX3
                    P.w(t, xh_ev)
                    if i == 0:
                        P.wd(t, s_wf, WFB)
                    if i >= 2:
                        P.w(t, fh_ev[i - 2])
                    ti = None
                    for c in range(NCH):
                        ti = t.matmul(pX[:, c * 128:(c + 1) * 128],
                                      xh[:, c * 128:(c + 1) * 128], IDB[:],
                                      start=True, stop=True)
                    txf_ev[i % 2] = P.tlast(ti)
                    P.w(v, txf_ev[i % 2])
                    ev = None
                    for c in range(NCH):
                        ev = P.vo(lambda c=c, i=i, pX=pX: v.tensor_copy(
                            out=FH[:, c * T + i * 128: c * T + (i + 1) * 128],
                            in_=pX[:, c * 128:(c + 1) * 128]))
                    fh_ev[i] = ev
                fh_done = fh_ev[NT - 1]

                guev = None
                ev = None
                for ftl in range(4):
                    for tr in range(8):
                        for hf in range(2):
                            pDst = pA2 if hf == 0 else pG2
                            BUF = GH2 if hf == 0 else AH2
                            WB = FWG if hf == 0 else FWU
                            if ftl == 0 and tr == 0 and hf == 0:
                                P.w(t, fh_done)
                                P.w(t, fh_ev[NT - 2])
                                P.w(t, fw_ev)
                            if guev is not None:
                                P.w(t, guev)
                            ti = None
                            for c in range(NCH):
                                ti = t.matmul(
                                    pDst[:],
                                    WB[:, ftl * 1024 + c * 128:
                                       ftl * 1024 + (c + 1) * 128],
                                    FH[:, c * T + tr * 512: c * T + (tr + 1) * 512],
                                    start=(c == 0), stop=(c == NCH - 1))
                            mmev = P.tlast(ti)
                            P.w(sc, mmev)
                            if guev is not None:
                                P.w(sc, guev)
                            fn = AF.Gelu if hf == 0 else AF.Identity
                            ev = P.so(lambda fn=fn, BUF=BUF, pDst=pDst:
                                      sc.activation(out=BUF[:], in_=pDst[:], func=fn))
                        P.w(v, ev)
                        guev = P.vo(lambda ftl=ftl, tr=tr: v.tensor_mul(
                            FBG[:, ftl * T + tr * 512: ftl * T + (tr + 1) * 512],
                            GH2[:], AH2[:]))
                fb_done = guev

                pWs = [pW2, pW3]
                r_cons = None
                fbt_ev = None
                tyv_last = None
                yadd_ev = [None] * NT
                for tr in range(8):
                    if tr == 0:
                        P.w(g, cc4)
                        P.wd(g, s_sc, P.cnt[id(s_sc)])
                        P.w(g, kw_ev)
                        P.w(g, ov_ev)
                    else:
                        P.w(g, r_cons)
                    gat = P._mark(None, g.dma_gather(
                        RG[:], fo_all.ap(), WCG[:, tr * 64:(tr + 1) * 64],
                        1024, 1024, 1024), s_g2, 16)
                    RGT = gat[1]
                    o2_ev = [None] * NCH
                    for fo in range(NCH):
                        k = tr * NCH + fo
                        pU = pWs[k % 2]
                        if k == 0:
                            P.w(t, fb_done)
                        if k >= 2:
                            P.w(t, o2_ev[fo - 2] if fo >= 2 else tyv_last or fb_done)
                        ti = None
                        for c in range(4):
                            ti = t.matmul(
                                pU[:],
                                FWD[:, fo * 512 + c * 128: fo * 512 + (c + 1) * 128],
                                FBG[:, c * T + tr * 512: c * T + (tr + 1) * 512],
                                start=(c == 0), stop=(c == 3))
                        mm2 = P.tlast(ti)
                        P.w(sc, mm2)
                        if tr > 0 and fo == 0:
                            P.w(sc, tyv_last)
                        o2_ev[fo] = P.so(lambda fo=fo, pU=pU: sc.activation(
                            out=OUTC[:, fo * 512:(fo + 1) * 512], in_=pU[:],
                            func=AF.Identity))
                    for i in range(4):
                        gidx = tr * 4 + i
                        P.w(t, o2_ev[NCH - 1])
                        if fbt_ev is not None:
                            P.w(t, fbt_ev)
                        ti = None
                        for do in range(NCH):
                            s2 = OUTC[:, do * 512 + i * 128: do * 512 + (i + 1) * 128]
                            dst = (pA2 if do < 4 else pG2)[:, (do % 4) * 128:
                                                           (do % 4 + 1) * 128]
                            ti = t.transpose(dst, s2, IDF[:])
                        tyv = P.tlast(ti)
                        tyv_last = tyv
                        if gidx < 2:
                            P.w(sy, fh_done)
                        else:
                            P.w(sy, yadd_ev[gidx - 2])
                        ld_ev = P.dma(sy, xbufs[gidx % 2][:],
                                      x[gidx * 128:(gidx + 1) * 128, :],
                                      lds[gidx % 2])
                        P.w(v, ld_ev)
                        P.vo(lambda gidx=gidx: v.tensor_scalar(
                            out=hsf[:], in0=xbufs[gidx % 2][:],
                            scalar1=STM[:, gidx:gidx + 1],
                            scalar2=STR[:, gidx:gidx + 1],
                            op0=ALU.add, op1=ALU.mult))
                        P.vo(lambda: v.tensor_copy(out=CMBV[:], in_=hsf[:]))
                        P.wd(v, s_g2, RGT)
                        P.vo(lambda i=i: v.tensor_copy(
                            out=sq[:], in_=RG[:, 2 * i:2 * i + 1, :].rearrange(
                                "p o e -> p (o e)")))
                        P.vo(lambda gidx=gidx: v.scalar_tensor_tensor(
                            out=CMBV[:], in0=sq[:], scalar=KWT[:, gidx:gidx + 1],
                            in1=CMBV[:], op0=ALU.mult, op1=ALU.add))
                        r_cons = P.vo(lambda i=i: v.tensor_copy(
                            out=sq[:], in_=RG[:, 2 * i + 1:2 * i + 2, :].rearrange(
                                "p o e -> p (o e)")))
                        P.vo(lambda gidx=gidx: v.scalar_tensor_tensor(
                            out=CMBV[:], in0=sq[:],
                            scalar=KWT[:, 32 + gidx:33 + gidx],
                            in1=CMBV[:], op0=ALU.mult, op1=ALU.add))
                        P.w(v, tyv)
                        P.vo(lambda: v.tensor_copy(out=FBT[:, 0:512], in_=pA2[:]))
                        P.vo(lambda: v.tensor_copy(out=FBT[:, 512:1024], in_=pG2[:]))
                        fbt_ev = P.vo(lambda: v.tensor_add(FBT[:], hsf[:], FBT[:]))
                        P.vo(lambda: v.tensor_sub(FBT[:], FBT[:], CMBV[:]))
                        P.vo(lambda gidx=gidx: v.scalar_tensor_tensor(
                            out=CMBV[:], in0=FBT[:], scalar=OVT[:, gidx:gidx + 1],
                            in1=CMBV[:], op0=ALU.mult, op1=ALU.add))
                        ln_stats(CMBV, sq)
                        P.wd(v, s_st, P.cnt.get(id(s_st), 0))
                        P.vo(lambda: v.tensor_scalar(
                            out=yt0[:], in0=CMBV[:], scalar1=st8[:, 2:3],
                            scalar2=st8[:, 6:7], op0=ALU.add, op1=ALU.mult))
                        yadd_ev[gidx] = P.vo(lambda gidx=gidx: v.tensor_add(
                            yt0[:], xbufs[gidx % 2][:], yt0[:]))
                        P.w(sy, yadd_ev[gidx])
                        P.dma(sy, y[gidx * 128:(gidx + 1) * 128, :], yt0[:], s_st)
            P.wd(sy, s_st, P.cnt[id(s_st)])
            P.wd(sy, s_g2, P.cnt.get(id(s_g2), 0))
            P.wd(sy, s_wf, P.cnt.get(id(s_wf), 0))

        emit(None)


def _stage4_inputs(inputs):
    def tm(w):
        C, F = w.shape
        return np.ascontiguousarray(
            w.reshape(C // 128, 128, F // 128, 128).transpose(2, 1, 0, 3)).reshape(
                F, C)

    wr = np.ascontiguousarray(np.asarray(inputs["L1_ffn_Wr"], np.float32))
    Wg = np.asarray(inputs["L1_ffn_Wg"], np.float32)
    Wu = np.asarray(inputs["L1_ffn_Wu"], np.float32)
    Wd = np.asarray(inputs["L1_ffn_Wd"], np.float32)
    nw = np.asarray(inputs["L1_ffn_norm_w"], np.float32)
    fn = np.asarray(inputs["L1_ffn_f_norm"], np.float32)
    fwg = tm(np.asarray(inputs["L1_ffn_fWg"], np.float32))
    fwu = tm(np.asarray(inputs["L1_ffn_fWu"], np.float32))
    fwd = tm(np.asarray(inputs["L1_ffn_fWd"], np.float32))
    maps = []
    for b in range(NCORES):
        toks = ((b * 8192 + np.arange(64)[None, :] * 128
                 + np.arange(128)[:, None]) // 2).astype(np.float32)
        itb = b * 32 if b < 4 else TBW // 128 + (b - 4) * 32
        co = np.array([[b * 64, itb, b * 32, 256 + b * 32]], np.int32)
        vecs = np.concatenate([nw[b], fn]).reshape(2 * D, 1).astype(np.float32)
        maps.append({
            "s4_wr": wr,
            "s4_wgt": tm(Wg[b]), "s4_wut": tm(Wu[b]), "s4_wdt": tm(Wd[b]),
            "s4_fwgt": fwg, "s4_fwut": fwu, "s4_fwdt": fwd,
            "s4_vecs": np.ascontiguousarray(vecs),
            "s4_tokc": np.ascontiguousarray(toks), "s4_coff": co,
        })
    return maps



def emit_stage3(nc, P, env):
    x = env["x2"]
    wi = env["s3_wi"]
    wo = env["s3_wo"]
    vin = env["s3_vecs"]
    y = env["x3"]
    idb = env["idb"]
    idf = env["idf"]

    ctx = ExitStack()
    with ctx:
        XH = _sb(ctx, nc, "XH", [128, NCH * T], BF16)
        H = _sb(ctx, nc, "H", [128, NCH * T], BF16)
        WI = _sb(ctx, nc, "WI", [128, NCH * D], BF16)
        WO = _sb(ctx, nc, "WO", [128, NCH * D], BF16)
        GB = _sb(ctx, nc, "GB", [128, NCH], F32)
        GAM = _sb(ctx, nc, "GAM", [128, NCH], F32)
        LAM = _sb(ctx, nc, "LAM", [128, NCH], F32)
        BO = _sb(ctx, nc, "BO", [128, NCH], F32)
        LB = _sb(ctx, nc, "LB", [128, 512], F32)
        IDB = _sb(ctx, nc, "IDB", [128, 128], BF16)
        IDF = _sb(ctx, nc, "IDF", [128, 128], F32)
        xb0 = _sb(ctx, nc, "xb0", [128, D], F32)
        xb1 = _sb(ctx, nc, "xb1", [128, D], F32)
        sq = _sb(ctx, nc, "sq", [128, D], F32)
        st8 = _sb(ctx, nc, "st8", [128, 16], F32)
        xhbt = _sb(ctx, nc, "xhbt", [128, D], BF16)
        gu0 = _sb(ctx, nc, "gu0", [128, 512], F32)
        gu1 = _sb(ctx, nc, "gu1", [128, 512], F32)
        hf0 = _sb(ctx, nc, "hf0", [128, 512], F32)
        hf1 = _sb(ctx, nc, "hf1", [128, 512], F32)
        OUTF = _sb(ctx, nc, "OUTF", [128, NCH * 512], F32)
        yt0 = _sb(ctx, nc, "yt0", [128, D], F32)
        tpA = _ps(ctx, nc, "tpA", [128, 1024], BF16)
        tpB = _ps(ctx, nc, "tpB", [128, 1024], BF16)
        pU0 = _ps(ctx, nc, "pU0", [128, 512], F32)
        pU1 = _ps(ctx, nc, "pU1", [128, 512], F32)
        pT0 = _ps(ctx, nc, "pT0", [128, 512], F32)
        pT1 = _ps(ctx, nc, "pT1", [128, 512], F32)
        pV0 = _ps(ctx, nc, "pV0", [128, 512], F32)
        pV1 = _ps(ctx, nc, "pV1", [128, 512], F32)
        s_w = env["s_w"]
        s_ld = env["s_ld"]
        s_le = env["s_le"]
        s_st = env["s_st"]

        xbufs = [xb0, xb1]
        lds = [s_ld, s_le]
        pUs = [pU0, pU1]
        gus = [gu0, gu1]
        hfs = [hf0, hf1]

        def emit(_):
            v, sc, t, sy, g = P.v, P.s, P.t, P.sy, P.g
            for c in range(NCH):
                P.dma(g, WI[:, c * D:(c + 1) * D], wi[c * 128:(c + 1) * 128, :], s_w)
                P.dma(g, WO[:, c * D:(c + 1) * D], wo[c * 128:(c + 1) * 128, :], s_w)
            for vi, dst in enumerate((GB, GAM, LAM, BO)):
                for c in range(NCH):
                    P.dma(g, dst[:, c:c + 1],
                          vin[(vi * NCH + c) * 128:(vi * NCH + c + 1) * 128, :], s_w)
            P.dma(g, IDB[:], idb.ap(), s_w)
            P.dma(g, IDF[:], idf.ap(), s_w)
            W_ALL = P.cnt[id(s_w)]

            P.wd(sc, s_w, W_ALL)
            lam_ev = P.so(lambda: sc.activation(out=LAM[:], in_=LAM[:], func=AF.Exp))
            P.wd(v, s_w, W_ALL)
            eps_ev = P.vo(lambda: v.memset(st8[:, 7:8], EPS_LN))
            P.vo(lambda: v.memset(st8[:, 15:16], EPS_LN))
            gb_ev = P.vo(lambda: v.tensor_mul(GB[:], GB[:], GAM[:]))

            xh_ev = [None] * NT
            ev_ev = [None] * NT
            vst_ev = [None] * NT
            tp_ev = [None] * NT
            sqA_ev = [None] * NT
            sqs3 = [sq, yt0]

            def s3a_part1(i):
                xb = xbufs[i % 2]
                o = 8 * (i % 2)
                sqb = sqs3[i % 2]
                if i >= 2:
                    P.w(sy, xh_ev[i - 2])
                ld_ev = P.dma(sy, xb[:], x[i * 128:(i + 1) * 128, :], lds[i % 2])
                P.w(sc, ld_ev)
                if i >= 2:
                    P.w(sc, vst_ev[i - 2])
                sq_ev = P.so(lambda: sc.activation(out=sqb[:], in_=xb[:], func=AF.Square))
                sqA_ev[i] = sq_ev
                P.w(v, ld_ev)
                P.w(v, sq_ev)
                P.vo(lambda: v.reduce_sum(st8[:, o + 0:o + 1], xb[:], axis=AX.X))
                P.vo(lambda: v.reduce_sum(st8[:, o + 1:o + 2], sqb[:], axis=AX.X))
                P.vo(lambda: v.tensor_scalar_mul(st8[:, o + 2:o + 3], st8[:, o + 0:o + 1], -1.0 / D))
                P.vo(lambda: v.tensor_scalar_mul(st8[:, o + 3:o + 4], st8[:, o + 1:o + 2], 1.0 / D))
                P.vo(lambda: v.tensor_mul(st8[:, o + 4:o + 5], st8[:, o + 2:o + 3], st8[:, o + 2:o + 3]))
                vst_ev[i] = P.vo(lambda: v.tensor_sub(st8[:, o + 3:o + 4], st8[:, o + 3:o + 4], st8[:, o + 4:o + 5]))

            def s3a_part2(i):
                xb = xbufs[i % 2]
                o = 8 * (i % 2)
                P.w(sc, vst_ev[i])
                sd_ev = P.so(lambda: sc.activation(out=st8[:, o + 5:o + 6], in_=st8[:, o + 3:o + 4],
                                                   func=AF.Sqrt, bias=st8[:, o + 7:o + 8]))
                P.w(v, sd_ev)
                P.vo(lambda: v.reciprocal(st8[:, o + 6:o + 7], st8[:, o + 5:o + 6]))
                if i >= 2:
                    P.w(v, tp_ev[i - 1])
                xh_ev[i] = P.vo(lambda: v.tensor_scalar(
                    out=xhbt[:], in0=xb[:], scalar1=st8[:, o + 2:o + 3], scalar2=st8[:, o + 6:o + 7],
                    op0=ALU.add, op1=ALU.mult))
                tp = tpA if i % 2 == 0 else tpB
                P.w(t, xh_ev[i])
                if i == 0:
                    P.wd(t, s_w, W_ALL)
                if i >= 2:
                    P.w(t, ev_ev[i - 2])
                ti = None
                for c in range(NCH):
                    ti = t.transpose(tp[:, c * 128:(c + 1) * 128],
                                     xhbt[:, c * 128:(c + 1) * 128], IDB[:])
                tp_ev[i] = P.tlast(ti)
                P.w(v, tp_ev[i])
                for c in range(NCH):
                    ev_ev[i] = P.vo(lambda c=c, tp=tp, i=i: v.tensor_copy(
                        out=XH[:, c * T + i * 128: c * T + (i + 1) * 128],
                        in_=tp[:, c * 128:(c + 1) * 128]))

            s3a_part1(0)
            for i in range(NT):
                if i + 1 < NT:
                    s3a_part1(i + 1)
                s3a_part2(i)

            gu_ev = [None] * (NCH * 8)
            scan_ev = [None] * (NCH * 8)
            lb_ev = None
            for eo in range(NCH):
                P.w(v, lam_ev)
                P.vo(lambda: v.memset(LB[:], 1.0))
                lb_ev = P.vo(lambda: v.tensor_scalar_mul(LB[:], LB[:], LAM[:, eo:eo + 1]))
                for tr in range(8):
                    k = eo * 8 + tr
                    pU = pUs[k % 2]
                    gu = gus[k % 2]
                    hf = hfs[tr % 2]
                    if k == 0:
                        P.w(t, ev_ev[NT - 1])
                    if k >= 2:
                        P.w(t, gu_ev[k - 2])
                    ti = None
                    for c in range(NCH):
                        ti = t.matmul(pU[:], WI[:, c * D + eo * 128: c * D + (eo + 1) * 128],
                                      XH[:, c * T + tr * 512: c * T + (tr + 1) * 512],
                                      start=(c == 0), stop=(c == NCH - 1))
                    mm_ev = P.tlast(ti)
                    P.w(sc, mm_ev)
                    if k >= 2:
                        P.w(sc, scan_ev[k - 2])
                    gu_ev[k] = P.so(lambda: sc.activation(
                        out=gu[:], in_=pU[:], func=AF.Identity,
                        bias=GB[:, eo:eo + 1], scale=GAM[:, eo:eo + 1]))
                    P.w(v, gu_ev[k])
                    ini = 0.0 if tr == 0 else hfs[(tr - 1) % 2][:, 511:512]
                    P.vo(lambda: v.tensor_tensor_scan(
                        out=hf[:], data0=LB[:], data1=gu[:], initial=ini,
                        op0=ALU.mult, op1=ALU.add))
                    scan_ev[k] = P.vo(lambda: v.tensor_copy(
                        out=H[:, eo * T + tr * 512: eo * T + (tr + 1) * 512], in_=hf[:]))

            of_ev = [None] * (8 * NCH)
            ty_ev = [None] * NT
            ev2_ev = [None] * NT
            fin_ev = [None] * NT
            stb3 = P.cnt.get(id(s_st), 0)
            for tr in range(8):
                add_ev = None
                for do in range(NCH):
                    k = tr * NCH + do
                    pU = pUs[k % 2]
                    if k == 0:
                        P.w(t, scan_ev[NCH * 8 - 1])
                    if k >= 2:
                        P.w(t, of_ev[k - 2])
                    ti = None
                    for c in range(NCH):
                        ti = t.matmul(pU[:], WO[:, c * D + do * 128: c * D + (do + 1) * 128],
                                      H[:, c * T + tr * 512: c * T + (tr + 1) * 512],
                                      start=(c == 0), stop=(c == NCH - 1))
                    mm_ev = P.tlast(ti)
                    P.w(sc, mm_ev)
                    if tr > 0:
                        P.w(sc, ty_ev[tr * 4 - 1])
                    of_ev[k] = P.so(lambda: sc.activation(
                        out=OUTF[:, do * 512:(do + 1) * 512], in_=pU[:],
                        func=AF.Identity, bias=BO[:, do:do + 1]))
                    P.w(v, of_ev[k])
                    add_ev = P.vo(lambda: v.tensor_add(
                        OUTF[:, do * 512:(do + 1) * 512],
                        XH[:, do * T + tr * 512: do * T + (tr + 1) * 512],
                        OUTF[:, do * 512:(do + 1) * 512]))
                for i in range(4):
                    gidx = tr * 4 + i
                    pa, pb = (pT0, pT1) if gidx % 2 == 0 else (pV0, pV1)
                    P.w(t, add_ev)
                    if gidx >= 2:
                        P.w(t, ev2_ev[gidx - 2])
                    ti = None
                    for do in range(NCH):
                        src = OUTF[:, do * 512 + i * 128: do * 512 + (i + 1) * 128]
                        dst = (pa if do < 4 else pb)[:, (do % 4) * 128:(do % 4 + 1) * 128]
                        ti = t.transpose(dst, src, IDF[:])
                    ty_ev[gidx] = P.tlast(ti)
                    if gidx >= 2:
                        P.w(sy, fin_ev[gidx - 2])
                    else:
                        P.w(sy, xh_ev[NT - 2 + gidx])
                        P.w(sy, sqA_ev[NT - 2 + gidx])
                    ld_ev = P.dma(sy, xbufs[gidx % 2][:],
                                  x[gidx * 128:(gidx + 1) * 128, :], lds[gidx % 2])
                    P.w(v, ty_ev[gidx])
                    if gidx >= 1:
                        P.wd(v, s_st, stb3 + 16 * gidx)
                    P.vo(lambda: v.tensor_copy(out=yt0[:, 0:512], in_=pa[:]))
                    ev2_ev[gidx] = P.vo(lambda: v.tensor_copy(out=yt0[:, 512:1024], in_=pb[:]))
                    P.w(sc, ev2_ev[gidx])
                    if gidx >= 1:
                        P.w(sc, vst2_ev[gidx - 1])
                    sq_ev = P.so(lambda: sc.activation(out=sq[:], in_=yt0[:], func=AF.Square))
                    P.w(v, sq_ev)
                    P.vo(lambda: v.reduce_sum(st8[:, 0:1], yt0[:], axis=AX.X))
                    P.vo(lambda: v.reduce_sum(st8[:, 1:2], sq[:], axis=AX.X))
                    P.vo(lambda: v.tensor_scalar_mul(st8[:, 2:3], st8[:, 0:1], -1.0 / D))
                    P.vo(lambda: v.tensor_scalar_mul(st8[:, 3:4], st8[:, 1:2], 1.0 / D))
                    P.vo(lambda: v.tensor_mul(st8[:, 4:5], st8[:, 2:3], st8[:, 2:3]))
                    vst2_ev[gidx] = P.vo(lambda: v.tensor_sub(st8[:, 3:4], st8[:, 3:4],
                                                              st8[:, 4:5]))
                    P.w(sc, vst2_ev[gidx])
                    sd_ev = P.so(lambda: sc.activation(out=st8[:, 5:6], in_=st8[:, 3:4],
                                                       func=AF.Sqrt, bias=st8[:, 7:8]))
                    P.w(v, sd_ev)
                    P.vo(lambda: v.reciprocal(st8[:, 6:7], st8[:, 5:6]))
                    P.vo(lambda: v.tensor_scalar(out=yt0[:], in0=yt0[:],
                                                 scalar1=st8[:, 2:3], scalar2=st8[:, 6:7],
                                                 op0=ALU.add, op1=ALU.mult))
                    P.w(v, ld_ev)
                    fin_ev[gidx] = P.vo(lambda: v.tensor_add(yt0[:], xbufs[gidx % 2][:], yt0[:]))
                    P.w(sy, fin_ev[gidx])
                    P.dma(sy, y[gidx * 128:(gidx + 1) * 128, :], yt0[:], s_st)
            P.wd(sy, s_st, P.cnt[id(s_st)])
        vst2_ev = [None] * NT
        emit(None)
        _stage_fence(P, env)


def _stage3_inputs(inputs):
    wi_t = np.ascontiguousarray(np.asarray(inputs["L1_attn_W_in"]).T).astype(np.float32)
    wo_t = np.ascontiguousarray(np.asarray(inputs["L1_attn_W_out"]).T).astype(np.float32)
    vecs = np.concatenate([
        np.asarray(inputs["L1_attn_b_in"], np.float32).ravel(),
        np.asarray(inputs["L1_attn_gamma"], np.float32).ravel(),
        np.asarray(inputs["L1_attn_lambda_log"], np.float32).ravel(),
        np.asarray(inputs["L1_attn_b_out"], np.float32).ravel(),
    ]).reshape(4 * D, 1)
    return {"s3_wi": wi_t, "s3_wo": wo_t, "s3_vecs": vecs}



def build_fused():
    nc = bass.Bass("TRN2", target_bir_lowering=False, num_devices=8)
    env = {}
    env["x0"] = nc.dram_tensor("x", [T, D], F32, kind="ExternalInput")
    env["y"] = nc.dram_tensor("y", [T, D], F32, kind="ExternalOutput")
    env["x1"] = nc.dram_tensor("x1b", [T, D], F32, kind="Internal")
    env["x2"] = nc.dram_tensor("x2b", [T, D], F32, kind="Internal")
    env["x3"] = nc.dram_tensor("x3b", [T, D], F32, kind="Internal")
    env["s1_wqkv"] = nc.dram_tensor("s1_wqkv", [3 * 8 * 128, D], F32,
                                    kind="ExternalInput")
    env["s1_wo"] = nc.dram_tensor("s1_wo", [D, D], F32, kind="ExternalInput")
    env["s1_vecs"] = nc.dram_tensor("s1_vecs", [D, 1], F32, kind="ExternalInput")
    env["s2_wqkvp"] = nc.dram_tensor("s2_wqkvp", [2 * 12 * 128, D], F32,
                                     kind="ExternalInput")
    env["s2_wol"] = nc.dram_tensor("s2_wol", [2 * 512, D], F32,
                                   kind="ExternalInput")
    env["s2_w1t"] = nc.dram_tensor("s2_w1t", [2 * 2 * NF2 * 128, D], F32,
                                   kind="ExternalInput")
    env["s2_w2t"] = nc.dram_tensor("s2_w2t", [2 * NCH * 128, NF2 * 128], F32,
                                   kind="ExternalInput")
    env["s2_vec3"] = nc.dram_tensor("s2_vec3", [2 * 3 * 128, NCH], F32,
                                    kind="ExternalInput")
    env["s2_qsks"] = nc.dram_tensor("s2_qsks", [2 * 2 * 64, 1], F32,
                                    kind="ExternalInput")
    env["s2_b1p"] = nc.dram_tensor("s2_b1p", [2 * 2 * 128, NF2], F32,
                                   kind="ExternalInput")
    env["s2_xi"] = nc.dram_tensor("s2_xi", [T, D], F32, kind="Internal")
    env["s3_wi"] = nc.dram_tensor("s3_wi", [D, D], F32, kind="ExternalInput")
    env["s3_wo"] = nc.dram_tensor("s3_wo", [D, D], F32, kind="ExternalInput")
    env["s3_vecs"] = nc.dram_tensor("s3_vecs", [4 * D, 1], F32,
                                    kind="ExternalInput")
    env["s4_wr"] = nc.dram_tensor("s4_wr", [D, NE], F32, kind="ExternalInput")
    env["s4_wgt"] = nc.dram_tensor("s4_wgt", [16 * 128, D], F32,
                                   kind="ExternalInput")
    env["s4_wut"] = nc.dram_tensor("s4_wut", [16 * 128, D], F32,
                                   kind="ExternalInput")
    env["s4_wdt"] = nc.dram_tensor("s4_wdt", [NCH * 128, 2048], F32,
                                   kind="ExternalInput")
    env["s4_fwgt"] = nc.dram_tensor("s4_fwgt", [4 * 128, D], F32,
                                    kind="ExternalInput")
    env["s4_fwut"] = nc.dram_tensor("s4_fwut", [4 * 128, D], F32,
                                    kind="ExternalInput")
    env["s4_fwdt"] = nc.dram_tensor("s4_fwdt", [NCH * 128, 512], F32,
                                    kind="ExternalInput")
    env["s4_vecs"] = nc.dram_tensor("s4_vecs", [2 * D, 1], F32,
                                    kind="ExternalInput")
    env["s4_tokc"] = nc.dram_tensor("s4_tokc", [128, 64], F32,
                                    kind="ExternalInput")
    env["s4_coff"] = nc.dram_tensor("s4_coff", [1, 4], I32,
                                    kind="ExternalInput")
    env["s4_hsb"] = nc.dram_tensor("s4_hsb", [T, D], BF16, kind="Internal")
    env["s4_lgd"] = nc.dram_tensor("s4_lgd", [T, NE], F32, kind="Internal")
    env["s4_hs_all"] = nc.dram_tensor("s4_hs_all", [NCORES * T, D], BF16,
                                      kind="Internal")
    env["s4_lg_all"] = nc.dram_tensor("s4_lg_all", [NCORES * T, NE], F32,
                                      kind="Internal")
    env["s4_esd"] = nc.dram_tensor("s4_esd", [NA, 1], F32, kind="Internal")
    env["s4_paf"] = nc.dram_tensor("s4_paf", [NA, 1], F32, kind="Internal")
    env["s4_pafR"] = nc.dram_tensor("s4_pafR", [128, 512], F32, kind="Internal")
    env["s4_esdR"] = nc.dram_tensor("s4_esdR", [128, 512], F32, kind="Internal")
    env["s4_w12d"] = nc.dram_tensor("s4_w12d", [128, 512], F32, kind="Internal")
    env["s4_itd"] = nc.dram_tensor("s4_itd", [NTB, 64], F32, kind="Internal")
    env["s4_it_all"] = nc.dram_tensor("s4_it_all", [NTB, 64], F32,
                                      kind="Internal")
    env["s4_fod"] = nc.dram_tensor("s4_fod", [T, D], BF16, kind="Internal")
    env["s4_fo_all"] = nc.dram_tensor("s4_fo_all", [NCORES * T, D], BF16,
                                      kind="Internal")

    env["idb"] = nc.inline_tensor(ID_BF16, name="id_bf16")
    env["idf"] = nc.inline_tensor(ID_F32, name="id_f32")
    ones2_np = np.zeros((128, 2), np.float32)
    ones2_np[:64, 0] = 1.0
    ones2_np[64:, 1] = 1.0
    env["ones2c"] = nc.inline_tensor(ones2_np, name="ones2c")
    sel65_np = np.zeros((65, 64), np.float32)
    sel65_np[64, :] = 1.0
    env["sel65c"] = nc.inline_tensor(sel65_np, name="sel65c")
    csel_np = np.zeros((2, 2 * 128), np.float32)
    csel_np[0, 0:128] = 1.0
    csel_np[1, 128:256] = 1.0
    env["cselc"] = nc.inline_tensor(csel_np, name="cselc")
    _plan, bias_np, rows_np, mlo_np, mup_np, _hspan, _pairw = _S1
    env["s1_biasc"] = nc.inline_tensor(bias_np, name="s1_biasc")
    env["s1_rowsc"] = nc.inline_tensor(
        np.ascontiguousarray(rows_np.reshape(1, 16 * 1024)), name="s1_rowsc")
    env["s1_mloc"] = nc.inline_tensor(mlo_np, name="s1_mloc")
    env["s1_mupc"] = nc.inline_tensor(mup_np, name="s1_mupc")
    env["s1_onesrc"] = nc.inline_tensor(np.ones((1, 128), np.float32),
                                        name="s1_onesrc")
    cosq_np, sinq_np, cosk_np, sink_np, perm_np = _rotary_tables()
    n_ = np.arange(1024)[None, :]
    p_ = np.arange(128)[:, None]
    amge_np = np.where(p_ >= n_ - 512, 1.0, 0.0).astype(BF16NP)
    amle_np = np.where(p_ <= n_ - 512, 1.0, 0.0).astype(BF16NP)
    env["s2_cosqc"] = nc.inline_tensor(cosq_np, name="s2_cosqc")
    env["s2_sinqc"] = nc.inline_tensor(sinq_np, name="s2_sinqc")
    env["s2_coskc"] = nc.inline_tensor(cosk_np, name="s2_coskc")
    env["s2_sinkc"] = nc.inline_tensor(sink_np, name="s2_sinkc")
    env["s2_permc"] = nc.inline_tensor(perm_np, name="s2_permc")
    env["s2_amgec"] = nc.inline_tensor(amge_np, name="s2_amgec")
    env["s2_amlec"] = nc.inline_tensor(amle_np, name="s2_amlec")
    sh1_np = np.zeros((128, 128), np.float32)
    for i in range(127):
        sh1_np[i, i + 1] = 1.0
    env["s2_sh1c"] = nc.inline_tensor(sh1_np, name="s2_sh1c")
    e0_np = np.zeros((1, 128), np.float32)
    e0_np[0, 0] = 1.0
    env["s2_e0c"] = nc.inline_tensor(e0_np, name="s2_e0c")
    env["s2_one1c"] = nc.inline_tensor(np.ones((1, 128), np.float32),
                                       name="s2_one1c")
    iotae = np.tile(np.arange(8, dtype=np.float32), (128, 256))
    env["s4_iotaec"] = nc.inline_tensor(np.ascontiguousarray(iotae),
                                        name="s4_iotaec")
    env["s4_iota8c"] = nc.inline_tensor(
        np.arange(8, dtype=np.float32).reshape(8, 1), name="s4_iota8c")

    ctx = ExitStack()
    with ctx:
        for n in SEM_NAMES:
            env[n] = _sm(ctx, nc, n)
        blk = ctx.enter_context(nc.Block())
        P = Pgm(nc, ctx)

        def emit(_):
            P.g.load_library(library_config.mlp)
            emit_stage1(NCProxy(nc, "a_"), P, env)
            emit_stage2(NCProxy(nc, "b_"), P, env)
            emit_stage3(NCProxy(nc, "c_"), P, env)
            emit_stage4(NCProxy(nc, "d_"), P, env)
        blk.gpsimd(emit)

    return nc


def kernel(**inputs):
    x0 = np.ascontiguousarray(np.asarray(inputs["L0_attn_x"], np.float32))
    nc = build_fused()
    lower_extended_insts(nc)
    m1 = _stage1_inputs(inputs)
    m2 = _stage2_inputs(inputs)
    m3 = _stage3_inputs(inputs)
    m4 = _stage4_inputs(inputs)
    maps = []
    for b in range(NCORES):
        m = {"x": np.ascontiguousarray(x0[b])}
        m.update(m1)
        m.update(m2)
        m.update(m3)
        m.update(m4[b])
        maps.append(m)
    res = run_bass_kernel_spmd(nc, maps, core_ids=list(range(NCORES)))
    out = np.stack([res.results[b]["y"] for b in range(NCORES)])
    return np.ascontiguousarray(np.asarray(out, np.float32))


# version 09 (final)
# speedup vs baseline: 1.0296x; 1.0296x over previous
import numpy as np
import ml_dtypes
from contextlib import ExitStack
import concourse.bass as bass
import concourse.mybir as mybir
from concourse.bass_utils import run_bass_kernel_spmd
from concourse.bass_core_rust import IndirectOffsetOnAxis
from concourse import library_config
from concourse.library_overlay import lower_extended_insts

F32 = mybir.dt.float32
BF16 = mybir.dt.bfloat16
I32 = mybir.dt.int32
I16 = mybir.dt.int16
U32 = mybir.dt.uint32
AF = mybir.ActivationFunctionType
ALU = mybir.AluOpType
AX = mybir.AxisListType

NCORES = 8
T = 4096
D = 1024
NT = T // 128
NCH = D // 128
EPS_LN = 1e-5

BF16NP = ml_dtypes.bfloat16
ID_F32 = np.eye(128, dtype=np.float32)
ID_BF16 = np.eye(128, dtype=BF16NP)




def _sb(ctx, nc, name, shape, dt):
    return ctx.enter_context(nc.sbuf_tensor(name, shape, dt))


def _ps(ctx, nc, name, shape, dt=F32):
    return ctx.enter_context(nc.psum_tensor(name, shape, dt))


def _sm(ctx, nc, name):
    return ctx.enter_context(nc.semaphore(name))


class NCProxy:

    def __init__(self, nc, pre):
        self._nc = nc
        self._pre = pre

    def sbuf_tensor(self, name, shape, dt):
        return self._nc.sbuf_tensor(self._pre + name, shape, dt)

    def psum_tensor(self, name, shape, dt):
        return self._nc.psum_tensor(self._pre + name, shape, dt)

    def __getattr__(self, a):
        return getattr(self._nc, a)


SEM_NAMES = ["s_w", "s_ld", "s_le", "s_st", "s_wp", "s_wp0", "s_sr", "s_sx",
             "s_wf", "s_wg", "s_cc", "s_sc", "s_g2"]


def _stage_fence(P, env):
    P.fence(extra=[(env[n], P.cnt.get(id(env[n]), 0)) for n in SEM_NAMES])


class Pgm:

    def __init__(self, nc, ctx):
        self.nc = nc
        self.v, self.s, self.t = nc.vector, nc.scalar, nc.tensor
        self.sy, self.g = nc.sync, nc.gpsimd
        self.cnt = {}
        self.last = {}
        self.SV = ctx.enter_context(nc.semaphore("chainV"))
        self.SS = ctx.enter_context(nc.semaphore("chainS"))
        self.ST = ctx.enter_context(nc.semaphore("chainT"))
        self.SG = ctx.enter_context(nc.semaphore("chainG"))

    def _mark(self, name, instr, sem, k):
        c = self.cnt.get(id(sem), 0) + k
        self.cnt[id(sem)] = c
        instr.then_inc(sem, k)
        ev = (sem, c)
        if name is not None:
            self.last[name] = ev
        return ev

    def _chain(self, name, eng, dep):
        if dep and name in self.last:
            s, c = self.last[name]
            eng.wait_ge(s, c)

    def vo(self, instr_f, dep=True):
        self._chain("v", self.v, dep)
        return self._mark("v", instr_f(), self.SV, 1)

    def so(self, instr_f, dep=True):
        self._chain("s", self.s, dep)
        return self._mark("s", instr_f(), self.SS, 1)

    def go(self, instr_f, dep=True):
        self._chain("g", self.g, dep)
        return self._mark("g", instr_f(), self.SG, 1)

    def tmark(self):
        return self._mark("t", self.t.nop(), self.ST, 1)

    def tlast(self, instr):
        return self._mark("t", instr, self.ST, 1)

    def dma(self, eng, out, in_, sem, **kw):
        return self._mark(None, eng.dma_start(out=out, in_=in_, **kw), sem, 16)

    def dma_serial(self, eng, out, in_, sem, **kw):
        prev = self.cnt.get(id(sem), 0)
        eng.wait_ge(sem, prev)
        return self._mark(None, eng.dma_start(out=out, in_=in_, **kw), sem, 16)

    def w(self, eng, ev):
        eng.wait_ge(ev[0], ev[1])

    def fence(self, extra=()):
        evs = list(self.last.values()) + [e for e in extra if e is not None]
        engs = [self.v, self.s, self.t, self.sy, self.g]
        for e in engs:
            for ev in evs:
                e.wait_ge(ev[0], ev[1])

    def wd(self, eng, sem, val):
        eng.wait_ge(sem, val)



SLOPES = [2.0 ** (-(h + 1) / 2.0) for h in range(16)]
R_PAIR = [128, 128, 256, 512, 512, 512, 512, 512]


def _s1_plan():
    cols = {}
    vals = []

    def col(h, doff, sign):
        key = (h, doff, sign)
        if key not in cols:
            cols[key] = len(vals)
            vals.append(key)
        return cols[key]

    plan = {}
    for h in range(16):
        R = R_PAIR[h // 2]
        sl = SLOPES[h]
        A_off = R // 2
        for ri in range(T // R):
            q0 = ri * R
            A = q0 + A_off
            items = []
            for c0 in range(0, T, 128):
                rel = c0 - q0
                if rel < 0:
                    if 3 + sl * (c0 + 127 - A) >= -110:
                        items.append((c0, "lo", col(h, c0 - A, 1), None))
                elif rel < R:
                    j = rel // 128
                    items.append((c0, "lo", col(h, c0 - A, 1), j))
                    items.append((c0, "up", col(h, c0 - A, -1), j))
                else:
                    if 3 - sl * (c0 - A) >= -110:
                        items.append((c0, "up", col(h, c0 - A, -1), None))
            plan[(h, ri)] = items
    hspan = {}
    for cidx, (h, doff, sign) in enumerate(vals):
        if h not in hspan:
            hspan[h] = [cidx, cidx + 1]
        else:
            hspan[h][1] = cidx + 1
    nb = len(vals)
    bias = np.zeros((128, nb), np.float32)
    i_ = np.arange(128, dtype=np.float64)
    for c, (h, doff, sign) in enumerate(vals):
        bias[:, c] = (sign * SLOPES[h] * (doff + i_)).astype(np.float32)
    rows = np.zeros((16, 2, 512), np.float32)
    for h in range(16):
        R = R_PAIR[h // 2]
        m = np.arange(R, dtype=np.float64)
        rows[h, 0, :R] = np.exp(-SLOPES[h] * (m - R // 2)).astype(np.float32)
        rows[h, 1, :R] = np.exp(-SLOPES[h] * (R // 2 - m)).astype(np.float32)
    n_ = np.arange(1024)[None, :]
    i2 = np.arange(128)[:, None]
    mlo = (n_ >= i2 + 512).astype(BF16NP)
    mup = (n_ < i2 + 512).astype(BF16NP)
    pairw = {}
    for hp in range(8):
        lo = hspan[2 * hp][0]
        hi = hspan[2 * hp + 1][1]
        pairw[hp] = (lo, hi)
    return plan, bias, rows, mlo, mup, hspan, pairw


_S1 = _s1_plan()


def emit_stage1(nc, P, env):
    plan, bias_np, rows_np, mlo_np, mup_np, hspan, pairw = _S1
    PAIRNB = max(hi - lo for lo, hi in pairw.values())
    x = env["x0"]
    wqkv3 = env["s1_wqkv"]
    wo = env["s1_wo"]
    vin = env["s1_vecs"]
    y = env["x1"]
    idb = env["idb"]
    idf = env["idf"]
    biasc = env["s1_biasc"]
    rowsc = env["s1_rowsc"]
    mloc = env["s1_mloc"]
    mupc = env["s1_mupc"]
    ones2c = env["ones2c"]
    sel65c = env["sel65c"]
    cselc = env["cselc"]
    onesrc = env["s1_onesrc"]

    ctx = ExitStack()
    with ctx:
        ATT = _sb(ctx, nc, "ATT", [128, NCH * T], BF16)
        QF = _sb(ctx, nc, "QF", [128, T], BF16)
        KF = _sb(ctx, nc, "KF", [128, T], BF16)
        VF = _sb(ctx, nc, "VF", [128, T], BF16)
        VT = _sb(ctx, nc, "VT", [128, NT * 130], BF16)
        WP0 = _sb(ctx, nc, "WP0", [128, 3 * D], BF16)
        BIASP = _sb(ctx, nc, "BIASP", [128, PAIRNB], F32)
        BD = _sb(ctx, nc, "BD", [128, PAIRNB], F32)
        MLO = _sb(ctx, nc, "MLO", [128, 1024], BF16)
        MUP = _sb(ctx, nc, "MUP", [128, 1024], BF16)
        ONES2 = _sb(ctx, nc, "ONES2", [128, 2], F32)
        SEL65 = _sb(ctx, nc, "SEL65", [65, 64], F32)
        CSEL = _sb(ctx, nc, "CSEL", [2, 256], F32)
        ONESR = _sb(ctx, nc, "ONESR", [1, 128], F32)
        AB = [_sb(ctx, nc, f"AB{i}", [128, 512], BF16) for i in range(4)]
        PB = [_sb(ctx, nc, f"PB{i}", [128, 512], BF16) for i in range(4)]
        CMB = [_sb(ctx, nc, f"CMB{i}", [65, 512], F32) for i in range(2)]
        CM2 = [_sb(ctx, nc, f"CM2{i}", [65, 512], F32) for i in range(2)]
        IDB = _sb(ctx, nc, "IDB", [128, 128], BF16)
        IDF = _sb(ctx, nc, "IDF", [128, 128], F32)
        BO = _sb(ctx, nc, "BO", [128, NCH], F32)
        CST = _sb(ctx, nc, "CST", [2, 72], F32)
        CB = _sb(ctx, nc, "CB", [128, 2], F32)
        qsq = _sb(ctx, nc, "qsq", [128, 512], F32)
        st8 = _sb(ctx, nc, "st8", [128, 8], F32)
        s_w = env["s_w"]
        s_wp = env["s_wp"]
        s_ld = env["s_ld"]
        s_le = env["s_le"]
        s_st = env["s_st"]

        xctx = ExitStack()
        XH = xctx.enter_context(nc.sbuf_tensor("XH", [128, NCH * T], BF16))
        actxA = ExitStack()
        xb0 = actxA.enter_context(nc.sbuf_tensor("xb0", [128, D], F32))
        xb1 = actxA.enter_context(nc.sbuf_tensor("xb1", [128, D], F32))
        sq = actxA.enter_context(nc.sbuf_tensor("sqA", [128, D], F32))
        xhbt = actxA.enter_context(nc.sbuf_tensor("xhbt", [128, D], BF16))
        xbufs = [xb0, xb1]
        lds = [s_ld, s_le]
        WPs = [WP0, WP0]

        def emit(_):
            v, sc, t, sy, g = P.v, P.s, P.t, P.sy, P.g
            P.dma(g, IDB[:], idb.ap(), s_w)
            P.dma(g, IDF[:], idf.ap(), s_w)
            P.dma(g, MLO[:], mloc.ap(), s_w)
            P.dma(g, MUP[:], mupc.ap(), s_w)
            P.dma(g, ONES2[:], ones2c.ap(), s_w)
            P.dma(g, SEL65[:], sel65c.ap(), s_w)
            P.dma(g, CSEL[:], cselc.ap(), s_w)
            P.dma(g, ONESR[:], onesrc.ap(), s_w)
            for c in range(NCH):
                P.dma(g, BO[:, c:c + 1], vin[c * 128:(c + 1) * 128, :], s_w)
            vt_ini = P.go(lambda: g.memset(VT[:], 1.0))
            W_ALL = P.cnt[id(s_w)]

            xh_ev = [None] * NT
            ev_ev = [None] * NT
            vst_ev = [None] * NT
            tp_ev = [None] * NT
            sqA_ev = [None] * NT
            with ExitStack() as actx:
                tpA = _ps(actx, nc, "tpA", [128, 1024], BF16)
                tpB = _ps(actx, nc, "tpB", [128, 1024], BF16)
                P.wd(v, s_w, W_ALL)
                P.vo(lambda: v.memset(st8[:, 7:8], EPS_LN))
                for i in range(NT):
                    xb = xbufs[i % 2]
                    if i >= 2:
                        P.w(sy, xh_ev[i - 2])
                        P.w(sy, sqA_ev[i - 2])
                    ld_ev = P.dma(sy, xb[:], x[i * 128:(i + 1) * 128, :], lds[i % 2])
                    P.w(sc, ld_ev)
                    if i >= 1:
                        P.w(sc, vst_ev[i - 1])
                    sqA_ev[i] = P.so(lambda: sc.activation(out=sq[:], in_=xb[:],
                                                           func=AF.Square))
                    P.w(v, ld_ev)
                    P.w(v, sqA_ev[i])
                    P.vo(lambda: v.reduce_sum(st8[:, 0:1], xb[:], axis=AX.X))
                    P.vo(lambda: v.reduce_sum(st8[:, 1:2], sq[:], axis=AX.X))
                    P.vo(lambda: v.tensor_scalar_mul(st8[:, 2:3], st8[:, 0:1], -1.0 / D))
                    P.vo(lambda: v.tensor_scalar_mul(st8[:, 3:4], st8[:, 1:2], 1.0 / D))
                    P.vo(lambda: v.tensor_mul(st8[:, 4:5], st8[:, 2:3], st8[:, 2:3]))
                    vst_ev[i] = P.vo(lambda: v.tensor_sub(st8[:, 3:4], st8[:, 3:4],
                                                          st8[:, 4:5]))
                    P.w(sc, vst_ev[i])
                    sd_ev = P.so(lambda: sc.activation(out=st8[:, 5:6], in_=st8[:, 3:4],
                                                       func=AF.Sqrt, bias=st8[:, 7:8]))
                    P.w(v, sd_ev)
                    P.vo(lambda: v.reciprocal(st8[:, 6:7], st8[:, 5:6]))
                    if i >= 2:
                        P.w(v, tp_ev[i - 1])
                    xh_ev[i] = P.vo(lambda: v.tensor_scalar(
                        out=xhbt[:], in0=xb[:], scalar1=st8[:, 2:3],
                        scalar2=st8[:, 6:7], op0=ALU.add, op1=ALU.mult))
                    tp = tpA if i % 2 == 0 else tpB
                    P.w(t, xh_ev[i])
                    if i == 0:
                        P.wd(t, s_w, W_ALL)
                    if i >= 2:
                        P.w(t, ev_ev[i - 2])
                    ti = None
                    for c in range(NCH):
                        ti = t.transpose(tp[:, c * 128:(c + 1) * 128],
                                         xhbt[:, c * 128:(c + 1) * 128], IDB[:])
                    tp_ev[i] = P.tlast(ti)
                    P.w(v, tp_ev[i])
                    for c in range(NCH):
                        ev_ev[i] = P.vo(lambda c=c: v.tensor_copy(
                            out=XH[:, c * T + i * 128: c * T + (i + 1) * 128],
                            in_=tp[:, c * 128:(c + 1) * 128]))
            xh_done = ev_ev[NT - 1]
            P.fence(extra=[(s_ld, P.cnt.get(id(s_ld), 0)), (s_le, P.cnt.get(id(s_le), 0)),
                           (s_w, P.cnt.get(id(s_w), 0))])
            actxA.close()

            wp_sems = [env["s_wp0"]]
            wp_tot = [0]
            proj_done = [None] * 8

            def load_pair_weights(hp):
                if hp >= 1 and proj_done[hp - 1] is not None:
                    P.w(g, proj_done[hp - 1])
                for mi in range(3):
                    P.dma(g, WPs[0][:, mi * D:(mi + 1) * D],
                          wqkv3[(mi * 8 + hp) * 128:(mi * 8 + hp + 1) * 128, :],
                          wp_sems[0])
                wp_tot[0] = P.cnt[id(wp_sems[0])]
                return wp_tot[0]

            pair_last_exp = [None]
            prev_pair_done = [None]

            with ExitStack() as bctx:
                pQ0 = _ps(bctx, nc, "pQ0", [128, 512], F32)
                pQ1 = _ps(bctx, nc, "pQ1", [128, 512], F32)
                pS0 = _ps(bctx, nc, "pS0", [128, 512], F32)
                pS1 = _ps(bctx, nc, "pS1", [128, 512], F32)
                pAV = [_ps(bctx, nc, f"pAV{i}", [65, 512], F32) for i in range(4)]
                pC = pAV[0]

                for hp in range(8):
                    slot = 0
                    wpt = load_pair_weights(hp)
                    WPt = WPs[0]
                    R = R_PAIR[hp]
                    evac_ev = [None] * 24
                    cst_ev = {}
                    cst_last = None
                    qsq_last_mm = None
                    for mi, DF in enumerate((QF, KF, VF)):
                        for trg in range(8):
                            kk = mi * 8 + trg
                            pQ = pQ0 if kk % 2 == 0 else pQ1
                            if kk == 0:
                                P.wd(t, wp_sems[slot], wpt)
                                if hp == 0:
                                    P.w(t, xh_done)
                                if prev_pair_done[0] is not None:
                                    P.w(t, prev_pair_done[0])
                                if pair_last_exp[0] is not None:
                                    P.w(t, pair_last_exp[0])
                            if kk >= 2:
                                P.w(t, evac_ev[kk - 2])
                            ti = None
                            for c in range(NCH):
                                ti = t.matmul(pQ[:],
                                              WPt[:, mi * D + c * 128: mi * D + (c + 1) * 128],
                                              XH[:, c * T + trg * 512: c * T + (trg + 1) * 512],
                                              start=(c == 0), stop=(c == NCH - 1))
                            mm_ev = P.tlast(ti)
                            P.w(sc, mm_ev)
                            evac_ev[kk] = P.so(lambda: sc.activation(
                                out=DF[:, trg * 512:(trg + 1) * 512], in_=pQ[:],
                                func=AF.Identity))
                            if mi < 2:
                                if qsq_last_mm is not None:
                                    P.w(sc, qsq_last_mm)
                                sq_ev = P.so(lambda: sc.activation(
                                    out=qsq[:], in_=pQ[:], func=AF.Square))
                                P.w(t, sq_ev)
                                if cst_last is not None:
                                    P.w(t, cst_last)
                                ti = t.matmul(pC[0:2, :], ONES2[:], qsq[:],
                                              start=True, stop=True)
                                cm_ev = P.tlast(ti)
                                qsq_last_mm = cm_ev
                                P.w(v, cm_ev)
                                P.vo(lambda: v.reduce_max(CST[:, 4:5], pC[0:2, :],
                                                          axis=AX.X))
                                if trg == 0:
                                    cst_ev[mi] = P.vo(lambda: v.tensor_copy(
                                        out=CST[:, mi:mi + 1], in_=CST[:, 4:5]))
                                else:
                                    cst_ev[mi] = P.vo(lambda: v.tensor_max(
                                        CST[:, mi:mi + 1], CST[:, mi:mi + 1],
                                        CST[:, 4:5]))
                                cst_last = cst_ev[mi]
                    proj_done[hp] = evac_ev[23]

                    P.w(sc, cst_ev[1])
                    P.so(lambda: sc.activation(out=CST[:, 2:3], in_=CST[:, 0:1],
                                               func=AF.Sqrt))
                    sd_ev = P.so(lambda: sc.activation(out=CST[:, 3:4], in_=CST[:, 1:2],
                                                       func=AF.Sqrt))
                    P.w(v, sd_ev)
                    P.vo(lambda: v.tensor_mul(CST[:, 5:6], CST[:, 2:3], CST[:, 3:4]))
                    c2_ev = P.vo(lambda: v.tensor_scalar_mul(CST[:, 5:6], CST[:, 5:6],
                                                             0.125))
                    P.w(t, c2_ev)
                    t.matmul(pS0[:, 0:1], CSEL[:, 0:128], CST[:, 5:6],
                             start=True, stop=True)
                    ti = t.matmul(pS0[:, 1:2], CSEL[:, 128:256], CST[:, 5:6],
                                  start=True, stop=True)
                    cb_ev = P.tlast(ti)
                    P.w(v, cb_ev)
                    cbv_ev = P.vo(lambda: v.tensor_copy(out=CB[:], in_=pS0[:, 0:2]))
                    plo, phi = pairw[hp]
                    if pair_last_exp[0] is not None:
                        P.w(g, pair_last_exp[0])
                    bp_ev = P.dma_serial(g, BIASP[:, 0:phi - plo],
                                         biasc.ap()[:, plo:phi], s_w)
                    if pair_last_exp[0] is not None:
                        P.w(v, pair_last_exp[0])
                    P.wd(v, s_w, bp_ev[1])
                    bd_ev = cbv_ev
                    for hh in range(2):
                        h = hp * 2 + hh
                        lo, hi = hspan[h]
                        bd_ev = P.vo(lambda lo=lo, hi=hi, hh=hh: v.tensor_scalar(
                            out=BD[:, lo - plo:hi - plo], in0=BIASP[:, lo - plo:hi - plo],
                            scalar1=CB[:, hh:hh + 1], scalar2=None, op0=ALU.subtract))
                    ab_last = None
                    for hh in range(2):
                        h = hp * 2 + hh
                        if prev_pair_done[0] is not None:
                            P.w(g, prev_pair_done[0])
                        if ab_last is not None:
                            P.w(g, ab_last)
                        rw0 = P.dma_serial(g, CMB[hh][0:1, 0:512],
                                           rowsc.ap()[0:1, h * 1024: h * 1024 + 512], s_w)
                        rw1 = P.dma_serial(g, CM2[hh][0:1, 0:512],
                                           rowsc.ap()[0:1, h * 1024 + 512: (h + 1) * 1024], s_w)
                        for si in range(2):
                            if ab_last is not None:
                                P.w(t, ab_last)
                            P.wd(t, s_w, rw1[1])
                            src_row = CMB[hh] if si == 0 else CM2[hh]
                            ti = t.matmul(pS1[:, 0:R], ONESR[:],
                                          src_row[0:1, 0:R],
                                          start=True, stop=True)
                            rb_ev = P.tlast(ti)
                            P.w(v, rb_ev)
                            ab_last = P.vo(lambda hh=hh, si=si: v.tensor_copy(
                                out=AB[hh * 2 + si][:, 0:R], in_=pS1[:, 0:R]))
                    vt_ev = [None] * NT
                    for i2 in range(NT):
                        pT = pS0 if i2 % 2 == 0 else pS1
                        if i2 == 0:
                            P.w(t, evac_ev[23])
                            P.w(t, cbv_ev)
                        if i2 == 1:
                            P.w(t, ab_last)
                        if i2 >= 2:
                            P.w(t, vt_ev[i2 - 2])
                        ti = t.matmul(pT[:, 0:128], VF[:, i2 * 128:(i2 + 1) * 128],
                                      IDB[:], start=True, stop=True)
                        tt_ev = P.tlast(ti)
                        P.w(v, tt_ev)
                        if hp == 0 and i2 == 0:
                            P.w(v, vt_ini)
                        P.vo(lambda: v.tensor_copy(
                            out=VT[:, i2 * 130: i2 * 130 + 64], in_=pT[:, 0:64]))
                        vt_ev[i2] = P.vo(lambda: v.tensor_copy(
                            out=VT[:, i2 * 130 + 65: i2 * 130 + 129], in_=pT[:, 64:128]))
                    vt_done = vt_ev[NT - 1]

                    last_fin = None
                    last_exp = None
                    for hh in range(2):
                        h = hp * 2 + hh
                        rbase = 64 * hh
                        spool = [pS0, pQ0] if hh == 0 else [pS1, pQ1]
                        sfree = [[vt_done], [proj_done[hp]]]
                        sidx = 0
                        pAVlo = pAV[hh * 2]
                        pAVup = pAV[hh * 2 + 1]
                        comb = CMB[hh]
                        comb2 = CM2[hh]
                        prev_fin = None
                        for ri in range(T // R):
                            q0 = ri * R
                            items = plan[(h, ri)]
                            n_it = len(items)
                            n_lo = sum(1 for it in items if it[1] == "lo")
                            n_up = n_it - n_lo
                            base = sidx
                            sidx += n_it + 1
                            ilo = iup = 0
                            avlo_ev = avup_ev = None
                            prev_exp = None
                            pb_av = [None, None]
                            smm = [None] * n_it

                            def emit_S(i, q0=q0):
                                c0 = items[i][0]
                                sl = (base + i) % 2
                                pS_ = spool[sl]
                                for evf in sfree[sl]:
                                    P.w(t, evf)
                                sfree[sl] = []
                                ti = t.matmul(pS_[:, 0:R],
                                              KF[rbase:rbase + 64, c0:c0 + 128],
                                              QF[rbase:rbase + 64, q0:q0 + R],
                                              start=True, stop=True)
                                smm[i] = P.tlast(ti)

                            emit_S(0)
                            if ri == 0:
                                P.w(sc, bd_ev)
                            for i, (c0, side, bcol, j) in enumerate(items):
                                if i + 1 < n_it:
                                    emit_S(i + 1)
                                sl = (base + i) % 2
                                pS_ = spool[sl]
                                pbslot = i % 2
                                pb = PB[hh * 2 + pbslot]
                                P.w(sc, smm[i])
                                if pb_av[pbslot] is not None:
                                    P.w(sc, pb_av[pbslot])
                                e_ev = P.so(lambda: sc.activation(
                                    out=pb[:, 0:R], in_=pS_[:, 0:R], func=AF.Exp,
                                    bias=BD[:, bcol - pairw[hp][0]:bcol - pairw[hp][0] + 1], scale=0.125))
                                sfree[sl].append(e_ev)
                                prev_exp = e_ev
                                mask_ev = e_ev
                                if j is not None:
                                    M = MLO if side == "lo" else MUP
                                    P.w(v, e_ev)
                                    mask_ev = P.vo(lambda: v.tensor_mul(
                                        pb[:, 0:R], pb[:, 0:R],
                                        M[:, 512 - 128 * j: 512 - 128 * j + R]))
                                P.w(t, mask_ev)
                                if side == "lo":
                                    first = ilo == 0
                                    ilo += 1
                                    if first and prev_fin is not None:
                                        P.w(t, prev_fin)
                                    ti = t.matmul(pAVlo[:, 0:R],
                                                  VT[:, (c0 // 128) * 130 + 65 * hh:
                                                     (c0 // 128) * 130 + 65 * hh + 65],
                                                  pb[:, 0:R], start=first,
                                                  stop=(ilo == n_lo),
                                                  skip_group_check=True)
                                    avlo_ev = P.tlast(ti)
                                    pb_av[pbslot] = avlo_ev
                                else:
                                    first = iup == 0
                                    iup += 1
                                    if first and prev_fin is not None:
                                        P.w(t, prev_fin)
                                    ti = t.matmul(pAVup[:, 0:R],
                                                  VT[:, (c0 // 128) * 130 + 65 * hh:
                                                     (c0 // 128) * 130 + 65 * hh + 65],
                                                  pb[:, 0:R], start=first,
                                                  stop=(iup == n_up),
                                                  skip_group_check=True)
                                    avup_ev = P.tlast(ti)
                                    pb_av[pbslot] = avup_ev
                            last_exp = prev_exp
                            P.w(v, avlo_ev)
                            P.vo(lambda: v.tensor_mul(comb2[:, 0:R], pAVlo[:, 0:R],
                                                      AB[hh * 2][0:65, 0:R]))
                            if avup_ev is not None:
                                P.w(v, avup_ev)
                                P.vo(lambda: v.tensor_mul(comb[:, 0:R], pAVup[:, 0:R],
                                                          AB[hh * 2 + 1][0:65, 0:R]))
                                P.vo(lambda: v.tensor_add(comb[:, 0:R], comb2[:, 0:R],
                                                          comb[:, 0:R]))
                            else:
                                P.vo(lambda: v.tensor_copy(out=comb[:, 0:R],
                                                           in_=comb2[:, 0:R]))
                            rc_ev = P.vo(lambda: v.reciprocal(comb[64:65, 0:R],
                                                              comb[64:65, 0:R]))
                            rbslot = (base + n_it) % 2
                            pS_ = spool[rbslot]
                            P.w(t, rc_ev)
                            for evf in sfree[rbslot]:
                                P.w(t, evf)
                            sfree[rbslot] = []
                            ti = t.matmul(pS_[0:64, 0:R], SEL65[:], comb[:, 0:R],
                                          start=True, stop=True)
                            rbc_ev = P.tlast(ti)
                            P.w(v, rbc_ev)
                            last_fin = P.vo(lambda pS_=pS_: v.tensor_mul(
                                ATT[rbase:rbase + 64, hp * T + q0: hp * T + q0 + R],
                                comb[0:64, 0:R], pS_[0:64, 0:R]))
                            sfree[rbslot].append(last_fin)
                            prev_fin = last_fin
                    pair_last_exp[0] = last_exp
                    prev_pair_done[0] = last_fin

            att_all = prev_pair_done[0]
            P.fence(extra=[(s_w, P.cnt.get(id(s_w), 0)),
                           (wp_sems[0], P.cnt.get(id(wp_sems[0]), 0))])
            xctx.close()
            WOS = ctx.enter_context(nc.sbuf_tensor("WOS", [128, NCH * D], BF16))
            OUTF = ctx.enter_context(nc.sbuf_tensor("OUTF", [128, NCH * 512], F32))
            yt0 = ctx.enter_context(nc.sbuf_tensor("yt0", [128, D], F32))
            xbC = [ctx.enter_context(nc.sbuf_tensor("xc0", [128, D], F32)),
                   ctx.enter_context(nc.sbuf_tensor("xc1", [128, D], F32))]
            sqC = ctx.enter_context(nc.sbuf_tensor("sqC", [128, D], F32))
            first = True
            for c in range(NCH):
                if first:
                    P.dma_serial(g, WOS[:, c * D:(c + 1) * D],
                                 wo[c * 128:(c + 1) * 128, :], s_w)
                    first = False
                else:
                    P.dma(g, WOS[:, c * D:(c + 1) * D], wo[c * 128:(c + 1) * 128, :], s_w)
            WOS_ALL = P.cnt[id(s_w)]
            of_ev = [None] * (8 * NCH)
            ty_ev = [None] * NT
            ev2_ev = [None] * NT
            fin_ev = [None] * NT
            vst2_ev = [None] * NT
            stb1 = P.cnt.get(id(s_st), 0)
            with ExitStack() as cctx:
                pU0 = _ps(cctx, nc, "pU0c", [128, 512], F32)
                pU1 = _ps(cctx, nc, "pU1c", [128, 512], F32)
                pT0 = _ps(cctx, nc, "pT0c", [128, 512], F32)
                pT1 = _ps(cctx, nc, "pT1c", [128, 512], F32)
                pV0 = _ps(cctx, nc, "pV0c", [128, 512], F32)
                pV1 = _ps(cctx, nc, "pV1c", [128, 512], F32)
                pUs2 = [pU0, pU1]
                for tr in range(8):
                    add_ev = None
                    for do in range(NCH):
                        k = tr * NCH + do
                        pU = pUs2[k % 2]
                        if k == 0:
                            P.w(t, att_all)
                            P.wd(t, s_w, WOS_ALL)
                        if k >= 2:
                            P.w(t, of_ev[k - 2])
                        ti = None
                        for c in range(NCH):
                            ti = t.matmul(pU[:], WOS[:, c * D + do * 128: c * D + (do + 1) * 128],
                                          ATT[:, c * T + tr * 512: c * T + (tr + 1) * 512],
                                          start=(c == 0), stop=(c == NCH - 1))
                        mm_ev = P.tlast(ti)
                        P.w(sc, mm_ev)
                        if tr > 0:
                            P.w(sc, ty_ev[tr * 4 - 1])
                        of_ev[k] = P.so(lambda: sc.activation(
                            out=OUTF[:, do * 512:(do + 1) * 512], in_=pU[:],
                            func=AF.Identity, bias=BO[:, do:do + 1]))
                        add_ev = of_ev[k]
                    for i in range(4):
                        gidx = tr * 4 + i
                        pa, pb2 = (pT0, pT1) if gidx % 2 == 0 else (pV0, pV1)
                        P.w(t, add_ev)
                        if gidx >= 2:
                            P.w(t, ev2_ev[gidx - 2])
                        ti = None
                        for do in range(NCH):
                            src = OUTF[:, do * 512 + i * 128: do * 512 + (i + 1) * 128]
                            dst = (pa if do < 4 else pb2)[:, (do % 4) * 128:(do % 4 + 1) * 128]
                            ti = t.transpose(dst, src, IDF[:])
                        ty_ev[gidx] = P.tlast(ti)
                        if gidx >= 2:
                            P.w(sy, fin_ev[gidx - 2])
                        ld_ev = P.dma(sy, xbC[gidx % 2][:],
                                      x[gidx * 128:(gidx + 1) * 128, :], lds[gidx % 2])
                        P.w(v, ty_ev[gidx])
                        if gidx >= 1:
                            P.wd(v, s_st, stb1 + 16 * gidx)
                        P.vo(lambda: v.tensor_copy(out=yt0[:, 0:512], in_=pa[:]))
                        ev2_ev[gidx] = P.vo(lambda: v.tensor_copy(out=yt0[:, 512:1024],
                                                                  in_=pb2[:]))
                        P.w(sc, ev2_ev[gidx])
                        if gidx >= 1:
                            P.w(sc, vst2_ev[gidx - 1])
                        sq_ev = P.so(lambda: sc.activation(out=sqC[:], in_=yt0[:],
                                                           func=AF.Square))
                        P.w(v, sq_ev)
                        P.vo(lambda: v.reduce_sum(st8[:, 0:1], yt0[:], axis=AX.X))
                        P.vo(lambda: v.reduce_sum(st8[:, 1:2], sqC[:], axis=AX.X))
                        P.vo(lambda: v.tensor_scalar_mul(st8[:, 2:3], st8[:, 0:1], -1.0 / D))
                        P.vo(lambda: v.tensor_scalar_mul(st8[:, 3:4], st8[:, 1:2], 1.0 / D))
                        P.vo(lambda: v.tensor_mul(st8[:, 4:5], st8[:, 2:3], st8[:, 2:3]))
                        vst2_ev[gidx] = P.vo(lambda: v.tensor_sub(st8[:, 3:4], st8[:, 3:4],
                                                                  st8[:, 4:5]))
                        P.w(sc, vst2_ev[gidx])
                        sd_ev = P.so(lambda: sc.activation(out=st8[:, 5:6], in_=st8[:, 3:4],
                                                           func=AF.Sqrt, bias=st8[:, 7:8]))
                        P.w(v, sd_ev)
                        P.vo(lambda: v.reciprocal(st8[:, 6:7], st8[:, 5:6]))
                        P.vo(lambda: v.tensor_scalar(out=yt0[:], in0=yt0[:],
                                                     scalar1=st8[:, 2:3],
                                                     scalar2=st8[:, 6:7],
                                                     op0=ALU.add, op1=ALU.mult))
                        P.w(v, ld_ev)
                        fin_ev[gidx] = P.vo(lambda: v.tensor_add(yt0[:], xbC[gidx % 2][:],
                                                                 yt0[:]))
                        P.w(sy, fin_ev[gidx])
                        P.dma(sy, y[gidx * 128:(gidx + 1) * 128, :], yt0[:], s_st)
                P.wd(sy, s_st, P.cnt[id(s_st)])
        emit(None)
        _stage_fence(P, env)


def _stage1_inputs(inputs):
    def pack(name):
        w_t = np.asarray(inputs[name]).T.astype(np.float32)
        return w_t.reshape(NCH, 128, 8, 128).transpose(2, 1, 0, 3).reshape(
            8 * 128, D)
    wqkv3 = np.ascontiguousarray(np.concatenate(
        [pack("L0_attn_wq"), pack("L0_attn_wk"), pack("L0_attn_wv")], axis=0))
    wo_t = np.ascontiguousarray(np.asarray(inputs["L0_attn_wo"]).T).astype(np.float32)
    vecs = np.asarray(inputs["L0_attn_bo"], np.float32).reshape(D, 1)
    return {"s1_wqkv": wqkv3, "s1_wo": wo_t, "s1_vecs": vecs}



HID2 = 2816
NF2 = HID2 // 128


def _rotary_tables():
    DH, WIN, N = 64, 512, T
    half = np.arange(0, DH, 2, dtype=np.float64)
    inv = 1.0 / 10000.0 ** (half / DH)
    tt = np.arange(N, dtype=np.float64)
    fr = tt[:, None] * inv[None, :]
    fr = np.concatenate([fr, fr], axis=1)
    base = (half + 0.4 * DH) / (1.4 * DH)
    power = (tt - N // 2) / float(WIN // 2)
    scl = base[None, :] ** power[:, None]
    scl = np.concatenate([scl, scl], axis=1)
    def dup(a):
        return np.ascontiguousarray(np.concatenate([a, a], axis=0))
    cosq = dup((np.cos(fr) * scl).T.astype(BF16NP))
    sinq = dup((np.sin(fr) * scl).T.astype(BF16NP))
    cosk = dup((np.cos(fr) / scl).T.astype(BF16NP))
    sink = dup((np.sin(fr) / scl).T.astype(BF16NP))
    perm = np.zeros((64, 64), np.float32)
    for d in range(32):
        perm[d + 32, d] = -1.0
        perm[d, d + 32] = 1.0
    permc = np.zeros((128, 128), BF16NP)
    permc[:64, :64] = perm
    permc[64:, 64:] = perm
    return cosq, sinq, cosk, sink, permc


def emit_stage2(nc, P, env):
    x = env["x1"]
    wqkv = env["s2_wqkvp"]
    wol = env["s2_wol"]
    w1t = env["s2_w1t"]
    w2t = env["s2_w2t"]
    vec3 = env["s2_vec3"]
    qsks = env["s2_qsks"]
    b1p = env["s2_b1p"]
    y = env["x2"]
    xi = env["s2_xi"]
    idb = env["idb"]
    idf = env["idf"]
    cosqc = env["s2_cosqc"]
    sinqc = env["s2_sinqc"]
    coskc = env["s2_coskc"]
    sinkc = env["s2_sinkc"]
    permc = env["s2_permc"]
    amgec = env["s2_amgec"]
    amlec = env["s2_amlec"]
    ones2c = env["ones2c"]
    sel65c = env["sel65c"]
    cselc = env["cselc"]
    sh1c = env["s2_sh1c"]
    e0c = env["s2_e0c"]
    one1c = env["s2_one1c"]

    TP1 = T + 1

    ctx = ExitStack()
    with ctx:
        XIB = _sb(ctx, nc, "XIB", [128, NCH * TP1], BF16)
        IDB = _sb(ctx, nc, "IDB", [128, 128], BF16)
        IDF = _sb(ctx, nc, "IDF", [128, 128], F32)
        SH1 = _sb(ctx, nc, "SH1", [128, 128], F32)
        E0 = _sb(ctx, nc, "E0", [1, 128], F32)
        ONE1 = _sb(ctx, nc, "ONE1", [1, 128], F32)
        PERM = _sb(ctx, nc, "PERM", [128, 128], BF16)
        ONES2 = _sb(ctx, nc, "ONES2", [128, 2], F32)
        SEL65 = _sb(ctx, nc, "SEL65", [65, 64], F32)
        CSEL = _sb(ctx, nc, "CSEL", [2, 256], F32)
        VEC = _sb(ctx, nc, "VEC", [128, 3 * NCH], F32)
        B1A = _sb(ctx, nc, "B1A", [128, NF2], F32)
        B1G = _sb(ctx, nc, "B1G", [128, NF2], F32)
        QS2 = _sb(ctx, nc, "QS2", [2, 256], F32)
        CST = _sb(ctx, nc, "CST", [2, 72], F32)
        CB = _sb(ctx, nc, "CB", [128, 2], F32)
        st8 = _sb(ctx, nc, "st8", [128, 8], F32)
        xb0 = _sb(ctx, nc, "xb0", [128, D], F32)
        xb1 = _sb(ctx, nc, "xb1", [128, D], F32)
        sq = _sb(ctx, nc, "sq", [128, D], F32)
        asm = _sb(ctx, nc, "asm", [128, D], F32)
        rowb = _sb(ctx, nc, "rowb", [1, 512], F32)
        xhbt = _sb(ctx, nc, "xhbt", [128, D], BF16)
        yt0 = _sb(ctx, nc, "yt0", [128, D], F32)
        s_w = env["s_w"]
        s_ld = env["s_ld"]
        s_le = env["s_le"]
        s_st = env["s_st"]
        s_sr = env["s_sr"]
        s_sx = env["s_sx"]
        s_wp = env["s_wp"]
        s_wf = env["s_wf"]
        s_wg = env["s_wg"]

        xbufs = [xb0, xb1]
        lds = [s_ld, s_le]

        def emit(_):
            v, sc, t, sy, g = P.v, P.s, P.t, P.sy, P.g

            def ln_stats(src, sqbuf):
                if "v" in P.last:
                    P.w(sc, P.last["v"])
                sq_ev = P.so(lambda: sc.activation(out=sqbuf[:], in_=src[:],
                                                   func=AF.Square))
                P.w(v, sq_ev)
                P.vo(lambda: v.reduce_sum(st8[:, 0:1], src[:], axis=AX.X))
                P.vo(lambda: v.reduce_sum(st8[:, 1:2], sqbuf[:], axis=AX.X))
                P.vo(lambda: v.tensor_scalar_mul(st8[:, 2:3], st8[:, 0:1], -1.0 / D))
                P.vo(lambda: v.tensor_scalar_mul(st8[:, 3:4], st8[:, 1:2], 1.0 / D))
                P.vo(lambda: v.tensor_mul(st8[:, 4:5], st8[:, 2:3], st8[:, 2:3]))
                vst = P.vo(lambda: v.tensor_sub(st8[:, 3:4], st8[:, 3:4], st8[:, 4:5]))
                P.w(sc, vst)
                sd = P.so(lambda: sc.activation(out=st8[:, 5:6], in_=st8[:, 3:4],
                                                func=AF.Sqrt, bias=st8[:, 7:8]))
                P.w(v, sd)
                P.vo(lambda: v.reciprocal(st8[:, 6:7], st8[:, 5:6]))
                return vst, sq_ev

            for dst, src in ((IDB, idb), (IDF, idf), (SH1, sh1c), (E0, e0c),
                             (ONE1, one1c), (PERM, permc), (ONES2, ones2c),
                             (SEL65, sel65c), (CSEL, cselc)):
                P.dma(g, dst[:], src.ap(), s_w)
            W_ALL0 = P.cnt[id(s_w)]
            P.wd(v, s_w, W_ALL0)
            P.vo(lambda: v.memset(st8[:, 7:8], EPS_LN))
            gz_ev = P.go(lambda: g.memset(XIB[:], 0.0))

            xh_ev = [None] * NT
            ev_ev = [None] * NT
            vst_ev = [None] * NT
            tp_ev = [None] * NT
            sqA_ev = [None] * NT
            stb2 = P.cnt.get(id(s_st), 0)
            with ExitStack() as actx:
                tpA = _ps(actx, nc, "tpA", [128, 1024], BF16)
                tpB = _ps(actx, nc, "tpB", [128, 1024], BF16)
                for i in range(NT):
                    xb = xbufs[i % 2]
                    if i >= 2:
                        P.w(sy, xh_ev[i - 2])
                        P.w(sy, sqA_ev[i - 2])
                    ld_ev = P.dma(sy, xb[:], x[i * 128:(i + 1) * 128, :], lds[i % 2])
                    P.w(sc, ld_ev)
                    if i >= 1:
                        P.w(sc, vst_ev[i - 1])
                    P.w(v, ld_ev)
                    vst_ev[i], sqA_ev[i] = ln_stats(xb, sq)
                    if i >= 2:
                        P.w(v, tp_ev[i - 1])
                    if i >= 1:
                        P.wd(v, s_st, stb2 + 16 * i)
                    P.vo(lambda: v.tensor_scalar(out=yt0[:], in0=xb[:],
                                                 scalar1=st8[:, 2:3],
                                                 scalar2=st8[:, 6:7],
                                                 op0=ALU.add, op1=ALU.mult))
                    xh_ev[i] = P.vo(lambda: v.tensor_copy(out=xhbt[:], in_=yt0[:]))
                    P.w(sy, xh_ev[i])
                    P.dma(sy, xi[i * 128:(i + 1) * 128, :], yt0[:], s_st)
                    tp = tpA if i % 2 == 0 else tpB
                    P.w(t, xh_ev[i])
                    if i == 0:
                        P.wd(t, s_w, W_ALL0)
                        P.w(t, gz_ev)
                    if i >= 2:
                        P.w(t, ev_ev[i - 2])
                    ti = None
                    for c in range(NCH):
                        ti = t.transpose(tp[:, c * 128:(c + 1) * 128],
                                         xhbt[:, c * 128:(c + 1) * 128], IDB[:])
                    tp_ev[i] = P.tlast(ti)
                    P.w(v, tp_ev[i])
                    for c in range(NCH):
                        ev_ev[i] = P.vo(lambda c=c: v.tensor_copy(
                            out=XIB[:, c * TP1 + 1 + i * 128: c * TP1 + 1 + (i + 1) * 128],
                            in_=tp[:, c * 128:(c + 1) * 128]))
            P.fence(extra=[(s_ld, P.cnt.get(id(s_ld), 0)), (s_le, P.cnt.get(id(s_le), 0)),
                           (s_st, P.cnt.get(id(s_st), 0)), (s_w, P.cnt.get(id(s_w), 0))])

            for l in range(2):
                for vi in range(3):
                    P.dma_serial(g, VEC[:, vi * NCH:(vi + 1) * NCH],
                                 vec3[(l * 3 + vi) * 128:(l * 3 + vi + 1) * 128, :],
                                 s_w)
                P.dma_serial(g, B1A[:], b1p[(l * 2 + 0) * 128:(l * 2 + 1) * 128, :],
                             s_w)
                P.dma_serial(g, B1G[:], b1p[(l * 2 + 1) * 128:(l * 2 + 2) * 128, :],
                             s_w)
                qz_ev = P.go(lambda: g.memset(QS2[:], 0.0))
                P.w(g, qz_ev)
                P.dma_serial(g, QS2[0:1, 0:64],
                             qsks[(l * 2 + 0) * 64:(l * 2 + 0) * 64 + 64, :].rearrange("a b -> b a"), s_w)
                P.dma_serial(g, QS2[1:2, 64:128],
                             qsks[(l * 2 + 0) * 64:(l * 2 + 0) * 64 + 64, :].rearrange("a b -> b a"), s_w)
                P.dma_serial(g, QS2[0:1, 128:192],
                             qsks[(l * 2 + 1) * 64:(l * 2 + 1) * 64 + 64, :].rearrange("a b -> b a"), s_w)
                P.dma_serial(g, QS2[1:2, 192:256],
                             qsks[(l * 2 + 1) * 64:(l * 2 + 1) * 64 + 64, :].rearrange("a b -> b a"), s_w)
                vec_all = P.cnt[id(s_w)]

                with ExitStack() as bctx:
                    ATT2 = bctx.enter_context(nc.sbuf_tensor(f"ATT2_{l}", [128, 4 * T], BF16))
                    attctx = ExitStack()
                    QP = attctx.enter_context(nc.sbuf_tensor(f"QP_{l}", [128, T], BF16))
                    KP = attctx.enter_context(nc.sbuf_tensor(f"KP_{l}", [128, T], BF16))
                    VSTG = attctx.enter_context(nc.sbuf_tensor(f"VSTG_{l}", [128, 512], BF16))
                    VT = attctx.enter_context(nc.sbuf_tensor(f"VT_{l}", [128, NT * 130], BF16))
                    QR = attctx.enter_context(nc.sbuf_tensor(f"QR_{l}", [128, 512], BF16))
                    RT = attctx.enter_context(nc.sbuf_tensor(f"RT_{l}", [128, 512], BF16))
                    WP = attctx.enter_context(nc.sbuf_tensor(f"WP_{l}", [128, 3 * D], BF16))
                    PB = [attctx.enter_context(nc.sbuf_tensor(f"PB{i}_{l}", [128, 512], BF16))
                          for i in range(4)]
                    CMB = [attctx.enter_context(nc.sbuf_tensor(f"CMB{i}_{l}", [65, 512], F32))
                           for i in range(2)]
                    qsqb = attctx.enter_context(nc.sbuf_tensor(f"qsqb_{l}", [128, 512], F32))
                    COSQ = attctx.enter_context(nc.sbuf_tensor(f"COSQ_{l}", [128, T], BF16))
                    SINQ = attctx.enter_context(nc.sbuf_tensor(f"SINQ_{l}", [128, T], BF16))
                    COSK = attctx.enter_context(nc.sbuf_tensor(f"COSK_{l}", [128, T], BF16))
                    SINK = attctx.enter_context(nc.sbuf_tensor(f"SINK_{l}", [128, T], BF16))
                    AMGE = attctx.enter_context(nc.sbuf_tensor(f"AMGE_{l}", [128, 1024], BF16))
                    AMLE = attctx.enter_context(nc.sbuf_tensor(f"AMLE_{l}", [128, 1024], BF16))
                    for dst, src in ((COSQ, cosqc), (SINQ, sinqc), (COSK, coskc),
                                     (SINK, sinkc), (AMGE, amgec), (AMLE, amlec)):
                        P.dma_serial(g, dst[:], src.ap(), s_w)
                    tabs_all = P.cnt[id(s_w)]
                    pQ0 = _ps(bctx, nc, f"pQ0_{l}", [128, 512], F32)
                    pQ1 = _ps(bctx, nc, f"pQ1_{l}", [128, 512], F32)
                    pS0 = _ps(bctx, nc, f"pS0_{l}", [128, 512], F32)
                    pS1 = _ps(bctx, nc, f"pS1_{l}", [128, 512], F32)
                    pAV0 = _ps(bctx, nc, f"pAV0_{l}", [65, 512], F32)
                    pAV1 = _ps(bctx, nc, f"pAV1_{l}", [65, 512], F32)
                    pR0 = _ps(bctx, nc, f"pR0_{l}", [128, 512], F32)
                    pR1 = _ps(bctx, nc, f"pR1_{l}", [128, 512], F32)
                    pQs = [pQ0, pQ1]
                    pSs = [pS0, pS1]
                    pAVs = [pAV0, pAV1]

                    vt_ini = P.go(lambda: g.memset(VT[:], 1.0))
                    P.wd(v, s_w, vec_all)
                    P.vo(lambda: v.tensor_mul(CST[0:1, 0:64], QS2[0:1, 0:64],
                                              QS2[0:1, 128:192]))
                    P.vo(lambda: v.reduce_max(CST[0:1, 64:65], CST[0:1, 0:64], axis=AX.X,
                                              apply_absolute_value=True))
                    cmax_ev = P.vo(lambda: v.tensor_scalar_mul(CST[0:1, 64:65],
                                                               CST[0:1, 64:65], -8.0))
                    P.w(t, cmax_ev)
                    if P.last.get("s") is not None:
                        P.w(t, P.last["s"])
                    ti = t.matmul(pR0[:, 0:1], ONE1[:],
                                  CST[0:1, 64:65], start=True, stop=True)
                    cbm_ev = P.tlast(ti)
                    P.w(v, cbm_ev)
                    cbv_ev = P.vo(lambda: v.tensor_copy(out=CB[:, 0:1], in_=pR0[:, 0:1]))
                    prev_pair_fin = [None]
                    pair_last_exp2 = [None]
                    for hp in range(4):
                        if prev_pair_fin[0] is not None:
                            P.w(g, prev_pair_fin[0])
                        for mi in range(3):
                            tile = mi * 4 + hp
                            P.dma_serial(
                                g, WP[:, mi * D:(mi + 1) * D],
                                wqkv[(l * 12 + tile) * 128:
                                     (l * 12 + tile + 1) * 128, :], s_wp)
                        wpt = P.cnt[id(s_wp)]
                        evq = [None] * 24
                        vt_ev = [None] * NT
                        cst_ev = {}
                        cst_last = None
                        qsq_last = None
                        prev_evq = None
                        for mi, DF in enumerate((QP, KP, None)):
                            for trg in range(8):
                                kk = mi * 8 + trg
                                pQ = pQs[kk % 2]
                                if kk == 0:
                                    P.wd(t, s_wp, wpt)
                                    if prev_pair_fin[0] is not None:
                                        P.w(t, prev_pair_fin[0])
                                    if pair_last_exp2[0] is not None:
                                        P.w(t, pair_last_exp2[0])
                                    if hp == 0:
                                        P.w(t, ev_ev[NT - 1])
                                if kk >= 2:
                                    P.w(t, evq[kk - 2])
                                ti = None
                                for c in range(NCH):
                                    off = c * TP1 + (1 if c < 4 else 0)
                                    ti = t.matmul(
                                        pQ[:],
                                        WP[:, mi * D + c * 128: mi * D + (c + 1) * 128],
                                        XIB[:, off + trg * 512: off + (trg + 1) * 512],
                                        start=(c == 0), stop=(c == NCH - 1))
                                mm_ev = P.tlast(ti)
                                if mi == 2:
                                    P.w(sc, mm_ev)
                                    if trg >= 1:
                                        P.w(sc, vt_ev[trg * 4 - 1])
                                    evq[kk] = P.so(lambda: sc.activation(
                                        out=VSTG[:], in_=pQ[:], func=AF.Identity))
                                    for j in range(4):
                                        i2 = trg * 4 + j
                                        pT = pSs[i2 % 2]
                                        P.w(t, evq[kk])
                                        if i2 >= 2:
                                            P.w(t, vt_ev[i2 - 2])
                                        ti = t.matmul(pT[:, 0:128],
                                                      VSTG[:, j * 128:(j + 1) * 128],
                                                      IDB[:], start=True, stop=True)
                                        tt_ev = P.tlast(ti)
                                        P.w(v, tt_ev)
                                        if hp == 0 and i2 == 0:
                                            P.w(v, vt_ini)
                                        P.vo(lambda i2=i2: v.tensor_copy(
                                            out=VT[:, i2 * 130: i2 * 130 + 64],
                                            in_=pT[:, 0:64]))
                                        vt_ev[i2] = P.vo(lambda i2=i2: v.tensor_copy(
                                            out=VT[:, i2 * 130 + 65: i2 * 130 + 129],
                                            in_=pT[:, 64:128]))
                                    continue
                                P.w(sc, mm_ev)
                                if qsq_last is not None:
                                    P.w(sc, qsq_last)
                                sqv = P.so(lambda: sc.activation(out=qsqb[:], in_=pQ[:],
                                                                 func=AF.Square))
                                P.w(t, sqv)
                                if cst_last is not None:
                                    P.w(t, cst_last)
                                ti = t.matmul(pAV0[0:2, :], ONES2[:], qsqb[:],
                                              start=True, stop=True)
                                nm_ev = P.tlast(ti)
                                qsq_last = nm_ev
                                P.w(sc, nm_ev)
                                sd = P.so(lambda: sc.activation(out=qsqb[0:2, 0:512],
                                                                in_=pAV0[0:2, :],
                                                                func=AF.Sqrt))
                                cst_last = sd
                                P.w(v, sd)
                                P.vo(lambda: v.tensor_scalar_max(qsqb[0:2, 0:512],
                                                                 qsqb[0:2, 0:512], 1e-12))
                                rstd_ev = P.vo(lambda: v.reciprocal(qsqb[0:2, 0:512],
                                                                    qsqb[0:2, 0:512]))
                                P.w(t, rstd_ev)
                                if prev_evq is not None:
                                    P.w(t, prev_evq)
                                qoff = 0 if mi == 0 else 128
                                ti = t.matmul(pR0[:], QS2[:, qoff:qoff + 128],
                                              qsqb[0:2, 0:512], start=True, stop=True)
                                bc_ev = P.tlast(ti)
                                P.w(sc, mm_ev)
                                if prev_evq is not None:
                                    P.w(sc, prev_evq)
                                ev_raw = P.so(lambda: sc.activation(out=QR[:], in_=pQ[:],
                                                                    func=AF.Identity))
                                P.w(v, ev_raw)
                                P.w(v, bc_ev)
                                qh_ev = P.vo(lambda: v.tensor_mul(RT[:], QR[:], pR0[:]))
                                P.w(t, qh_ev)
                                ti = t.matmul(pR1[:], PERM[:], RT[:], start=True, stop=True)
                                rot_ev = P.tlast(ti)
                                COS = COSQ if mi == 0 else COSK
                                SIN = SINQ if mi == 0 else SINK
                                P.w(v, rot_ev)
                                P.wd(v, s_w, tabs_all)
                                P.vo(lambda: v.tensor_mul(
                                    QR[:, 0:512], RT[:, 0:512],
                                    COS[:, trg * 512:(trg + 1) * 512]))
                                P.vo(lambda: v.tensor_mul(
                                    RT[:, 0:512], pR1[:, 0:512],
                                    SIN[:, trg * 512:(trg + 1) * 512]))
                                evq[kk] = P.vo(lambda: v.tensor_add(
                                    DF[:, trg * 512:(trg + 1) * 512], QR[:, 0:512],
                                    RT[:, 0:512]))
                                prev_evq = evq[kk]
                        vt_done = vt_ev[NT - 1]

                        last_fin = None
                        for hh in range(2):
                            rbase = 64 * hh
                            spool = [pSs[hh], pQs[hh]]
                            sfree = [[vt_done], [evq[23]]]
                            sidx = 0
                            pAVh = pAVs[hh]
                            comb = CMB[hh]
                            prev_fin = None
                            prev_exp = None
                            pb_av = [None, None]
                            for w in range(8):
                                q0 = w * 512
                                chunks = list(range(0 if w > 0 else 4, 8))
                                nch_w = len(chunks)
                                base = sidx
                                sidx += nch_w + 1
                                av_ev = None
                                smm = [None] * nch_w

                                def emit_S(k, w=w, q0=q0, chunks=chunks):
                                    ci = chunks[k]
                                    c0 = (w - 1) * 512 + ci * 128
                                    sl = (base + k) % 2
                                    pS_ = spool[sl]
                                    if k == 0 and hh == 0 and w == 0:
                                        P.w(t, cbv_ev)
                                    for evf in sfree[sl]:
                                        P.w(t, evf)
                                    sfree[sl] = []
                                    ti = t.matmul(pS_[:, 0:512],
                                                  KP[rbase:rbase + 64, c0:c0 + 128],
                                                  QP[rbase:rbase + 64, q0:q0 + 512],
                                                  start=True, stop=False)
                                    if ci < 4:
                                        M = AMGE[:, 512 - 128 * ci: 1024 - 128 * ci]
                                    else:
                                        M = AMLE[:, 512 - 128 * (ci - 4):
                                                 1024 - 128 * (ci - 4)]
                                    if hh == 0 and w == 0 and k == 0:
                                        P.wd(t, s_w, tabs_all)
                                    ti = t.matmul(pS_[:, 0:512], IDB[:], M,
                                                  start=False, stop=True)
                                    smm[k] = P.tlast(ti)

                                emit_S(0)
                                if hh == 0 and w == 0:
                                    P.w(sc, cbv_ev)
                                for k in range(nch_w):
                                    if k + 1 < nch_w:
                                        emit_S(k + 1)
                                    sl = (base + k) % 2
                                    pS_ = spool[sl]
                                    pbslot = k % 2
                                    pb = PB[hh * 2 + pbslot]
                                    c0 = (w - 1) * 512 + chunks[k] * 128
                                    P.w(sc, smm[k])
                                    if pb_av[pbslot] is not None:
                                        P.w(sc, pb_av[pbslot])
                                    e_ev = P.so(lambda: sc.activation(
                                        out=pb[:, 0:512], in_=pS_[:, 0:512], func=AF.Exp,
                                        bias=CB[:, 0:1], scale=8.0))
                                    sfree[sl].append(e_ev)
                                    prev_exp = e_ev
                                    P.w(t, e_ev)
                                    if k == 0 and prev_fin is not None:
                                        P.w(t, prev_fin)
                                    ti = t.matmul(pAVh[:, 0:512],
                                                  VT[:, (c0 // 128) * 130 + 65 * hh:
                                                     (c0 // 128) * 130 + 65 * hh + 65],
                                                  pb[:, 0:512], start=(k == 0),
                                                  stop=(k == nch_w - 1),
                                                  skip_group_check=True)
                                    av_ev = P.tlast(ti)
                                    pb_av[pbslot] = av_ev
                                P.w(v, av_ev)
                                P.vo(lambda: v.tensor_copy(out=comb[:, 0:512],
                                                           in_=pAVh[:, 0:512]))
                                rc_ev = P.vo(lambda: v.reciprocal(comb[64:65, 0:512],
                                                                  comb[64:65, 0:512]))
                                rbslot = (base + nch_w) % 2
                                pS_ = spool[rbslot]
                                P.w(t, rc_ev)
                                for evf in sfree[rbslot]:
                                    P.w(t, evf)
                                sfree[rbslot] = []
                                ti = t.matmul(pS_[0:64, 0:512], SEL65[:], comb[:, 0:512],
                                              start=True, stop=True)
                                rbc_ev = P.tlast(ti)
                                P.w(v, rbc_ev)
                                last_fin = P.vo(lambda pS_=pS_: v.tensor_mul(
                                    ATT2[rbase:rbase + 64, hp * T + q0: hp * T + q0 + 512],
                                    comb[0:64, 0:512], pS_[0:64, 0:512]))
                                sfree[rbslot].append(last_fin)
                                prev_fin = last_fin
                        prev_pair_fin[0] = last_fin
                        pair_last_exp2[0] = prev_exp
                    att_done = prev_pair_fin[0]
                    P.fence(extra=[(s_wp, P.cnt.get(id(s_wp), 0)),
                                   (s_w, P.cnt.get(id(s_w), 0))])
                    attctx.close()
                    OUTF = bctx.enter_context(nc.sbuf_tensor(f"OUTF_{l}", [128, NCH * 512], F32))
                    WPO = bctx.enter_context(nc.sbuf_tensor(f"WPO_{l}", [128, 4 * D], BF16))

                    for c in range(4):
                        for fo in range(NCH):
                            P.dma_serial(g, WPO[:, c * D + fo * 128: c * D + (fo + 1) * 128],
                                         wol[l * 512 + c * 128: l * 512 + (c + 1) * 128,
                                             fo * 128:(fo + 1) * 128], s_wp)
                    wot = P.cnt[id(s_wp)]
                    of_ev = [None] * (8 * NCH)
                    ty_ev = [None] * NT
                    ev2_ev = [None] * NT
                    fin_ev = [None] * NT
                    vstc_ev = [None] * NT
                    xfe_ev = [None] * NT
                    rb_z = P.vo(lambda: v.memset(rowb[:], 0.0))
                    sh_prev = [None]
                    for tr in range(8):
                        for do in range(NCH):
                            k = tr * NCH + do
                            pU = pQs[k % 2]
                            if k == 0:
                                P.w(t, att_done)
                                if pair_last_exp2[0] is not None:
                                    P.w(t, pair_last_exp2[0])
                                P.wd(t, s_wp, wot)
                            if k >= 2:
                                P.w(t, of_ev[k - 2])
                            ti = None
                            for c in range(4):
                                ti = t.matmul(pU[:],
                                              WPO[:, c * D + do * 128: c * D + (do + 1) * 128],
                                              ATT2[:, c * T + tr * 512:
                                                   c * T + (tr + 1) * 512],
                                              start=(c == 0), stop=(c == 3))
                            mm_ev = P.tlast(ti)
                            P.w(sc, mm_ev)
                            if tr > 0:
                                P.w(sc, ty_ev[tr * 4 - 1])
                            of_ev[k] = P.so(lambda: sc.activation(
                                out=OUTF[:, do * 512:(do + 1) * 512], in_=pU[:],
                                func=AF.Identity))
                        for i in range(4):
                            gidx = tr * 4 + i
                            pa, pb2 = (pR0, pR1) if gidx % 2 == 0 else (pS0, pS1)
                            P.w(t, of_ev[tr * NCH + NCH - 1])
                            if gidx >= 2:
                                P.w(t, ev2_ev[gidx - 2])
                            ti = None
                            for do in range(NCH):
                                src = OUTF[:, do * 512 + i * 128: do * 512 + (i + 1) * 128]
                                dst = (pa if do < 4 else pb2)[:, (do % 4) * 128:
                                                              (do % 4 + 1) * 128]
                                ti = t.transpose(dst, src, IDF[:])
                            ty_ev[gidx] = P.tlast(ti)
                            if gidx >= 2:
                                P.w(sy, fin_ev[gidx - 2])
                            P.wd(sy, s_st, 16 * NT * (l + 1) if False else P.cnt[id(s_st)])
                            ld_ev = P.dma(sy, xbufs[gidx % 2][:],
                                          xi[gidx * 128:(gidx + 1) * 128, :], lds[gidx % 2])
                            P.w(v, ty_ev[gidx])
                            if gidx >= 1:
                                P.w(v, vstc_ev[gidx - 1] or ty_ev[gidx])
                            if sh_prev[0] is not None:
                                P.w(v, sh_prev[0])
                            P.wd(v, s_st, P.cnt[id(s_st)])
                            P.vo(lambda: v.tensor_copy(out=yt0[:, 0:512], in_=pa[:]))
                            ev2_ev[gidx] = P.vo(lambda: v.tensor_copy(out=yt0[:, 512:1024],
                                                                      in_=pb2[:]))
                            P.w(v, ld_ev)
                            xm_ev = P.vo(lambda: v.tensor_add(yt0[:], xbufs[gidx % 2][:],
                                                              yt0[:]))
                            P.w(sy, xm_ev)
                            st_ev = P.dma(sy, xi[gidx * 128:(gidx + 1) * 128, :], yt0[:],
                                          s_st)
                            if gidx >= 1:
                                P.wd(g, s_st, P.cnt[id(s_st)] - 16)
                                if sh_prev[0] is not None:
                                    P.w(g, sh_prev[0])
                                rb_ev = P.dma(g, rowb[:],
                                              xi[gidx * 128 - 1: gidx * 128, 512:1024],
                                              s_sr)
                            else:
                                rb_ev = None
                            psh = pb2 if gidx % 2 == 0 else pR1
                            psh = pQs[(tr * NCH) % 2] if False else psh
                            P.w(t, ev2_ev[gidx])
                            P.w(t, xm_ev)
                            ti = t.matmul(pa[:, 0:512], SH1[:], yt0[:, 512:1024],
                                          start=True, stop=False)
                            if rb_ev is not None:
                                P.wd(t, s_sr, rb_ev[1])
                            else:
                                P.w(t, rb_z)
                            ti = t.matmul(pa[:, 0:512], E0[:], rowb[0:1, 0:512],
                                          start=False, stop=True)
                            sh_ev = P.tlast(ti)
                            sh_prev[0] = sh_ev
                            P.w(v, sh_ev)
                            P.vo(lambda: v.tensor_copy(out=asm[:, 0:512], in_=yt0[:, 0:512]))
                            asm_ev = P.vo(lambda: v.tensor_copy(out=asm[:, 512:1024],
                                                                in_=pa[:, 0:512]))
                            P.w(sc, asm_ev)
                            if gidx >= 1:
                                P.w(sc, vstc_ev[gidx - 1])
                            vstc_ev[gidx], _sqe = ln_stats(asm, sq)
                            xh2 = P.vo(lambda: v.tensor_scalar(
                                out=xhbt[:], in0=asm[:], scalar1=st8[:, 2:3],
                                scalar2=st8[:, 6:7], op0=ALU.add, op1=ALU.mult))
                            P.w(t, xh2)
                            if gidx >= 2:
                                P.w(t, xfe_ev[gidx - 2])
                            tpx = pb2
                            ti = None
                            for c in range(NCH):
                                ti = t.matmul(tpx[:, 0:128] if False else tpx[:, (c % 4) * 128:(c % 4 + 1) * 128],
                                              xhbt[:, c * 128:(c + 1) * 128], IDB[:],
                                              start=True, stop=True)
                                if c % 4 == 3 or c == NCH - 1:
                                    tx_ev = P.tlast(ti)
                                    P.w(v, tx_ev)
                                    for c2 in range(c - c % 4, c + 1):
                                        xfe_ev[gidx] = P.vo(lambda c2=c2: v.tensor_copy(
                                            out=XIB[:, c2 * TP1 + 1 + gidx * 128:
                                                    c2 * TP1 + 1 + (gidx + 1) * 128],
                                            in_=tpx[:, (c2 % 4) * 128:(c2 % 4 + 1) * 128]))
                                    if c == 3:
                                        P.w(t, xfe_ev[gidx])
                            fin_ev[gidx] = xfe_ev[gidx]
                    aff_ev = None
                    P.wd(v, s_w, vec_all)
                    for c in range(NCH):
                        P.w(v, fin_ev[NT - 1])
                        aff_ev = P.vo(lambda c=c: v.tensor_scalar(
                            out=XIB[:, c * TP1 + 1: c * TP1 + 1 + T],
                            in0=XIB[:, c * TP1 + 1: c * TP1 + 1 + T],
                            scalar1=VEC[:, 0 * NCH + c: 0 * NCH + c + 1],
                            scalar2=VEC[:, 1 * NCH + c: 1 * NCH + c + 1],
                            op0=ALU.mult, op1=ALU.add))
                    P.fence(extra=[(s_ld, P.cnt.get(id(s_ld), 0)),
                                   (s_le, P.cnt.get(id(s_le), 0)),
                                   (s_st, P.cnt.get(id(s_st), 0)),
                                   (s_sr, P.cnt.get(id(s_sr), 0)),
                                   (s_wp, P.cnt.get(id(s_wp), 0)),
                                   (s_w, P.cnt.get(id(s_w), 0))])

                with ExitStack() as fctx:
                    GU = fctx.enter_context(nc.sbuf_tensor(f"GU_{l}", [128, NF2 * 1024], BF16))
                    W2B = fctx.enter_context(nc.sbuf_tensor(f"W2B_{l}", [128, NCH * NF2 * 128],
                                                            BF16))
                    W1B = [fctx.enter_context(nc.sbuf_tensor(f"W1B{i}_{l}", [128, D], BF16))
                           for i in range(4)]
                    AH = fctx.enter_context(nc.sbuf_tensor(f"AH_{l}", [128, 512], BF16))
                    GH = fctx.enter_context(nc.sbuf_tensor(f"GH_{l}", [128, 512], BF16))
                    O2F = fctx.enter_context(nc.sbuf_tensor(f"O2F_{l}", [128, NCH * 512], F32))
                    pA = _ps(fctx, nc, f"pA_{l}", [128, 512], F32)
                    pG = _ps(fctx, nc, f"pG_{l}", [128, 512], F32)
                    pW0 = _ps(fctx, nc, f"pW0_{l}", [128, 512], F32)
                    pW1 = _ps(fctx, nc, f"pW1_{l}", [128, 512], F32)
                    pX0 = _ps(fctx, nc, f"pX0_{l}", [128, 512], F32)
                    pX1 = _ps(fctx, nc, f"pX1_{l}", [128, 512], F32)
                    pY0 = _ps(fctx, nc, f"pY0_{l}", [128, 512], F32)
                    pY1 = _ps(fctx, nc, f"pY1_{l}", [128, 512], F32)
                    pWs = [pW0, pW1]

                    gu_last = [None] * NF2
                    fin2_ev = [None] * NT
                    w1_dm = [None] * 4
                    w1_cons = [None] * 4
                    if l > 0:
                        P.w(g, P.last.get("t", (P.SG, 0)))
                    prev_wg = P.cnt.get(id(s_wg), 0)
                    g.wait_ge(s_wg, prev_wg)
                    for fo in range(NCH):
                        P.dma(
                            g, W2B[:, fo * NF2 * 128:(fo + 1) * NF2 * 128],
                            w2t[(l * NCH + fo) * 128:(l * NCH + fo + 1) * 128, :],
                            s_wg)
                    w2t_all = P.cnt[id(s_wg)]
                    for tr2 in range(4):
                        t0 = tr2 * 1024

                        def load_w1(ft):
                            for hf in range(2):
                                slot = (ft % 2) * 2 + hf
                                if w1_cons[slot] is not None:
                                    P.w(g, w1_cons[slot])
                                w1_dm[slot] = P.dma_serial(
                                    g, W1B[slot][:],
                                    w1t[((l * 2 + hf) * NF2 + ft) * 128:
                                        ((l * 2 + hf) * NF2 + ft + 1) * 128, :],
                                    s_wf)

                        load_w1(0)
                        guev = None
                        for ft in range(NF2):
                            if ft + 1 < NF2:
                                load_w1(ft + 1)
                            for sb in range(2):
                                for hf, (pDst, BUF, BV) in enumerate(
                                        (((pA, AH, B1A)), ((pG, GH, B1G)))):
                                    slot = (ft % 2) * 2 + hf
                                    P.wd(t, s_wf, w1_dm[slot][1])
                                    if ft == 0 and sb == 0 and hf == 0 and tr2 == 0:
                                        P.w(t, aff_ev)
                                    if guev is not None:
                                        P.w(t, guev)
                                    ti = None
                                    for c in range(NCH):
                                        off = c * TP1 + 1
                                        ti = t.matmul(
                                            pDst[:],
                                            W1B[slot][:, c * 128:(c + 1) * 128],
                                            XIB[:, off + t0 + sb * 512:
                                                off + t0 + (sb + 1) * 512],
                                            start=(c == 0), stop=(c == NCH - 1))
                                    mmev = P.tlast(ti)
                                    P.w(sc, mmev)
                                    fn = AF.Identity if hf == 0 else AF.Gelu
                                    ev = P.so(lambda fn=fn, BUF=BUF, BV=BV, pDst=pDst, ft=ft:
                                              sc.activation(out=BUF[:], in_=pDst[:],
                                                            func=fn,
                                                            bias=BV[:, ft:ft + 1]))
                                P.w(v, ev)
                                guev = P.vo(lambda ft=ft, sb=sb: v.tensor_mul(
                                    GU[:, ft * 1024 + sb * 512: ft * 1024 + (sb + 1) * 512],
                                    AH[:], GH[:]))
                                w1_cons[(ft % 2) * 2 + 0] = guev
                                w1_cons[(ft % 2) * 2 + 1] = guev
                        gu_done = guev
                        o2_ev = [None] * (2 * NCH)
                        for sb in range(2):
                            for fo in range(NCH):
                                k = sb * NCH + fo
                                pU = pWs[k % 2]
                                if k == 0:
                                    P.w(t, gu_done)
                                    P.wd(t, s_wg, w2t_all)
                                if k >= 2:
                                    P.w(t, o2_ev[k - 2])
                                ti = None
                                for c in range(NF2):
                                    ti = t.matmul(
                                        pU[:],
                                        W2B[:, fo * NF2 * 128 + c * 128:
                                            fo * NF2 * 128 + (c + 1) * 128],
                                        GU[:, c * 1024 + sb * 512: c * 1024 + (sb + 1) * 512],
                                        start=(c == 0), stop=(c == NF2 - 1))
                                mm2 = P.tlast(ti)
                                P.w(sc, mm2)
                                if sb > 0 or tr2 > 0:
                                    pass
                                P.w(sc, P.last["v"])
                                o2_ev[k] = P.so(lambda fo=fo, pU=pU: sc.activation(
                                    out=O2F[:, fo * 512:(fo + 1) * 512], in_=pU[:],
                                    func=AF.Identity,
                                    bias=VEC[:, 2 * NCH + fo: 2 * NCH + fo + 1]))
                            for i in range(4):
                                gidx = tr2 * 8 + sb * 4 + i
                                pa2, pb3 = (pX0, pX1) if gidx % 2 == 0 else (pY0, pY1)
                                P.w(t, o2_ev[sb * NCH + NCH - 1])
                                if fin2_ev[gidx - 2] is not None if gidx >= 2 else False:
                                    P.w(t, fin2_ev[gidx - 2])
                                ti = None
                                for do in range(NCH):
                                    src = O2F[:, do * 512 + i * 128: do * 512 + (i + 1) * 128]
                                    dst = (pa2 if do < 4 else pb3)[:, (do % 4) * 128:
                                                                   (do % 4 + 1) * 128]
                                    ti = t.transpose(dst, src, IDF[:])
                                tyv = P.tlast(ti)
                                if gidx >= 2:
                                    P.w(sy, fin2_ev[gidx - 2])
                                P.wd(sy, s_st, P.cnt[id(s_st)])
                                ldv = P.dma(sy, xbufs[gidx % 2][:],
                                            xi[gidx * 128:(gidx + 1) * 128, :],
                                            lds[gidx % 2])
                                P.w(v, tyv)
                                P.wd(v, s_st, P.cnt[id(s_st)])
                                P.vo(lambda: v.tensor_copy(out=yt0[:, 0:512], in_=pa2[:]))
                                P.vo(lambda: v.tensor_copy(out=yt0[:, 512:1024], in_=pb3[:]))
                                P.w(v, ldv)
                                fin2_ev[gidx] = P.vo(lambda: v.tensor_add(
                                    yt0[:], xbufs[gidx % 2][:], yt0[:]))
                                P.w(sy, fin2_ev[gidx])
                                P.dma(sy, xi[gidx * 128:(gidx + 1) * 128, :], yt0[:], s_st)
                    P.fence(extra=[(s_ld, P.cnt.get(id(s_ld), 0)),
                                   (s_le, P.cnt.get(id(s_le), 0)),
                                   (s_st, P.cnt.get(id(s_st), 0)),
                                   (s_wf, P.cnt.get(id(s_wf), 0)),
                                   (s_wg, P.cnt.get(id(s_wg), 0))])

                with ExitStack() as rctx:
                    tpA = _ps(rctx, nc, f"tpR0_{l}", [128, 1024], BF16)
                    tpB = _ps(rctx, nc, f"tpR1_{l}", [128, 1024], BF16)
                    rtp = [tpA, tpB]
                    rxh = [None] * NT
                    rtp_ev = [None] * NT
                    rev = [None] * NT
                    rvst = [None] * NT
                    rfin = [None] * NT
                    for i in range(NT):
                        xb = xbufs[i % 2]
                        if i >= 2:
                            P.w(sy, rxh[i - 2])
                        ld_ev = P.dma(sy, xb[:], xi[i * 128:(i + 1) * 128, :], lds[i % 2])
                        if l == 0:
                            P.w(v, ld_ev)
                            if i >= 2:
                                P.w(v, rtp_ev[i - 1])
                            rxh[i] = P.vo(lambda: v.tensor_copy(out=xhbt[:], in_=xb[:]))
                            tp = rtp[i % 2]
                            P.w(t, rxh[i])
                            if i >= 2:
                                P.w(t, rev[i - 2])
                            ti = None
                            for c in range(NCH):
                                ti = t.transpose(tp[:, c * 128:(c + 1) * 128],
                                                 xhbt[:, c * 128:(c + 1) * 128], IDB[:])
                            rtp_ev[i] = P.tlast(ti)
                            P.w(v, rtp_ev[i])
                            for c in range(NCH):
                                rev[i] = P.vo(lambda c=c: v.tensor_copy(
                                    out=XIB[:, c * TP1 + 1 + i * 128:
                                            c * TP1 + 1 + (i + 1) * 128],
                                    in_=tp[:, c * 128:(c + 1) * 128]))
                        else:
                            P.w(sc, ld_ev)
                            if i >= 1:
                                P.w(sc, rvst[i - 1])
                            P.w(v, ld_ev)
                            if i >= 2:
                                P.w(v, rfin[i - 2])
                            rvst[i], _sqe = ln_stats(xb, sq)
                            if i >= 1:
                                P.wd(v, s_st, P.cnt[id(s_st)])
                            P.vo(lambda: v.tensor_scalar(out=yt0[:], in0=xb[:],
                                                         scalar1=st8[:, 2:3],
                                                         scalar2=st8[:, 6:7],
                                                         op0=ALU.add, op1=ALU.mult))
                            if i >= 1:
                                P.w(sy, rfin[i - 1])
                            ld2 = P.dma(sy, asm[:], x[i * 128:(i + 1) * 128, :], s_sx)
                            P.wd(v, s_sx, ld2[1])
                            rxh[i] = rfin[i] = P.vo(lambda: v.tensor_add(yt0[:], asm[:],
                                                                         yt0[:]))
                            P.w(sy, rfin[i])
                            P.dma(sy, y[i * 128:(i + 1) * 128, :], yt0[:], s_st)
                    if l == 0:
                        P.fence(extra=[(s_ld, P.cnt.get(id(s_ld), 0)),
                                       (s_le, P.cnt.get(id(s_le), 0))])
            P.wd(sy, s_st, P.cnt[id(s_st)])
        emit(None)
        _stage_fence(P, env)


def _stage2_inputs(inputs):
    wqkv = np.asarray(inputs["L0_ffn_wqkv"], np.float32)
    wqkvp = np.ascontiguousarray(
        wqkv.reshape(2, NCH, 128, 12, 128).transpose(0, 3, 2, 1, 4)).reshape(
            2 * 12 * 128, D)
    wo = np.asarray(inputs["L0_ffn_wo"], np.float32).reshape(2 * 512, D)
    wo = np.ascontiguousarray(wo)
    w1 = np.asarray(inputs["L0_ffn_w1"], np.float32)
    w1a = np.zeros((2, D, HID2), np.float32)
    w1g = np.zeros((2, D, HID2), np.float32)
    w1a[:, :, :2730] = w1[:, :, :2730]
    w1g[:, :, :2730] = w1[:, :, 2730:]
    w1s = np.stack([w1a, w1g], axis=1)
    w1t = np.ascontiguousarray(
        w1s.reshape(2, 2, NCH, 128, NF2, 128).transpose(0, 1, 4, 3, 2, 5)).reshape(
            2 * 2 * NF2 * 128, D)
    w2 = np.asarray(inputs["L0_ffn_w2"], np.float32)
    w2p = np.zeros((2, HID2, D), np.float32)
    w2p[:, :2730, :] = w2
    w2t = np.ascontiguousarray(
        w2p.reshape(2, NF2, 128, NCH, 128).transpose(0, 3, 2, 1, 4)).reshape(
            2 * NCH * 128, NF2 * 128)
    b1 = np.asarray(inputs["L0_ffn_b1"], np.float32)
    b1a = np.zeros((2, HID2), np.float32)
    b1g = np.zeros((2, HID2), np.float32)
    b1a[:, :2730] = b1[:, :2730]
    b1g[:, :2730] = b1[:, 2730:]
    b1p = np.ascontiguousarray(
        np.stack([b1a, b1g], axis=1).reshape(2, 2, NF2, 128).transpose(
            0, 1, 3, 2)).reshape(2 * 2 * 128, NF2)
    qs = np.asarray(inputs["L0_ffn_q_scale"], np.float32)
    ks = np.asarray(inputs["L0_ffn_k_scale"], np.float32)
    qsks = np.ascontiguousarray(
        np.stack([qs, ks], axis=1).reshape(2 * 2 * 64, 1))
    vec3 = np.stack([np.asarray(inputs["L0_ffn_ln_g"], np.float32),
                     np.asarray(inputs["L0_ffn_ln_b"], np.float32),
                     np.asarray(inputs["L0_ffn_b2"], np.float32)],
                    axis=1)
    vec3 = np.ascontiguousarray(
        vec3.reshape(2, 3, NCH, 128).transpose(0, 1, 3, 2)).reshape(
            2 * 3 * 128, NCH)
    return {"s2_wqkvp": wqkvp, "s2_wol": wo, "s2_w1t": w1t, "s2_w2t": w2t,
            "s2_vec3": vec3, "s2_qsks": qsks, "s2_b1p": b1p}



ECAP = 4096
NE = 8
NA = 2 * NCORES * T
NTB = 36864
TBW = 18432
GROW = 16384.0


def emit_stage4(nc, P, env):
    x = env["x3"]
    wr = env["s4_wr"]
    wgt = env["s4_wgt"]
    wut = env["s4_wut"]
    wdt = env["s4_wdt"]
    fwgt = env["s4_fwgt"]
    fwut = env["s4_fwut"]
    fwdt = env["s4_fwdt"]
    vin = env["s4_vecs"]
    tokc = env["s4_tokc"]
    coff = env["s4_coff"]
    y = env["y"]
    hsb = env["s4_hsb"]
    lgd = env["s4_lgd"]
    hs_all = env["s4_hs_all"]
    lg_all = env["s4_lg_all"]
    esd = env["s4_esd"]
    paf = env["s4_paf"]
    pafR = env["s4_pafR"]
    esdR = env["s4_esdR"]
    w12d = env["s4_w12d"]
    itd = env["s4_itd"]
    it_all = env["s4_it_all"]
    fod = env["s4_fod"]
    fo_all = env["s4_fo_all"]

    idb = env["idb"]
    idf = env["idf"]
    iotaec = env["s4_iotaec"]
    iota8c = env["s4_iota8c"]
    rg = [list(range(8))]
    ue = "p (u e) -> p u e"
    uo = "p (u o) -> p u o"

    ctx = ExitStack()
    with ctx:
        IDB = _sb(ctx, nc, "IDB", [128, 128], BF16)
        IDF = _sb(ctx, nc, "IDF", [128, 128], F32)
        WRS = _sb(ctx, nc, "WRS", [128, NCH * NE], F32)
        NRM = _sb(ctx, nc, "NRM", [128, NCH], F32)
        FNM = _sb(ctx, nc, "FNM", [128, NCH], F32)
        STM = _sb(ctx, nc, "STM", [128, NT], F32)
        STR = _sb(ctx, nc, "STR", [128, NT], F32)
        st8 = _sb(ctx, nc, "st8", [128, 12], F32)
        xb0 = _sb(ctx, nc, "xb0", [128, D], F32)
        xb1 = _sb(ctx, nc, "xb1", [128, D], F32)
        sq = _sb(ctx, nc, "sq", [128, D], F32)
        xhbt = _sb(ctx, nc, "xhbt", [128, D], BF16)
        hsf = _sb(ctx, nc, "hsf", [128, D], F32)
        yt0 = _sb(ctx, nc, "yt0", [128, D], F32)
        W12 = _sb(ctx, nc, "W12", [128, 512], F32)
        POSL = _sb(ctx, nc, "POSL", [128, 64], F32)
        ESL = _sb(ctx, nc, "ESL", [128, 64], F32)
        POSTL = _sb(ctx, nc, "POSTL", [128, 64], F32)
        ESTL = _sb(ctx, nc, "ESTL", [128, 64], F32)
        KWT = _sb(ctx, nc, "KWT", [128, 64], F32)
        KPT = _sb(ctx, nc, "KPT", [128, 64], F32)
        TOKS = _sb(ctx, nc, "TOKS", [128, 64], F32)
        OVT = _sb(ctx, nc, "OVT", [128, 32], F32)
        ITS = _sb(ctx, nc, "ITS", [128, 64], F32)
        ITSC = _sb(ctx, nc, "ITSC", [128, 32], F32)
        WCG = _sb(ctx, nc, "WCG", [128, 512], I16)
        WEG = _sb(ctx, nc, "WEG", [128, 256], I16)
        W16F = _sb(ctx, nc, "W16F", [16, 512], F32)
        s_w = env["s_w"]
        s_ld = env["s_ld"]
        s_le = env["s_le"]
        s_st = env["s_st"]
        s_cc = env["s_cc"]
        s_sc = env["s_sc"]
        s_g2 = env["s_g2"]
        s_wf = env["s_wf"]

        xbufs = [xb0, xb1]
        lds = [s_ld, s_le]

        def emit(_):
            v, sc, t, sy, g = P.v, P.s, P.t, P.sy, P.g

            def ln_stats(src, sqbuf):
                if "v" in P.last:
                    P.w(sc, P.last["v"])
                sq_ev = P.so(lambda: sc.activation(out=sqbuf[:], in_=src[:],
                                                   func=AF.Square))
                P.w(v, sq_ev)
                P.vo(lambda: v.reduce_sum(st8[:, 0:1], src[:], axis=AX.X))
                P.vo(lambda: v.reduce_sum(st8[:, 1:2], sqbuf[:], axis=AX.X))
                P.vo(lambda: v.tensor_scalar_mul(st8[:, 2:3], st8[:, 0:1], -1.0 / D))
                P.vo(lambda: v.tensor_scalar_mul(st8[:, 3:4], st8[:, 1:2], 1.0 / D))
                P.vo(lambda: v.tensor_mul(st8[:, 4:5], st8[:, 2:3], st8[:, 2:3]))
                vst = P.vo(lambda: v.tensor_sub(st8[:, 3:4], st8[:, 3:4], st8[:, 4:5]))
                P.w(sc, vst)
                sd = P.so(lambda: sc.activation(out=st8[:, 5:6], in_=st8[:, 3:4],
                                                func=AF.Sqrt, bias=st8[:, 7:8]))
                P.w(v, sd)
                return P.vo(lambda: v.reciprocal(st8[:, 6:7], st8[:, 5:6]))

            def rms_stats(src):
                if "v" in P.last:
                    P.w(sc, P.last["v"])
                sqv = P.so(lambda: sc.activation(out=sq[:], in_=src[:],
                                                 func=AF.Square))
                P.w(v, sqv)
                rs = P.vo(lambda: v.reduce_sum(st8[:, 0:1], sq[:], axis=AX.X))
                P.w(sc, rs)
                sd = P.so(lambda: sc.activation(out=st8[:, 5:6], in_=st8[:, 0:1],
                                                func=AF.Sqrt, bias=st8[:, 11:12],
                                                scale=1.0 / D))
                P.w(v, sd)
                return P.vo(lambda: v.reciprocal(st8[:, 6:7], st8[:, 5:6]))

            def wbuild(srcap, C, pbank, wi):
                P.w(t, P.last["v"])
                ti = None
                for pp in range(8):
                    ti = t.matmul(pbank[0:16, pp * C:(pp + 1) * C],
                                  IDF[:, pp * 16:(pp + 1) * 16], srcap,
                                  start=True, stop=True)
                me = P.tlast(ti)
                P.w(v, me)
                for pp in range(8):
                    P.vo(lambda pp=pp: v.tensor_copy(
                        out=W16F[0:16, 0:8 * C].rearrange(
                            "q (c pp) -> q c pp", pp=8)[:, :, pp:pp + 1],
                        in_=pbank[0:16, pp * C:(pp + 1) * C].rearrange(
                            "q (c o) -> q c o", o=1)))
                cast = P.vo(lambda: v.tensor_copy(out=wi[0:16, 0:8 * C],
                                                  in_=W16F[0:16, 0:8 * C]))
                P.w(g, cast)
                ev = P.dma(g, wi[16:32, 0:8 * C], wi[0:16, 0:8 * C], s_sc)
                P.wd(g, s_sc, ev[1])
                ev = P.dma(g, wi[32:64, 0:8 * C], wi[0:32, 0:8 * C], s_sc)
                P.wd(g, s_sc, ev[1])
                ev = P.dma(g, wi[64:128, 0:8 * C], wi[0:64, 0:8 * C], s_sc)
                return ev

            for dst, src in ((IDB, idb), (IDF, idf), (TOKS, tokc)):
                P.dma(g, dst[:], src.ap(), s_w)
            for c in range(NCH):
                P.dma(g, WRS[:, c * NE:(c + 1) * NE], wr[c * 128:(c + 1) * 128, :], s_w)
                P.dma(g, NRM[:, c:c + 1], vin[c * 128:(c + 1) * 128, :], s_w)
                P.dma(g, FNM[:, c:c + 1], vin[D + c * 128: D + (c + 1) * 128, :], s_w)
            W_ALL = P.cnt[id(s_w)]
            P.wd(v, s_w, W_ALL)
            P.vo(lambda: v.memset(st8[:, 7:8], EPS_LN))
            P.vo(lambda: v.memset(st8[:, 11:12], 1e-6))

            with ExitStack() as actx:
                tpF = _ps(actx, nc, "tpF", [128, 1024], F32)
                tpG = _ps(actx, nc, "tpG", [128, 1024], F32)
                pL0 = _ps(actx, nc, "pL0", [128, NE], F32)
                pL1 = _ps(actx, nc, "pL1", [128, NE], F32)
                hst = actx.enter_context(nc.sbuf_tensor("hst", [128, D], F32))
                lgs = actx.enter_context(nc.sbuf_tensor("lgs", [128, 2 * NE], F32))
                tp_cons = [None, None]
                lg_ev = [None] * NT
                hs_ev = [None] * NT
                lme_prev = None
                for i in range(NT):
                    xb = xbufs[i % 2]
                    if i >= 2:
                        P.w(sy, hs_ev[i - 2])
                    ld_ev = P.dma(sy, xb[:], x[i * 128:(i + 1) * 128, :], lds[i % 2])
                    P.w(sc, ld_ev)
                    P.w(v, ld_ev)
                    ln_stats(xb, sq)
                    P.vo(lambda i=i: v.tensor_copy(out=STM[:, i:i + 1], in_=st8[:, 2:3]))
                    P.vo(lambda i=i: v.tensor_copy(out=STR[:, i:i + 1], in_=st8[:, 6:7]))
                    hs_ev[i] = P.vo(lambda i=i: v.tensor_scalar(
                        out=hsf[:], in0=xb[:], scalar1=st8[:, 2:3], scalar2=st8[:, 6:7],
                        op0=ALU.add, op1=ALU.mult))
                    P.wd(v, s_st, P.cnt.get(id(s_st), 0))
                    bf_ev = P.vo(lambda: v.tensor_copy(out=xhbt[:], in_=hsf[:]))
                    P.w(sy, bf_ev)
                    P.dma(sy, hsb[i * 128:(i + 1) * 128, :], xhbt[:], s_st)
                    tp = tpF if i % 2 == 0 else tpG
                    P.w(t, hs_ev[i])
                    if i == 0:
                        P.wd(t, s_w, W_ALL)
                    if tp_cons[i % 2] is not None:
                        P.w(t, tp_cons[i % 2])
                    ti = None
                    for c in range(NCH):
                        ti = t.transpose(tp[:, c * 128:(c + 1) * 128],
                                         hsf[:, c * 128:(c + 1) * 128], IDF[:])
                    tpe = P.tlast(ti)
                    P.w(v, tpe)
                    if lme_prev is not None:
                        P.w(v, lme_prev)
                    hev = None
                    for c in range(NCH):
                        hev = P.vo(lambda c=c: v.tensor_copy(
                            out=hst[:, c * 128:(c + 1) * 128],
                            in_=tp[:, c * 128:(c + 1) * 128]))
                    tp_cons[i % 2] = hev
                    pL = pL0 if i % 2 == 0 else pL1
                    P.w(t, hev)
                    if i >= 2:
                        P.w(t, lg_ev[i - 2])
                    ti = None
                    for c in range(NCH):
                        ti = t.matmul(pL[:], hst[:, c * 128:(c + 1) * 128],
                                      WRS[:, c * NE:(c + 1) * NE],
                                      start=(c == 0), stop=(c == NCH - 1))
                    lme = P.tlast(ti)
                    lme_prev = lme
                    P.w(v, lme)
                    if i >= 2:
                        P.w(v, lg_ev[i - 2])
                    lgc = P.vo(lambda i=i, pL=pL: v.tensor_copy(
                        out=lgs[:, (i % 2) * NE:(i % 2 + 1) * NE], in_=pL[:]))
                    P.w(sy, lgc)
                    lg_ev[i] = P.dma(sy, lgd[i * 128:(i + 1) * 128, :],
                                     lgs[:, (i % 2) * NE:(i % 2 + 1) * NE], s_st)
            P.fence(extra=[(s_ld, P.cnt.get(id(s_ld), 0)), (s_le, P.cnt.get(id(s_le), 0)),
                           (s_st, P.cnt.get(id(s_st), 0))])

            cc1 = P.go(lambda: g.collective_compute(
                "AllGather", ALU.bypass, rg, ins=[hsb.ap()], outs=[hs_all.ap()]))
            cc2 = P.go(lambda: g.collective_compute(
                "AllGather", ALU.bypass, rg, ins=[lgd.ap()], outs=[lg_all.ap()]))

            with ExitStack() as rctx:
                LG = rctx.enter_context(nc.sbuf_tensor("LG", [128, 2048], F32))
                IOTAE = rctx.enter_context(nc.sbuf_tensor("IOTAE", [128, 2048], F32))
                IOTA8 = rctx.enter_context(nc.sbuf_tensor("IOTA8", [8, 1], F32))
                SM = rctx.enter_context(nc.sbuf_tensor("SM", [128, 256], F32))
                M1 = rctx.enter_context(nc.sbuf_tensor("M1", [128, 256], F32))
                EQ = rctx.enter_context(nc.sbuf_tensor("EQ", [128, 2048], F32))
                CU = rctx.enter_context(nc.sbuf_tensor("CU", [128, 2048], F32))
                E12 = rctx.enter_context(nc.sbuf_tensor("E12", [128, 512], F32))
                POSA = rctx.enter_context(nc.sbuf_tensor("POSA", [128, 512], F32))
                ESA = rctx.enter_context(nc.sbuf_tensor("ESA", [128, 512], F32))
                POSAT = rctx.enter_context(nc.sbuf_tensor("POSAT", [128, 512], F32))
                ZT = rctx.enter_context(nc.sbuf_tensor("ZT", [128, 256], F32))
                pE0 = _ps(rctx, nc, "pE0", [128, 512], F32)
                pE1 = _ps(rctx, nc, "pE1", [128, 512], F32)
                pF0 = _ps(rctx, nc, "pF0", [128, 512], F32)
                pF1 = _ps(rctx, nc, "pF1", [128, 512], F32)
                pW16 = _ps(rctx, nc, "pW16", [16, 512], F32)
                sctx = ExitStack()
                EB = sctx.enter_context(nc.sbuf_tensor("EB", [8, 8192], F32))
                EB2 = sctx.enter_context(nc.sbuf_tensor("EB2", [8, 8192], F32))
                Z8 = sctx.enter_context(nc.sbuf_tensor("Z8", [8, 8192], F32))
                PS = sctx.enter_context(nc.sbuf_tensor("PS", [8, 8192], F32))

                P.dma(g, IOTAE[:], iotaec.ap(), s_w)
                P.dma(g, IOTA8[:], iota8c.ap(), s_w)
                IOTA_ALL = P.cnt[id(s_w)]
                zt_ev = P.vo(lambda: v.memset(ZT[:], 0.0))
                z8_ev = P.vo(lambda: v.memset(Z8[:], 0.0))

                P.w(sy, cc2)
                ldL = P.dma(sy, LG[:].rearrange(ue, e=8),
                            lg_all.ap().rearrange("(u p) e -> p u e", p=128), s_cc)
                P.wd(sc, s_cc, ldL[1])
                ex_ev = P.so(lambda: sc.activation(out=LG[:], in_=LG[:], func=AF.Exp))
                P.w(v, ex_ev)
                P.wd(v, s_w, IOTA_ALL)
                P.vo(lambda: v.reduce_sum(SM[:].rearrange(uo, o=1),
                                          LG[:].rearrange(ue, e=8), axis=AX.X))
                P.vo(lambda: v.reciprocal(SM[:], SM[:]))
                P.vo(lambda: v.tensor_tensor(
                    LG[:].rearrange(ue, e=8), LG[:].rearrange(ue, e=8),
                    SM[:].rearrange(uo, o=1).to_broadcast([128, 256, 8]),
                    op=ALU.mult))

                def top1(dst_e, dst_w):
                    P.vo(lambda: v.reduce_max(M1[:].rearrange(uo, o=1),
                                              LG[:].rearrange(ue, e=8), axis=AX.X))
                    P.vo(lambda: v.tensor_tensor(
                        EQ[:].rearrange(ue, e=8), LG[:].rearrange(ue, e=8),
                        M1[:].rearrange(uo, o=1).to_broadcast([128, 256, 8]),
                        op=ALU.is_equal))
                    P.vo(lambda: v.tensor_copy(out=CU[:], in_=EQ[:]))
                    for e in range(1, 8):
                        P.vo(lambda e=e: v.tensor_add(
                            CU[:].rearrange(ue, e=8)[:, :, e:e + 1],
                            CU[:].rearrange(ue, e=8)[:, :, e - 1:e],
                            CU[:].rearrange(ue, e=8)[:, :, e:e + 1]))
                    P.vo(lambda: v.tensor_single_scalar(CU[:], CU[:], 1.0,
                                                        op=ALU.is_equal))
                    P.vo(lambda: v.tensor_mul(EQ[:], EQ[:], CU[:]))
                    P.vo(lambda: v.tensor_mul(CU[:], EQ[:], IOTAE[:]))
                    P.vo(lambda: v.reduce_sum(dst_e.rearrange(uo, o=1),
                                              CU[:].rearrange(ue, e=8), axis=AX.X))
                    return P.vo(lambda: v.tensor_copy(out=dst_w, in_=M1[:]))

                top1(E12[:, 0:256], W12[:, 0:256])
                P.vo(lambda: v.scalar_tensor_tensor(
                    out=LG[:], in0=EQ[:], scalar=-1e30, in1=LG[:],
                    op0=ALU.mult, op1=ALU.add))
                w2_ev = top1(E12[:, 256:512], W12[:, 256:512])

                for j in range(2):
                    P.vo(lambda j=j: v.tensor_copy(
                        out=CU[:, 0:512].rearrange("p (u j) -> p u j", j=2)[:, :, j:j + 1],
                        in_=E12[:, j * 256:(j + 1) * 256].rearrange(uo, o=1)))
                ei_ev = P.last["v"]
                P.w(sy, ei_ev)
                P.dma(sy, esd.ap().rearrange("(u p j) o -> p u (j o)", p=128, j=2),
                      CU[:, 0:512].rearrange("p (u j) -> p u j", j=2), s_cc)
                P.w(sy, w2_ev)
                P.dma(sy, w12d.ap(), W12[:], s_cc)
                ES_W = P.cnt[id(s_cc)]

                prd_ev = None
                prd2_ev = None
                ext_ev = [None] * 8
                ext2_ev = [None] * 8
                red_ev = [None] * 8
                red2_ev = [None] * 8
                for r in range(8):
                    a0 = r * 8192
                    P.wd(sy, s_cc, P.cnt[id(s_cc)])
                    if prd_ev is not None:
                        P.w(sy, prd_ev)
                    if r >= 1:
                        P.w(sy, ext_ev[r - 1])
                    for p8 in range(8):
                        P.dma(sy, EB[p8:p8 + 1, :],
                              esd[a0:a0 + 8192, :].rearrange("a o -> o a"), s_cc)
                    P.wd(v, s_cc, P.cnt[id(s_cc)])
                    P.vo(lambda: v.tensor_single_scalar(EB[:], EB[:], IOTA8[:],
                                                        op=ALU.is_equal))
                    if r >= 1:
                        P.w(v, ext2_ev[r - 1])
                    prd2_ev = P.vo(lambda: v.tensor_single_scalar(
                        EB2[:], EB[:], IOTA8[:], op=ALU.mult))
                    ini = 0.0 if r == 0 else PS[:, 8191:8192]
                    P.vo(lambda ini=ini: v.tensor_tensor_scan(
                        out=PS[:], data0=Z8[:], data1=EB[:], initial=ini,
                        op0=ALU.add, op1=ALU.add))
                    prd_ev = P.vo(lambda: v.tensor_mul(EB[:], EB[:], PS[:]))
                    pE = pE0 if r % 2 == 0 else pE1
                    pF = pF0 if r % 2 == 0 else pF1
                    P.w(t, prd_ev)
                    if r >= 2:
                        P.w(t, red_ev[r - 2])
                        P.w(t, red2_ev[r - 2])
                    ti = None
                    for at in range(64):
                        ti = t.matmul(pE[:, at * 8:(at + 1) * 8],
                                      EB[:, at * 128:(at + 1) * 128],
                                      IDF[0:8, 0:8], start=True, stop=True)
                    ext_ev[r] = P.tlast(ti)
                    ti = None
                    for at in range(64):
                        ti = t.matmul(pF[:, at * 8:(at + 1) * 8],
                                      EB2[:, at * 128:(at + 1) * 128],
                                      IDF[0:8, 0:8], start=True, stop=True)
                    ext2_ev[r] = P.tlast(ti)
                    P.w(v, ext_ev[r])
                    red_ev[r] = P.vo(lambda r=r, pE=pE: v.reduce_sum(
                        POSA[:, r * 64:(r + 1) * 64].rearrange(uo, o=1),
                        pE[:].rearrange("p (a e) -> p a e", e=8), axis=AX.X))
                    P.w(v, ext2_ev[r])
                    red2_ev[r] = P.vo(lambda r=r, pF=pF: v.reduce_sum(
                        ESA[:, r * 64:(r + 1) * 64].rearrange(uo, o=1),
                        pF[:].rearrange("p (a e) -> p a e", e=8), axis=AX.X))

                P.w(sy, red_ev[7])
                P.dma(sy, pafR.ap(), POSA[:], s_cc)
                P.w(sy, red2_ev[7])
                P.dma(sy, esdR.ap(), ESA[:], s_cc)
                P.w(t, red_ev[7])
                P.w(t, red_ev[6])
                ti = None
                for q in range(4):
                    ti = t.transpose(pE0[:, q * 128:(q + 1) * 128],
                                     POSA[:, q * 128:(q + 1) * 128], IDF[:])
                tpa = P.tlast(ti)
                P.w(v, tpa)
                pat_ev = P.vo(lambda: v.tensor_copy(out=POSAT[:], in_=pE0[:]))
                P.w(sy, pat_ev)
                P.dma(sy, paf.ap().rearrange("(q k p) o -> k q (p o)", q=4, k=128),
                      POSAT[:].rearrange("k (q p) -> k q p", p=128), s_cc)
                PAF_W = P.cnt[id(s_cc)]
                P.fence(extra=[(s_cc, PAF_W)])
                sctx.close()

                P.wd(g, s_cc, PAF_W)
                with g.register("rA") as regA, g.register("rB") as regB, \
                        g.register("rC") as regC:
                    g.reg_load(regA, coff.ap()[0:1, 0:1])
                    g.reg_load(regB, coff.ap()[0:1, 2:3])
                    g.reg_load(regC, coff.ap()[0:1, 3:4])
                    offA = g.snap(regA)
                    offB = g.snap(regB)
                    offC = g.snap(regC)
                    tm_ = "(i p j) o -> p i (j o)"
                    P.dma(g, POSL[:], pafR.ap()[:, bass.ds(offA, 64)], s_sc)
                    P.dma(g, ESL[:], esdR.ap()[:, bass.ds(offA, 64)], s_sc)
                    P.dma(g, POSTL[:].rearrange("p (i j) -> p i j", j=2),
                          paf.ap().rearrange(tm_, p=128, j=2)[:, bass.ds(offB, 32), :],
                          s_sc)
                    P.dma(g, ESTL[:].rearrange("p (i j) -> p i j", j=2),
                          esd.ap().rearrange(tm_, p=128, j=2)[:, bass.ds(offB, 32), :],
                          s_sc)
                    P.dma(g, KWT[:, 0:32], w12d.ap()[:, bass.ds(offB, 32)], s_sc)
                    P.dma(g, KWT[:, 32:64], w12d.ap()[:, bass.ds(offC, 32)], s_sc)
                SEL_ALL = P.cnt[id(s_sc)]

                with ExitStack() as pctx:
                    ZR = pctx.enter_context(nc.sbuf_tensor("ZRp", [128, 2048], F32))
                    WSA = pctx.enter_context(nc.sbuf_tensor("WSA", [128, 512], I16))
                    WSB = pctx.enter_context(nc.sbuf_tensor("WSB", [128, 512], I16))
                    SELA = pctx.enter_context(nc.sbuf_tensor("SELA", [128, 64], F32))
                    KPA = pctx.enter_context(nc.sbuf_tensor("KPA", [128, 64], F32))
                    SCIF = pctx.enter_context(nc.sbuf_tensor("SCIF", [128, 64], F32))
                    TMP = pctx.enter_context(nc.sbuf_tensor("TMPp", [128, 64], F32))
                    PAYV = pctx.enter_context(nc.sbuf_tensor("PAYV", [128, 64], F32))
                    PAYW = pctx.enter_context(
                        nc.sbuf_tensor("PAYW", [128, 64, 64], F32))

                    P.wd(v, s_sc, SEL_ALL)
                    P.vo(lambda: v.tensor_scalar_add(POSL[:], POSL[:], -1.0))
                    P.vo(lambda: v.tensor_single_scalar(KPA[:], POSL[:], float(ECAP),
                                                        op=ALU.is_lt))
                    P.vo(lambda: v.scalar_tensor_tensor(
                        out=SCIF[:], in0=ESL[:], scalar=float(ECAP), in1=POSL[:],
                        op0=ALU.mult, op1=ALU.add))
                    P.vo(lambda: v.tensor_single_scalar(SELA[:], ESL[:], 3.5,
                                                        op=ALU.is_lt))
                    P.vo(lambda: v.tensor_mul(SELA[:], SELA[:], KPA[:]))
                    P.vo(lambda: v.tensor_scalar_add(TMP[:], SCIF[:], -GROW))
                    P.vo(lambda: v.tensor_mul(TMP[:], TMP[:], SELA[:]))
                    P.vo(lambda: v.tensor_scalar_add(TMP[:], TMP[:], GROW))
                    wbuild(TMP[:], 64, pW16, WSA)
                    P.vo(lambda: v.tensor_sub(SELA[:], KPA[:], SELA[:]))
                    P.vo(lambda: v.tensor_scalar_add(TMP[:], SCIF[:],
                                                     -GROW - 4.0 * ECAP))
                    P.vo(lambda: v.tensor_mul(TMP[:], TMP[:], SELA[:]))
                    P.vo(lambda: v.tensor_scalar_add(TMP[:], TMP[:], GROW))
                    wbuild(TMP[:], 64, pW16, WSB)
                    P.vo(lambda: v.tensor_scalar_add(PAYV[:], TOKS[:], 1.0))
                    P.vo(lambda: v.memset(PAYW[:], 0.0))
                    payw_ev = P.vo(lambda: v.tensor_copy(
                        out=PAYW[:, :, 0:1],
                        in_=PAYV[:].rearrange("p (c o) -> p c o", o=1)))
                    zr_ev = P.vo(lambda: v.memset(ZR[:], 0.0))

                    P.vo(lambda: v.tensor_scalar_add(POSTL[:], POSTL[:], -1.0))
                    P.vo(lambda: v.tensor_single_scalar(KPT[:], POSTL[:],
                                                        float(ECAP), op=ALU.is_lt))
                    P.vo(lambda: v.tensor_single_scalar(POSTL[:], POSTL[:],
                                                        float(ECAP - 1), op=ALU.min))
                    P.vo(lambda: v.scalar_tensor_tensor(
                        out=POSTL[:], in0=ESTL[:], scalar=float(ECAP), in1=POSTL[:],
                        op0=ALU.mult, op1=ALU.add))
                    wcg_ev = wbuild(POSTL[:], 64, pW16, WCG)
                    k3 = KPT[:].rearrange("p (i j) -> p i j", j=2)
                    for j in range(2):
                        P.vo(lambda j=j: v.tensor_tensor(
                            KWT[:, j * 32:(j + 1) * 32].rearrange(uo, o=1),
                            KWT[:, j * 32:(j + 1) * 32].rearrange(uo, o=1),
                            k3[:, :, j:j + 1], op=ALU.mult))
                    kw_ev = P.last["v"]
                    P.vo(lambda: v.tensor_tensor(
                        OVT[:].rearrange(uo, o=1), k3[:, :, 0:1], k3[:, :, 1:2],
                        op=ALU.mult))
                    ov_ev = P.vo(lambda: v.tensor_single_scalar(OVT[:], OVT[:], 0.5,
                                                                op=ALU.is_lt))

                    P.w(g, zr_ev)
                    for q in range(9):
                        P.dma(g, itd.ap().rearrange("(q p c) o -> q p (c o)",
                                                    q=9, p=128)[q],
                              ZR[:], s_sc)
                    P.w(g, payw_ev)
                    P.wd(g, s_sc, P.cnt[id(s_sc)])
                    for w0, wi_ in ((0, WSA), (TBW, WSB)):
                        for sp in range(2):
                            ev = P._mark(None, g.dma_scatter_add(
                                itd.ap()[w0:w0 + TBW, :],
                                PAYW[:, sp * 32:(sp + 1) * 32, :],
                                wi_[:, sp * 256:(sp + 1) * 256],
                                4096, 4096, 64, elem_step=64), s_g2, 16)
                            P.wd(g, s_g2, ev[1])
                    cc3 = P.go(lambda: g.collective_compute(
                        "AllReduce", ALU.add, rg, ins=[itd.ap()],
                        outs=[it_all.ap()]))
            P.fence(extra=[(s_cc, P.cnt.get(id(s_cc), 0)),
                           (s_sc, P.cnt.get(id(s_sc), 0)),
                           (s_g2, P.cnt.get(id(s_g2), 0)),
                           (s_w, P.cnt.get(id(s_w), 0))])

            with ExitStack() as ectx:
                WGB = ectx.enter_context(nc.sbuf_tensor("WGB", [128, 16 * 1024], BF16))
                WUB = ectx.enter_context(nc.sbuf_tensor("WUB", [128, 16 * 1024], BF16))
                WDB = ectx.enter_context(nc.sbuf_tensor("WDB", [128, NCH * 2048], BF16))
                XE = ectx.enter_context(nc.sbuf_tensor("XE", [128, NCH * 1024], BF16))
                GUE = ectx.enter_context(nc.sbuf_tensor("GUE", [128, 16 * 1024], BF16))
                EG = ectx.enter_context(nc.sbuf_tensor("EG", [128, 8, 1024], BF16))
                XH2 = ectx.enter_context(nc.sbuf_tensor("XH2", [128, 2 * 1024], BF16))
                AH = ectx.enter_context(nc.sbuf_tensor("AHe", [128, 512], BF16))
                GH = ectx.enter_context(nc.sbuf_tensor("GHe", [128, 512], BF16))
                O2F = ectx.enter_context(nc.sbuf_tensor("O2Fe", [128, NCH * 512],
                                                         BF16))
                pA = _ps(ectx, nc, "pAe", [128, 512], F32)
                pG = _ps(ectx, nc, "pGe", [128, 512], F32)
                pX0 = _ps(ectx, nc, "pX0e", [128, 1024], F32)
                pX1 = _ps(ectx, nc, "pX1e", [128, 1024], F32)
                pW0 = _ps(ectx, nc, "pW0e", [128, 512], F32)
                pW1 = _ps(ectx, nc, "pW1e", [128, 512], F32)

                for ftl in range(16):
                    P.dma(g, WGB[:, ftl * 1024:(ftl + 1) * 1024],
                          wgt[ftl * 128:(ftl + 1) * 128, :], s_wf)
                    P.dma(g, WUB[:, ftl * 1024:(ftl + 1) * 1024],
                          wut[ftl * 128:(ftl + 1) * 128, :], s_wf)
                for fo in range(NCH):
                    P.dma(g, WDB[:, fo * 2048:(fo + 1) * 2048],
                          wdt[fo * 128:(fo + 1) * 128, :], s_wf)
                WEXP = P.cnt[id(s_wf)]
                P.wd(v, s_wf, WEXP)
                nw_ev = None
                for ftl in range(16):
                    for c in range(NCH):
                        P.vo(lambda ftl=ftl, c=c: v.tensor_scalar_mul(
                            WGB[:, ftl * 1024 + c * 128: ftl * 1024 + (c + 1) * 128],
                            WGB[:, ftl * 1024 + c * 128: ftl * 1024 + (c + 1) * 128],
                            NRM[:, c:c + 1]))
                        nw_ev = P.vo(lambda ftl=ftl, c=c: v.tensor_scalar_mul(
                            WUB[:, ftl * 1024 + c * 128: ftl * 1024 + (c + 1) * 128],
                            WUB[:, ftl * 1024 + c * 128: ftl * 1024 + (c + 1) * 128],
                            NRM[:, c:c + 1]))

                P.w(g, cc3)
                with g.register("rE") as regE:
                    g.reg_load(regE, coff.ap()[0:1, 1:2])
                    offE = g.snap(regE)
                    its_ev = P.dma(g, ITS[:].rearrange("p (c j) -> p c j", j=2),
                                   it_all.ap()[:, 0:2].rearrange(
                                       "(c p) o -> p c o", p=128)
                                   [:, bass.ds(offE, 32), :], s_sc)
                P.wd(v, s_sc, its_ev[1])
                P.vo(lambda: v.tensor_copy(
                    out=ITSC[:].rearrange("p (c o) -> p c o", o=1),
                    in_=ITS[:].rearrange("p (c j) -> p c j", j=2)[:, :, 0:1]))
                P.vo(lambda: v.tensor_scalar_add(ITSC[:], ITSC[:], -1.0))
                P.vo(lambda: v.tensor_single_scalar(ITSC[:], ITSC[:], 0.0,
                                                    op=ALU.max))
                weg_ev = wbuild(ITSC[:], 32, pA, WEG)

                wb_last = None
                eg_cons = None
                for sr in range(4):
                    if sr == 0:
                        P.wd(g, s_sc, weg_ev[1])
                        P.w(g, cc1)
                    else:
                        P.w(g, eg_cons)
                    gat = P._mark(None, g.dma_gather(
                        EG[:], hs_all.ap(), WEG[:, sr * 64:(sr + 1) * 64],
                        1024, 1024, 1024), s_g2, 16)
                    GT = gat[1]

                    def egs(i):
                        return EG[:, i:i + 1, :].rearrange("p o e -> p (o e)")

                    xe_ev = [None] * NCH
                    txe_ev = [None, None]
                    for i in range(NCH):
                        P.wd(sc, s_g2, GT)
                        P.wd(v, s_g2, GT)
                        rms_stats(egs(i))
                        xh = XH2[:, (i % 2) * 1024:(i % 2 + 1) * 1024]
                        if txe_ev[i % 2] is not None:
                            P.w(v, txe_ev[i % 2])
                        xh_ev = P.vo(lambda i=i, xh=xh: v.tensor_scalar_mul(
                            xh, egs(i), st8[:, 6:7]))
                        pX = pX0 if i % 2 == 0 else pX1
                        P.w(t, xh_ev)
                        if i >= 2:
                            P.w(t, xe_ev[i - 2])
                        elif sr > 0:
                            P.w(t, wb_last)
                        elif sr == 0 and i == 0:
                            P.w(t, P.last["v"])
                        ti = None
                        for c in range(NCH):
                            ti = t.matmul(pX[:, c * 128:(c + 1) * 128],
                                          xh[:, c * 128:(c + 1) * 128], IDB[:],
                                          start=True, stop=True)
                        txe_ev[i % 2] = P.tlast(ti)
                        P.w(v, txe_ev[i % 2])
                        ev = None
                        for c in range(NCH):
                            ev = P.vo(lambda c=c, i=i, pX=pX: v.tensor_copy(
                                out=XE[:, c * 1024 + i * 128: c * 1024 + (i + 1) * 128],
                                in_=pX[:, c * 128:(c + 1) * 128]))
                        xe_ev[i] = ev
                    guev = None
                    ev = None
                    for ftl in range(16):
                        for sb in range(2):
                            for hf in range(2):
                                pDst = pA if hf == 0 else pG
                                BUF = GH if hf == 0 else AH
                                WB = WGB if hf == 0 else WUB
                                if ftl == 0 and sb == 0 and hf == 0:
                                    P.w(t, xe_ev[NCH - 1])
                                    P.w(t, xe_ev[NCH - 2])
                                    P.w(t, nw_ev)
                                    if sr > 0:
                                        P.w(t, wb_last)
                                if guev is not None:
                                    P.w(t, guev)
                                ti = None
                                for c in range(NCH):
                                    ti = t.matmul(
                                        pDst[:],
                                        WB[:, ftl * 1024 + c * 128:
                                           ftl * 1024 + (c + 1) * 128],
                                        XE[:, c * 1024 + sb * 512:
                                           c * 1024 + (sb + 1) * 512],
                                        start=(c == 0), stop=(c == NCH - 1))
                                mmev = P.tlast(ti)
                                P.w(sc, mmev)
                                if guev is not None:
                                    P.w(sc, guev)
                                fn = AF.Gelu if hf == 0 else AF.Identity
                                ev = P.so(lambda fn=fn, BUF=BUF, pDst=pDst:
                                          sc.activation(out=BUF[:], in_=pDst[:],
                                                        func=fn))
                            P.w(v, ev)
                            guev = P.vo(lambda ftl=ftl, sb=sb: v.tensor_mul(
                                GUE[:, ftl * 1024 + sb * 512:
                                    ftl * 1024 + (sb + 1) * 512],
                                GH[:], AH[:]))
                    pWs = [pW0, pW1]
                    for sb in range(2):
                        o2_ev = [None] * NCH
                        for fo in range(NCH):
                            k = sb * NCH + fo
                            pU = pWs[k % 2]
                            if k == 0:
                                P.w(t, guev)
                            if k >= 2:
                                P.w(t, o2_ev[fo - 2] if fo >= 2 else wb_last or guev)
                            ti = None
                            for cc in range(16):
                                ti = t.matmul(
                                    pU[:],
                                    WDB[:, fo * 2048 + cc * 128:
                                        fo * 2048 + (cc + 1) * 128],
                                    GUE[:, cc * 1024 + sb * 512:
                                        cc * 1024 + (sb + 1) * 512],
                                    start=(cc == 0), stop=(cc == 15))
                            mm2 = P.tlast(ti)
                            P.w(sc, mm2)
                            if sb == 1 and fo == 0:
                                P.w(sc, wb_last or mm2)
                            o2_ev[fo] = P.so(lambda fo=fo, pU=pU: sc.activation(
                                out=O2F[:, fo * 512:(fo + 1) * 512], in_=pU[:],
                                func=AF.Identity))
                        for i in range(4):
                            gidx = sr * 8 + sb * 4 + i
                            tlo = sb * 4 + i
                            P.w(t, o2_ev[NCH - 1])
                            if wb_last is not None:
                                P.w(t, wb_last)
                            ti = None
                            for do in range(NCH):
                                src = O2F[:, do * 512 + i * 128:
                                          do * 512 + (i + 1) * 128]
                                dst = (pA if do < 4 else pG)[:, (do % 4) * 128:
                                                             (do % 4 + 1) * 128]
                                ti = t.matmul(dst, src, IDB[:], start=True,
                                              stop=True)
                            tyv = P.tlast(ti)
                            P.w(v, tyv)
                            P.vo(lambda: v.tensor_copy(out=yt0[:, 0:512], in_=pA[:]))
                            P.vo(lambda: v.tensor_copy(out=yt0[:, 512:1024],
                                                       in_=pG[:]))
                            P.vo(lambda tlo=tlo: v.tensor_copy(out=sq[:],
                                                               in_=egs(tlo)))
                            eg_c = P.vo(lambda: v.tensor_add(yt0[:], yt0[:], sq[:]))
                            if tlo == NCH - 1:
                                eg_cons = eg_c
                            P.wd(v, s_st, P.cnt.get(id(s_st), 0))
                            wb_last = P.vo(lambda: v.tensor_copy(out=xhbt[:],
                                                                 in_=yt0[:]))
                            P.w(sy, wb_last)
                            P.dma(sy, fod[gidx * 128:(gidx + 1) * 128, :], xhbt[:],
                                  s_st)
            P.fence(extra=[(s_st, P.cnt.get(id(s_st), 0)),
                           (s_g2, P.cnt.get(id(s_g2), 0)),
                           (s_sc, P.cnt.get(id(s_sc), 0)),
                           (s_wf, P.cnt.get(id(s_wf), 0))])

            cc4 = P.go(lambda: g.collective_compute(
                "AllGather", ALU.bypass, rg, ins=[fod.ap()], outs=[fo_all.ap()]))

            with ExitStack() as fctx:
                FH = fctx.enter_context(nc.sbuf_tensor("FH", [128, NCH * T], BF16))
                FBG = fctx.enter_context(nc.sbuf_tensor("FBG", [128, 4 * T], BF16))
                FWG = fctx.enter_context(nc.sbuf_tensor("FWG", [128, 4 * 1024], BF16))
                FWU = fctx.enter_context(nc.sbuf_tensor("FWU", [128, 4 * 1024], BF16))
                FWD = fctx.enter_context(nc.sbuf_tensor("FWD", [128, NCH * 512], BF16))
                AH2 = fctx.enter_context(nc.sbuf_tensor("AH2", [128, 512], BF16))
                GH2 = fctx.enter_context(nc.sbuf_tensor("GH2", [128, 512], BF16))
                RG = fctx.enter_context(nc.sbuf_tensor("RG", [128, 8, 1024], BF16))
                CMBV = fctx.enter_context(nc.sbuf_tensor("CMBV", [128, D], F32))
                FBT = fctx.enter_context(nc.sbuf_tensor("FBT", [128, D], F32))
                XH3 = fctx.enter_context(nc.sbuf_tensor("XH3", [128, 2 * 1024], BF16))
                OUTC = fctx.enter_context(nc.sbuf_tensor("OUTC", [128, NCH * 512],
                                                         F32))
                pA2 = _ps(fctx, nc, "pA2f", [128, 512], F32)
                pG2 = _ps(fctx, nc, "pG2f", [128, 512], F32)
                pX2 = _ps(fctx, nc, "pX2f", [128, 1024], F32)
                pX3 = _ps(fctx, nc, "pX3f", [128, 1024], F32)
                pW2 = _ps(fctx, nc, "pW2f", [128, 512], F32)
                pW3 = _ps(fctx, nc, "pW3f", [128, 512], F32)

                for ftl in range(4):
                    P.dma(g, FWG[:, ftl * 1024:(ftl + 1) * 1024],
                          fwgt[ftl * 128:(ftl + 1) * 128, :], s_wf)
                    P.dma(g, FWU[:, ftl * 1024:(ftl + 1) * 1024],
                          fwut[ftl * 128:(ftl + 1) * 128, :], s_wf)
                for fo in range(NCH):
                    P.dma(g, FWD[:, fo * 512:(fo + 1) * 512],
                          fwdt[fo * 128:(fo + 1) * 128, :], s_wf)
                WFB = P.cnt[id(s_wf)]
                P.wd(v, s_wf, WFB)
                fw_ev = None
                for ftl in range(4):
                    for c in range(NCH):
                        P.vo(lambda ftl=ftl, c=c: v.tensor_scalar_mul(
                            FWG[:, ftl * 1024 + c * 128: ftl * 1024 + (c + 1) * 128],
                            FWG[:, ftl * 1024 + c * 128: ftl * 1024 + (c + 1) * 128],
                            FNM[:, c:c + 1]))
                        fw_ev = P.vo(lambda ftl=ftl, c=c: v.tensor_scalar_mul(
                            FWU[:, ftl * 1024 + c * 128: ftl * 1024 + (c + 1) * 128],
                            FWU[:, ftl * 1024 + c * 128: ftl * 1024 + (c + 1) * 128],
                            FNM[:, c:c + 1]))

                fh_ev = [None] * NT
                hs2_ev = [None] * NT
                txf_ev = [None, None]
                for i in range(NT):
                    xb = xbufs[i % 2]
                    if i >= 2:
                        P.w(sy, hs2_ev[i - 2])
                    ld_ev = P.dma(sy, xb[:], x[i * 128:(i + 1) * 128, :], lds[i % 2])
                    P.w(v, ld_ev)
                    hs2_ev[i] = P.vo(lambda i=i: v.tensor_scalar(
                        out=hsf[:], in0=xb[:], scalar1=STM[:, i:i + 1],
                        scalar2=STR[:, i:i + 1], op0=ALU.add, op1=ALU.mult))
                    rms_stats(hsf)
                    xh = XH3[:, (i % 2) * 1024:(i % 2 + 1) * 1024]
                    if txf_ev[i % 2] is not None:
                        P.w(v, txf_ev[i % 2])
                    xh_ev = P.vo(lambda xh=xh: v.tensor_scalar_mul(xh, hsf[:],
                                                                   st8[:, 6:7]))
                    pX = pX2 if i % 2 == 0 else pX3
                    P.w(t, xh_ev)
                    if i == 0:
                        P.wd(t, s_wf, WFB)
                    if i >= 2:
                        P.w(t, fh_ev[i - 2])
                    ti = None
                    for c in range(NCH):
                        ti = t.matmul(pX[:, c * 128:(c + 1) * 128],
                                      xh[:, c * 128:(c + 1) * 128], IDB[:],
                                      start=True, stop=True)
                    txf_ev[i % 2] = P.tlast(ti)
                    P.w(v, txf_ev[i % 2])
                    ev = None
                    for c in range(NCH):
                        ev = P.vo(lambda c=c, i=i, pX=pX: v.tensor_copy(
                            out=FH[:, c * T + i * 128: c * T + (i + 1) * 128],
                            in_=pX[:, c * 128:(c + 1) * 128]))
                    fh_ev[i] = ev
                fh_done = fh_ev[NT - 1]

                guev = None
                ev = None
                for ftl in range(4):
                    for tr in range(8):
                        for hf in range(2):
                            pDst = pA2 if hf == 0 else pG2
                            BUF = GH2 if hf == 0 else AH2
                            WB = FWG if hf == 0 else FWU
                            if ftl == 0 and tr == 0 and hf == 0:
                                P.w(t, fh_done)
                                P.w(t, fh_ev[NT - 2])
                                P.w(t, fw_ev)
                            if guev is not None:
                                P.w(t, guev)
                            ti = None
                            for c in range(NCH):
                                ti = t.matmul(
                                    pDst[:],
                                    WB[:, ftl * 1024 + c * 128:
                                       ftl * 1024 + (c + 1) * 128],
                                    FH[:, c * T + tr * 512: c * T + (tr + 1) * 512],
                                    start=(c == 0), stop=(c == NCH - 1))
                            mmev = P.tlast(ti)
                            P.w(sc, mmev)
                            if guev is not None:
                                P.w(sc, guev)
                            fn = AF.Gelu if hf == 0 else AF.Identity
                            ev = P.so(lambda fn=fn, BUF=BUF, pDst=pDst:
                                      sc.activation(out=BUF[:], in_=pDst[:], func=fn))
                        P.w(v, ev)
                        guev = P.vo(lambda ftl=ftl, tr=tr: v.tensor_mul(
                            FBG[:, ftl * T + tr * 512: ftl * T + (tr + 1) * 512],
                            GH2[:], AH2[:]))
                fb_done = guev

                pWs = [pW2, pW3]
                r_cons = None
                fbt_ev = None
                tyv_last = None
                yadd_ev = [None] * NT
                for tr in range(8):
                    if tr == 0:
                        P.w(g, cc4)
                        P.wd(g, s_sc, P.cnt[id(s_sc)])
                        P.w(g, kw_ev)
                        P.w(g, ov_ev)
                    else:
                        P.w(g, r_cons)
                    gat = P._mark(None, g.dma_gather(
                        RG[:], fo_all.ap(), WCG[:, tr * 64:(tr + 1) * 64],
                        1024, 1024, 1024), s_g2, 16)
                    RGT = gat[1]
                    o2_ev = [None] * NCH
                    for fo in range(NCH):
                        k = tr * NCH + fo
                        pU = pWs[k % 2]
                        if k == 0:
                            P.w(t, fb_done)
                        if k >= 2:
                            P.w(t, o2_ev[fo - 2] if fo >= 2 else tyv_last or fb_done)
                        ti = None
                        for c in range(4):
                            ti = t.matmul(
                                pU[:],
                                FWD[:, fo * 512 + c * 128: fo * 512 + (c + 1) * 128],
                                FBG[:, c * T + tr * 512: c * T + (tr + 1) * 512],
                                start=(c == 0), stop=(c == 3))
                        mm2 = P.tlast(ti)
                        P.w(sc, mm2)
                        if tr > 0 and fo == 0:
                            P.w(sc, tyv_last)
                        o2_ev[fo] = P.so(lambda fo=fo, pU=pU: sc.activation(
                            out=OUTC[:, fo * 512:(fo + 1) * 512], in_=pU[:],
                            func=AF.Identity))
                    for i in range(4):
                        gidx = tr * 4 + i
                        P.w(t, o2_ev[NCH - 1])
                        if fbt_ev is not None:
                            P.w(t, fbt_ev)
                        ti = None
                        for do in range(NCH):
                            s2 = OUTC[:, do * 512 + i * 128: do * 512 + (i + 1) * 128]
                            dst = (pA2 if do < 4 else pG2)[:, (do % 4) * 128:
                                                           (do % 4 + 1) * 128]
                            ti = t.transpose(dst, s2, IDF[:])
                        tyv = P.tlast(ti)
                        tyv_last = tyv
                        if gidx < 2:
                            P.w(sy, fh_done)
                        else:
                            P.w(sy, yadd_ev[gidx - 2])
                        ld_ev = P.dma(sy, xbufs[gidx % 2][:],
                                      x[gidx * 128:(gidx + 1) * 128, :],
                                      lds[gidx % 2])
                        P.w(v, ld_ev)
                        P.vo(lambda gidx=gidx: v.tensor_scalar(
                            out=hsf[:], in0=xbufs[gidx % 2][:],
                            scalar1=STM[:, gidx:gidx + 1],
                            scalar2=STR[:, gidx:gidx + 1],
                            op0=ALU.add, op1=ALU.mult))
                        P.vo(lambda: v.tensor_copy(out=CMBV[:], in_=hsf[:]))
                        P.wd(v, s_g2, RGT)
                        P.vo(lambda i=i: v.tensor_copy(
                            out=sq[:], in_=RG[:, 2 * i:2 * i + 1, :].rearrange(
                                "p o e -> p (o e)")))
                        P.vo(lambda gidx=gidx: v.scalar_tensor_tensor(
                            out=CMBV[:], in0=sq[:], scalar=KWT[:, gidx:gidx + 1],
                            in1=CMBV[:], op0=ALU.mult, op1=ALU.add))
                        r_cons = P.vo(lambda i=i: v.tensor_copy(
                            out=sq[:], in_=RG[:, 2 * i + 1:2 * i + 2, :].rearrange(
                                "p o e -> p (o e)")))
                        P.vo(lambda gidx=gidx: v.scalar_tensor_tensor(
                            out=CMBV[:], in0=sq[:],
                            scalar=KWT[:, 32 + gidx:33 + gidx],
                            in1=CMBV[:], op0=ALU.mult, op1=ALU.add))
                        P.w(v, tyv)
                        P.vo(lambda: v.tensor_copy(out=FBT[:, 0:512], in_=pA2[:]))
                        P.vo(lambda: v.tensor_copy(out=FBT[:, 512:1024], in_=pG2[:]))
                        fbt_ev = P.vo(lambda: v.tensor_add(FBT[:], hsf[:], FBT[:]))
                        P.vo(lambda: v.tensor_sub(FBT[:], FBT[:], CMBV[:]))
                        P.vo(lambda gidx=gidx: v.scalar_tensor_tensor(
                            out=CMBV[:], in0=FBT[:], scalar=OVT[:, gidx:gidx + 1],
                            in1=CMBV[:], op0=ALU.mult, op1=ALU.add))
                        ln_stats(CMBV, sq)
                        P.wd(v, s_st, P.cnt.get(id(s_st), 0))
                        P.vo(lambda: v.tensor_scalar(
                            out=yt0[:], in0=CMBV[:], scalar1=st8[:, 2:3],
                            scalar2=st8[:, 6:7], op0=ALU.add, op1=ALU.mult))
                        yadd_ev[gidx] = P.vo(lambda gidx=gidx: v.tensor_add(
                            yt0[:], xbufs[gidx % 2][:], yt0[:]))
                        P.w(sy, yadd_ev[gidx])
                        P.dma(sy, y[gidx * 128:(gidx + 1) * 128, :], yt0[:], s_st)
            P.wd(sy, s_st, P.cnt[id(s_st)])
            P.wd(sy, s_g2, P.cnt.get(id(s_g2), 0))
            P.wd(sy, s_wf, P.cnt.get(id(s_wf), 0))

        emit(None)


def _stage4_inputs(inputs):
    def tm(w):
        C, F = w.shape
        return np.ascontiguousarray(
            w.reshape(C // 128, 128, F // 128, 128).transpose(2, 1, 0, 3)).reshape(
                F, C)

    wr = np.ascontiguousarray(np.asarray(inputs["L1_ffn_Wr"], np.float32))
    Wg = np.asarray(inputs["L1_ffn_Wg"], np.float32)
    Wu = np.asarray(inputs["L1_ffn_Wu"], np.float32)
    Wd = np.asarray(inputs["L1_ffn_Wd"], np.float32)
    nw = np.asarray(inputs["L1_ffn_norm_w"], np.float32)
    fn = np.asarray(inputs["L1_ffn_f_norm"], np.float32)
    fwg = tm(np.asarray(inputs["L1_ffn_fWg"], np.float32))
    fwu = tm(np.asarray(inputs["L1_ffn_fWu"], np.float32))
    fwd = tm(np.asarray(inputs["L1_ffn_fWd"], np.float32))
    maps = []
    for b in range(NCORES):
        toks = ((b * 8192 + np.arange(64)[None, :] * 128
                 + np.arange(128)[:, None]) // 2).astype(np.float32)
        itb = b * 32 if b < 4 else TBW // 128 + (b - 4) * 32
        co = np.array([[b * 64, itb, b * 32, 256 + b * 32]], np.int32)
        vecs = np.concatenate([nw[b], fn]).reshape(2 * D, 1).astype(np.float32)
        maps.append({
            "s4_wr": wr,
            "s4_wgt": tm(Wg[b]), "s4_wut": tm(Wu[b]), "s4_wdt": tm(Wd[b]),
            "s4_fwgt": fwg, "s4_fwut": fwu, "s4_fwdt": fwd,
            "s4_vecs": np.ascontiguousarray(vecs),
            "s4_tokc": np.ascontiguousarray(toks), "s4_coff": co,
        })
    return maps



def emit_stage3(nc, P, env):
    x = env["x2"]
    wi = env["s3_wi"]
    wo = env["s3_wo"]
    vin = env["s3_vecs"]
    y = env["x3"]
    idb = env["idb"]
    idf = env["idf"]

    ctx = ExitStack()
    with ctx:
        XH = _sb(ctx, nc, "XH", [128, NCH * T], BF16)
        H = _sb(ctx, nc, "H", [128, NCH * T], BF16)
        WI = _sb(ctx, nc, "WI", [128, NCH * D], BF16)
        WO = _sb(ctx, nc, "WO", [128, NCH * D], BF16)
        GB = _sb(ctx, nc, "GB", [128, NCH], F32)
        GAM = _sb(ctx, nc, "GAM", [128, NCH], F32)
        LAM = _sb(ctx, nc, "LAM", [128, NCH], F32)
        BO = _sb(ctx, nc, "BO", [128, NCH], F32)
        LB = _sb(ctx, nc, "LB", [128, 512], F32)
        IDB = _sb(ctx, nc, "IDB", [128, 128], BF16)
        IDF = _sb(ctx, nc, "IDF", [128, 128], F32)
        xb0 = _sb(ctx, nc, "xb0", [128, D], F32)
        xb1 = _sb(ctx, nc, "xb1", [128, D], F32)
        sq = _sb(ctx, nc, "sq", [128, D], F32)
        st8 = _sb(ctx, nc, "st8", [128, 8], F32)
        xhbt = _sb(ctx, nc, "xhbt", [128, D], BF16)
        gu0 = _sb(ctx, nc, "gu0", [128, 512], F32)
        gu1 = _sb(ctx, nc, "gu1", [128, 512], F32)
        hf0 = _sb(ctx, nc, "hf0", [128, 512], F32)
        hf1 = _sb(ctx, nc, "hf1", [128, 512], F32)
        OUTF = _sb(ctx, nc, "OUTF", [128, NCH * 512], F32)
        yt0 = _sb(ctx, nc, "yt0", [128, D], F32)
        tpA = _ps(ctx, nc, "tpA", [128, 1024], BF16)
        tpB = _ps(ctx, nc, "tpB", [128, 1024], BF16)
        pU0 = _ps(ctx, nc, "pU0", [128, 512], F32)
        pU1 = _ps(ctx, nc, "pU1", [128, 512], F32)
        pT0 = _ps(ctx, nc, "pT0", [128, 512], F32)
        pT1 = _ps(ctx, nc, "pT1", [128, 512], F32)
        pV0 = _ps(ctx, nc, "pV0", [128, 512], F32)
        pV1 = _ps(ctx, nc, "pV1", [128, 512], F32)
        s_w = env["s_w"]
        s_ld = env["s_ld"]
        s_le = env["s_le"]
        s_st = env["s_st"]

        xbufs = [xb0, xb1]
        lds = [s_ld, s_le]
        pUs = [pU0, pU1]
        gus = [gu0, gu1]
        hfs = [hf0, hf1]

        def emit(_):
            v, sc, t, sy, g = P.v, P.s, P.t, P.sy, P.g
            for c in range(NCH):
                P.dma(g, WI[:, c * D:(c + 1) * D], wi[c * 128:(c + 1) * 128, :], s_w)
                P.dma(g, WO[:, c * D:(c + 1) * D], wo[c * 128:(c + 1) * 128, :], s_w)
            for vi, dst in enumerate((GB, GAM, LAM, BO)):
                for c in range(NCH):
                    P.dma(g, dst[:, c:c + 1],
                          vin[(vi * NCH + c) * 128:(vi * NCH + c + 1) * 128, :], s_w)
            P.dma(g, IDB[:], idb.ap(), s_w)
            P.dma(g, IDF[:], idf.ap(), s_w)
            W_ALL = P.cnt[id(s_w)]

            P.wd(sc, s_w, W_ALL)
            lam_ev = P.so(lambda: sc.activation(out=LAM[:], in_=LAM[:], func=AF.Exp))
            P.wd(v, s_w, W_ALL)
            eps_ev = P.vo(lambda: v.memset(st8[:, 7:8], EPS_LN))
            gb_ev = P.vo(lambda: v.tensor_mul(GB[:], GB[:], GAM[:]))

            xh_ev = [None] * NT
            ev_ev = [None] * NT
            vst_ev = [None] * NT
            tp_ev = [None] * NT
            sqA_ev = [None] * NT
            for i in range(NT):
                xb = xbufs[i % 2]
                if i >= 2:
                    P.w(sy, xh_ev[i - 2])
                ld_ev = P.dma(sy, xb[:], x[i * 128:(i + 1) * 128, :], lds[i % 2])
                P.w(sc, ld_ev)
                if i >= 1:
                    P.w(sc, vst_ev[i - 1])
                sq_ev = P.so(lambda: sc.activation(out=sq[:], in_=xb[:], func=AF.Square))
                sqA_ev[i] = sq_ev
                P.w(v, ld_ev)
                P.w(v, sq_ev)
                P.vo(lambda: v.reduce_sum(st8[:, 0:1], xb[:], axis=AX.X))
                P.vo(lambda: v.reduce_sum(st8[:, 1:2], sq[:], axis=AX.X))
                P.vo(lambda: v.tensor_scalar_mul(st8[:, 2:3], st8[:, 0:1], -1.0 / D))
                P.vo(lambda: v.tensor_scalar_mul(st8[:, 3:4], st8[:, 1:2], 1.0 / D))
                P.vo(lambda: v.tensor_mul(st8[:, 4:5], st8[:, 2:3], st8[:, 2:3]))
                vst_ev[i] = P.vo(lambda: v.tensor_sub(st8[:, 3:4], st8[:, 3:4], st8[:, 4:5]))
                P.w(sc, vst_ev[i])
                sd_ev = P.so(lambda: sc.activation(out=st8[:, 5:6], in_=st8[:, 3:4],
                                                   func=AF.Sqrt, bias=st8[:, 7:8]))
                P.w(v, sd_ev)
                P.vo(lambda: v.reciprocal(st8[:, 6:7], st8[:, 5:6]))
                if i >= 2:
                    P.w(v, tp_ev[i - 1])
                xh_ev[i] = P.vo(lambda: v.tensor_scalar(
                    out=xhbt[:], in0=xb[:], scalar1=st8[:, 2:3], scalar2=st8[:, 6:7],
                    op0=ALU.add, op1=ALU.mult))
                tp = tpA if i % 2 == 0 else tpB
                P.w(t, xh_ev[i])
                if i == 0:
                    P.wd(t, s_w, W_ALL)
                if i >= 2:
                    P.w(t, ev_ev[i - 2])
                ti = None
                for c in range(NCH):
                    ti = t.transpose(tp[:, c * 128:(c + 1) * 128],
                                     xhbt[:, c * 128:(c + 1) * 128], IDB[:])
                tp_ev[i] = P.tlast(ti)
                P.w(v, tp_ev[i])
                for c in range(NCH):
                    ev_ev[i] = P.vo(lambda c=c: v.tensor_copy(
                        out=XH[:, c * T + i * 128: c * T + (i + 1) * 128],
                        in_=tp[:, c * 128:(c + 1) * 128]))

            gu_ev = [None] * (NCH * 8)
            scan_ev = [None] * (NCH * 8)
            lb_ev = None
            for eo in range(NCH):
                P.w(v, lam_ev)
                P.vo(lambda: v.memset(LB[:], 1.0))
                lb_ev = P.vo(lambda: v.tensor_scalar_mul(LB[:], LB[:], LAM[:, eo:eo + 1]))
                for tr in range(8):
                    k = eo * 8 + tr
                    pU = pUs[k % 2]
                    gu = gus[k % 2]
                    hf = hfs[tr % 2]
                    if k == 0:
                        P.w(t, ev_ev[NT - 1])
                    if k >= 2:
                        P.w(t, gu_ev[k - 2])
                    ti = None
                    for c in range(NCH):
                        ti = t.matmul(pU[:], WI[:, c * D + eo * 128: c * D + (eo + 1) * 128],
                                      XH[:, c * T + tr * 512: c * T + (tr + 1) * 512],
                                      start=(c == 0), stop=(c == NCH - 1))
                    mm_ev = P.tlast(ti)
                    P.w(sc, mm_ev)
                    if k >= 2:
                        P.w(sc, scan_ev[k - 2])
                    gu_ev[k] = P.so(lambda: sc.activation(
                        out=gu[:], in_=pU[:], func=AF.Identity,
                        bias=GB[:, eo:eo + 1], scale=GAM[:, eo:eo + 1]))
                    P.w(v, gu_ev[k])
                    ini = 0.0 if tr == 0 else hfs[(tr - 1) % 2][:, 511:512]
                    P.vo(lambda: v.tensor_tensor_scan(
                        out=hf[:], data0=LB[:], data1=gu[:], initial=ini,
                        op0=ALU.mult, op1=ALU.add))
                    scan_ev[k] = P.vo(lambda: v.tensor_copy(
                        out=H[:, eo * T + tr * 512: eo * T + (tr + 1) * 512], in_=hf[:]))

            of_ev = [None] * (8 * NCH)
            ty_ev = [None] * NT
            ev2_ev = [None] * NT
            fin_ev = [None] * NT
            stb3 = P.cnt.get(id(s_st), 0)
            for tr in range(8):
                add_ev = None
                for do in range(NCH):
                    k = tr * NCH + do
                    pU = pUs[k % 2]
                    if k == 0:
                        P.w(t, scan_ev[NCH * 8 - 1])
                    if k >= 2:
                        P.w(t, of_ev[k - 2])
                    ti = None
                    for c in range(NCH):
                        ti = t.matmul(pU[:], WO[:, c * D + do * 128: c * D + (do + 1) * 128],
                                      H[:, c * T + tr * 512: c * T + (tr + 1) * 512],
                                      start=(c == 0), stop=(c == NCH - 1))
                    mm_ev = P.tlast(ti)
                    P.w(sc, mm_ev)
                    if tr > 0:
                        P.w(sc, ty_ev[tr * 4 - 1])
                    of_ev[k] = P.so(lambda: sc.activation(
                        out=OUTF[:, do * 512:(do + 1) * 512], in_=pU[:],
                        func=AF.Identity, bias=BO[:, do:do + 1]))
                    P.w(v, of_ev[k])
                    add_ev = P.vo(lambda: v.tensor_add(
                        OUTF[:, do * 512:(do + 1) * 512],
                        XH[:, do * T + tr * 512: do * T + (tr + 1) * 512],
                        OUTF[:, do * 512:(do + 1) * 512]))
                for i in range(4):
                    gidx = tr * 4 + i
                    pa, pb = (pT0, pT1) if gidx % 2 == 0 else (pV0, pV1)
                    P.w(t, add_ev)
                    if gidx >= 2:
                        P.w(t, ev2_ev[gidx - 2])
                    ti = None
                    for do in range(NCH):
                        src = OUTF[:, do * 512 + i * 128: do * 512 + (i + 1) * 128]
                        dst = (pa if do < 4 else pb)[:, (do % 4) * 128:(do % 4 + 1) * 128]
                        ti = t.transpose(dst, src, IDF[:])
                    ty_ev[gidx] = P.tlast(ti)
                    if gidx >= 2:
                        P.w(sy, fin_ev[gidx - 2])
                    else:
                        P.w(sy, xh_ev[NT - 2 + gidx])
                        P.w(sy, sqA_ev[NT - 2 + gidx])
                    ld_ev = P.dma(sy, xbufs[gidx % 2][:],
                                  x[gidx * 128:(gidx + 1) * 128, :], lds[gidx % 2])
                    P.w(v, ty_ev[gidx])
                    if gidx >= 1:
                        P.wd(v, s_st, stb3 + 16 * gidx)
                    P.vo(lambda: v.tensor_copy(out=yt0[:, 0:512], in_=pa[:]))
                    ev2_ev[gidx] = P.vo(lambda: v.tensor_copy(out=yt0[:, 512:1024], in_=pb[:]))
                    P.w(sc, ev2_ev[gidx])
                    if gidx >= 1:
                        P.w(sc, vst2_ev[gidx - 1])
                    sq_ev = P.so(lambda: sc.activation(out=sq[:], in_=yt0[:], func=AF.Square))
                    P.w(v, sq_ev)
                    P.vo(lambda: v.reduce_sum(st8[:, 0:1], yt0[:], axis=AX.X))
                    P.vo(lambda: v.reduce_sum(st8[:, 1:2], sq[:], axis=AX.X))
                    P.vo(lambda: v.tensor_scalar_mul(st8[:, 2:3], st8[:, 0:1], -1.0 / D))
                    P.vo(lambda: v.tensor_scalar_mul(st8[:, 3:4], st8[:, 1:2], 1.0 / D))
                    P.vo(lambda: v.tensor_mul(st8[:, 4:5], st8[:, 2:3], st8[:, 2:3]))
                    vst2_ev[gidx] = P.vo(lambda: v.tensor_sub(st8[:, 3:4], st8[:, 3:4],
                                                              st8[:, 4:5]))
                    P.w(sc, vst2_ev[gidx])
                    sd_ev = P.so(lambda: sc.activation(out=st8[:, 5:6], in_=st8[:, 3:4],
                                                       func=AF.Sqrt, bias=st8[:, 7:8]))
                    P.w(v, sd_ev)
                    P.vo(lambda: v.reciprocal(st8[:, 6:7], st8[:, 5:6]))
                    P.vo(lambda: v.tensor_scalar(out=yt0[:], in0=yt0[:],
                                                 scalar1=st8[:, 2:3], scalar2=st8[:, 6:7],
                                                 op0=ALU.add, op1=ALU.mult))
                    P.w(v, ld_ev)
                    fin_ev[gidx] = P.vo(lambda: v.tensor_add(yt0[:], xbufs[gidx % 2][:], yt0[:]))
                    P.w(sy, fin_ev[gidx])
                    P.dma(sy, y[gidx * 128:(gidx + 1) * 128, :], yt0[:], s_st)
            P.wd(sy, s_st, P.cnt[id(s_st)])
        vst2_ev = [None] * NT
        emit(None)
        _stage_fence(P, env)


def _stage3_inputs(inputs):
    wi_t = np.ascontiguousarray(np.asarray(inputs["L1_attn_W_in"]).T).astype(np.float32)
    wo_t = np.ascontiguousarray(np.asarray(inputs["L1_attn_W_out"]).T).astype(np.float32)
    vecs = np.concatenate([
        np.asarray(inputs["L1_attn_b_in"], np.float32).ravel(),
        np.asarray(inputs["L1_attn_gamma"], np.float32).ravel(),
        np.asarray(inputs["L1_attn_lambda_log"], np.float32).ravel(),
        np.asarray(inputs["L1_attn_b_out"], np.float32).ravel(),
    ]).reshape(4 * D, 1)
    return {"s3_wi": wi_t, "s3_wo": wo_t, "s3_vecs": vecs}



def build_fused():
    nc = bass.Bass("TRN2", target_bir_lowering=False, num_devices=8)
    env = {}
    env["x0"] = nc.dram_tensor("x", [T, D], F32, kind="ExternalInput")
    env["y"] = nc.dram_tensor("y", [T, D], F32, kind="ExternalOutput")
    env["x1"] = nc.dram_tensor("x1b", [T, D], F32, kind="Internal")
    env["x2"] = nc.dram_tensor("x2b", [T, D], F32, kind="Internal")
    env["x3"] = nc.dram_tensor("x3b", [T, D], F32, kind="Internal")
    env["s1_wqkv"] = nc.dram_tensor("s1_wqkv", [3 * 8 * 128, D], F32,
                                    kind="ExternalInput")
    env["s1_wo"] = nc.dram_tensor("s1_wo", [D, D], F32, kind="ExternalInput")
    env["s1_vecs"] = nc.dram_tensor("s1_vecs", [D, 1], F32, kind="ExternalInput")
    env["s2_wqkvp"] = nc.dram_tensor("s2_wqkvp", [2 * 12 * 128, D], F32,
                                     kind="ExternalInput")
    env["s2_wol"] = nc.dram_tensor("s2_wol", [2 * 512, D], F32,
                                   kind="ExternalInput")
    env["s2_w1t"] = nc.dram_tensor("s2_w1t", [2 * 2 * NF2 * 128, D], F32,
                                   kind="ExternalInput")
    env["s2_w2t"] = nc.dram_tensor("s2_w2t", [2 * NCH * 128, NF2 * 128], F32,
                                   kind="ExternalInput")
    env["s2_vec3"] = nc.dram_tensor("s2_vec3", [2 * 3 * 128, NCH], F32,
                                    kind="ExternalInput")
    env["s2_qsks"] = nc.dram_tensor("s2_qsks", [2 * 2 * 64, 1], F32,
                                    kind="ExternalInput")
    env["s2_b1p"] = nc.dram_tensor("s2_b1p", [2 * 2 * 128, NF2], F32,
                                   kind="ExternalInput")
    env["s2_xi"] = nc.dram_tensor("s2_xi", [T, D], F32, kind="Internal")
    env["s3_wi"] = nc.dram_tensor("s3_wi", [D, D], F32, kind="ExternalInput")
    env["s3_wo"] = nc.dram_tensor("s3_wo", [D, D], F32, kind="ExternalInput")
    env["s3_vecs"] = nc.dram_tensor("s3_vecs", [4 * D, 1], F32,
                                    kind="ExternalInput")
    env["s4_wr"] = nc.dram_tensor("s4_wr", [D, NE], F32, kind="ExternalInput")
    env["s4_wgt"] = nc.dram_tensor("s4_wgt", [16 * 128, D], F32,
                                   kind="ExternalInput")
    env["s4_wut"] = nc.dram_tensor("s4_wut", [16 * 128, D], F32,
                                   kind="ExternalInput")
    env["s4_wdt"] = nc.dram_tensor("s4_wdt", [NCH * 128, 2048], F32,
                                   kind="ExternalInput")
    env["s4_fwgt"] = nc.dram_tensor("s4_fwgt", [4 * 128, D], F32,
                                    kind="ExternalInput")
    env["s4_fwut"] = nc.dram_tensor("s4_fwut", [4 * 128, D], F32,
                                    kind="ExternalInput")
    env["s4_fwdt"] = nc.dram_tensor("s4_fwdt", [NCH * 128, 512], F32,
                                    kind="ExternalInput")
    env["s4_vecs"] = nc.dram_tensor("s4_vecs", [2 * D, 1], F32,
                                    kind="ExternalInput")
    env["s4_tokc"] = nc.dram_tensor("s4_tokc", [128, 64], F32,
                                    kind="ExternalInput")
    env["s4_coff"] = nc.dram_tensor("s4_coff", [1, 4], I32,
                                    kind="ExternalInput")
    env["s4_hsb"] = nc.dram_tensor("s4_hsb", [T, D], BF16, kind="Internal")
    env["s4_lgd"] = nc.dram_tensor("s4_lgd", [T, NE], F32, kind="Internal")
    env["s4_hs_all"] = nc.dram_tensor("s4_hs_all", [NCORES * T, D], BF16,
                                      kind="Internal")
    env["s4_lg_all"] = nc.dram_tensor("s4_lg_all", [NCORES * T, NE], F32,
                                      kind="Internal")
    env["s4_esd"] = nc.dram_tensor("s4_esd", [NA, 1], F32, kind="Internal")
    env["s4_paf"] = nc.dram_tensor("s4_paf", [NA, 1], F32, kind="Internal")
    env["s4_pafR"] = nc.dram_tensor("s4_pafR", [128, 512], F32, kind="Internal")
    env["s4_esdR"] = nc.dram_tensor("s4_esdR", [128, 512], F32, kind="Internal")
    env["s4_w12d"] = nc.dram_tensor("s4_w12d", [128, 512], F32, kind="Internal")
    env["s4_itd"] = nc.dram_tensor("s4_itd", [NTB, 64], F32, kind="Internal")
    env["s4_it_all"] = nc.dram_tensor("s4_it_all", [NTB, 64], F32,
                                      kind="Internal")
    env["s4_fod"] = nc.dram_tensor("s4_fod", [T, D], BF16, kind="Internal")
    env["s4_fo_all"] = nc.dram_tensor("s4_fo_all", [NCORES * T, D], BF16,
                                      kind="Internal")

    env["idb"] = nc.inline_tensor(ID_BF16, name="id_bf16")
    env["idf"] = nc.inline_tensor(ID_F32, name="id_f32")
    ones2_np = np.zeros((128, 2), np.float32)
    ones2_np[:64, 0] = 1.0
    ones2_np[64:, 1] = 1.0
    env["ones2c"] = nc.inline_tensor(ones2_np, name="ones2c")
    sel65_np = np.zeros((65, 64), np.float32)
    sel65_np[64, :] = 1.0
    env["sel65c"] = nc.inline_tensor(sel65_np, name="sel65c")
    csel_np = np.zeros((2, 2 * 128), np.float32)
    csel_np[0, 0:128] = 1.0
    csel_np[1, 128:256] = 1.0
    env["cselc"] = nc.inline_tensor(csel_np, name="cselc")
    _plan, bias_np, rows_np, mlo_np, mup_np, _hspan, _pairw = _S1
    env["s1_biasc"] = nc.inline_tensor(bias_np, name="s1_biasc")
    env["s1_rowsc"] = nc.inline_tensor(
        np.ascontiguousarray(rows_np.reshape(1, 16 * 1024)), name="s1_rowsc")
    env["s1_mloc"] = nc.inline_tensor(mlo_np, name="s1_mloc")
    env["s1_mupc"] = nc.inline_tensor(mup_np, name="s1_mupc")
    env["s1_onesrc"] = nc.inline_tensor(np.ones((1, 128), np.float32),
                                        name="s1_onesrc")
    cosq_np, sinq_np, cosk_np, sink_np, perm_np = _rotary_tables()
    NEG = -1.0e30
    n_ = np.arange(1024)[None, :]
    p_ = np.arange(128)[:, None]
    amge_np = np.where(p_ >= n_ - 512, 0.0, NEG).astype(BF16NP)
    amle_np = np.where(p_ <= n_ - 512, 0.0, NEG).astype(BF16NP)
    env["s2_cosqc"] = nc.inline_tensor(cosq_np, name="s2_cosqc")
    env["s2_sinqc"] = nc.inline_tensor(sinq_np, name="s2_sinqc")
    env["s2_coskc"] = nc.inline_tensor(cosk_np, name="s2_coskc")
    env["s2_sinkc"] = nc.inline_tensor(sink_np, name="s2_sinkc")
    env["s2_permc"] = nc.inline_tensor(perm_np, name="s2_permc")
    env["s2_amgec"] = nc.inline_tensor(amge_np, name="s2_amgec")
    env["s2_amlec"] = nc.inline_tensor(amle_np, name="s2_amlec")
    sh1_np = np.zeros((128, 128), np.float32)
    for i in range(127):
        sh1_np[i, i + 1] = 1.0
    env["s2_sh1c"] = nc.inline_tensor(sh1_np, name="s2_sh1c")
    e0_np = np.zeros((1, 128), np.float32)
    e0_np[0, 0] = 1.0
    env["s2_e0c"] = nc.inline_tensor(e0_np, name="s2_e0c")
    env["s2_one1c"] = nc.inline_tensor(np.ones((1, 128), np.float32),
                                       name="s2_one1c")
    iotae = np.tile(np.arange(8, dtype=np.float32), (128, 256))
    env["s4_iotaec"] = nc.inline_tensor(np.ascontiguousarray(iotae),
                                        name="s4_iotaec")
    env["s4_iota8c"] = nc.inline_tensor(
        np.arange(8, dtype=np.float32).reshape(8, 1), name="s4_iota8c")

    ctx = ExitStack()
    with ctx:
        for n in SEM_NAMES:
            env[n] = _sm(ctx, nc, n)
        blk = ctx.enter_context(nc.Block())
        P = Pgm(nc, ctx)

        def emit(_):
            P.g.load_library(library_config.mlp)
            emit_stage1(NCProxy(nc, "a_"), P, env)
            emit_stage2(NCProxy(nc, "b_"), P, env)
            emit_stage3(NCProxy(nc, "c_"), P, env)
            emit_stage4(NCProxy(nc, "d_"), P, env)
        blk.gpsimd(emit)

    return nc


def kernel(**inputs):
    x0 = np.ascontiguousarray(np.asarray(inputs["L0_attn_x"], np.float32))
    nc = build_fused()
    lower_extended_insts(nc)
    m1 = _stage1_inputs(inputs)
    m2 = _stage2_inputs(inputs)
    m3 = _stage3_inputs(inputs)
    m4 = _stage4_inputs(inputs)
    maps = []
    for b in range(NCORES):
        m = {"x": np.ascontiguousarray(x0[b])}
        m.update(m1)
        m.update(m2)
        m.update(m3)
        m.update(m4[b])
        maps.append(m)
    res = run_bass_kernel_spmd(nc, maps, core_ids=list(range(NCORES)))
    out = np.stack([res.results[b]["y"] for b in range(NCORES)])
    return np.ascontiguousarray(np.asarray(out, np.float32))
